# version 16
# baseline (speedup 1.0000x reference)
import numpy as np
import concourse.bass as bass
import concourse.mybir as mybir
from concourse.bass_utils import run_bass_kernel_spmd
from contextlib import ExitStack

F32 = mybir.dt.float32
BF16 = mybir.dt.bfloat16
I32 = mybir.dt.int32
AF = mybir.ActivationFunctionType
ALU = mybir.AluOpType
AX = mybir.AxisListType

NCORES = 8
NSEQ = 2
T = 2048
NT = 16
D = 1024
EPS = 1e-5


def host_constants():
    c = {}
    c["ident"] = np.eye(128, dtype=np.float32)
    m = np.zeros((128, 128), np.float32)
    m[np.triu_indices(128, 1)] = -1e30
    c["causal_add"] = np.ascontiguousarray(m)
    m2 = np.ones((128, 128), np.float32)
    m2[np.tril_indices(128, -1)] = 0.0
    c["causal01"] = np.ascontiguousarray(m2)
    c["ones1"] = np.ones((1, 128), np.float32)

    KEY_DIM = 128
    lin = np.linspace(0.0, 1.0, KEY_DIM // 2, dtype=np.float64)
    angle = np.repeat(1.0 / 10000.0 ** lin, 2)
    pos = np.arange(T, dtype=np.float64)
    grid = pos[None, :] * angle[:, None]
    c["cos_fm"] = np.cos(grid).astype(np.float32)
    sin = np.sin(grid).astype(np.float32)
    sgn = np.where(np.arange(KEY_DIM) % 2 == 0, -1.0, 1.0).astype(np.float32)
    c["sin_fm"] = np.ascontiguousarray(sin * sgn[:, None])

    H1 = 8
    d = np.log1p(-2.0 ** (-5.0 - np.arange(H1, dtype=np.float64)))
    k = np.arange(T, dtype=np.float64)
    B = np.exp(-k[:, None] * d[None, :]) * (128.0 ** -0.5)
    ed = np.exp(d)
    rs = (1.0 - np.exp((k[:, None] + 1.0) * d[None, :])) / (1.0 - ed[None, :])
    A = np.exp(k[:, None] * d[None, :]) / np.sqrt(rs)
    c["Btab"] = np.ascontiguousarray(
        B.astype(np.float32).reshape(NT, 128, H1).transpose(1, 0, 2))
    c["Atab"] = np.ascontiguousarray(
        A.astype(np.float32).reshape(NT, 128, H1).transpose(1, 0, 2))
    return c

_CONSTS = host_constants()


ENGINES = ("y", "g", "t", "v", "s")


class Sem:
    __slots__ = ("name", "h", "n", "step")

    def __init__(self, name, step=1):
        self.name = name
        self.h = None
        self.n = 0
        self.step = step

    def mark(self):
        return (self, self.n)


class Prog:
    def __init__(self):
        self.q = {e: [] for e in ENGINES}
        self.sems = []
        self.last_wait = {}

    def sem(self, name, step=1):
        s = Sem(name, step)
        self.sems.append(s)
        return s

    def wait(self, eng, sem, val=None):
        v = sem.n if val is None else val
        if v <= 0:
            return
        key = (eng, sem.name)
        if self.last_wait.get(key, -1) >= v:
            return
        self.last_wait[key] = v
        self.q[eng].append(("w", sem, v))

    def wait_guard(self, eng, guard):
        if guard is None:
            return
        if isinstance(guard, list):
            for g in guard:
                self.wait_guard(eng, g)
            return
        self.wait(eng, guard[0], guard[1])

    def do(self, eng, fn, inc=None):
        n = 0
        if inc is not None:
            n = inc.step
            inc.n += n
        self.q[eng].append(("i", fn, inc, n))

    def replay(self, eng_name, eng):
        for item in self.q[eng_name]:
            if item[0] == "w":
                eng.wait_ge(item[1].h, item[2])
            else:
                ins = item[1](eng)
                if item[2] is not None:
                    ins.then_inc(item[2].h, item[3])


def _as_i32(ap):
    t = ap.tensor.bitcast(I32)
    return bass.AP(tensor=t, offset=ap.offset, ap=ap.ap)



def emit_rsqrt_act(p, out_ap, in_ap, tmp_ap, sS, eps_ap):
    p.do("s", lambda e: e.activation(tmp_ap, in_ap, AF.Ln, bias=eps_ap))
    p.do("s", lambda e: e.activation(out_ap, tmp_ap, AF.Exp, scale=-0.5), inc=sS)


def emit_rsqrt_dve(p, out_ap, in_ap, v_ap, t0_ap, t1_ap, eps=EPS):
    p.do("v", lambda e: e.tensor_single_scalar(v_ap, in_ap, float(eps), ALU.add))
    vi = _as_i32(v_ap)
    oi = _as_i32(out_ap)
    p.do("v", lambda e: e.tensor_single_scalar(oi, vi, 1, ALU.arith_shift_right))
    p.do("v", lambda e: e.tensor_scalar(oi, oi, -1, 0x5F3759DF, ALU.mult, ALU.add))
    for _ in range(3):
        p.do("v", lambda e: e.tensor_mul(t0_ap, out_ap, out_ap))
        p.do("v", lambda e: e.tensor_mul(t0_ap, t0_ap, v_ap))
        p.do("v", lambda e: e.tensor_scalar(t0_ap, t0_ap, -0.5, 1.5, ALU.mult, ALU.add))
        p.do("v", lambda e: e.tensor_mul(out_ap, out_ap, t0_ap))


def emit_ln_smalls(p, sm, sS, sV, eps_ap, double_ln=False, base=0, s1_guard=None):
    c_s1, c_s2, c_m, c_sc, c_a, c_b, c_c = (
        sm[:, base + i:base + i + 1] for i in range(7))
    if s1_guard is not None:
        p.wait_guard("s", s1_guard)
    else:
        p.wait("s", sV)
    p.do("s", lambda e: e.activation(c_m, c_s1, AF.Copy, scale=1.0 / D))
    p.do("s", lambda e: e.activation(c_a, c_s2, AF.Copy, scale=1.0 / D))
    p.do("s", lambda e: e.activation(c_b, c_m, AF.Square))
    p.do("s", lambda e: e.activation(c_c, c_b, AF.Relu, scale=-1.0, bias=c_a))
    p.do("s", lambda e: e.activation(c_a, c_c, AF.Ln, bias=eps_ap))
    if not double_ln:
        p.do("s", lambda e: e.activation(c_sc, c_a, AF.Exp, scale=-0.5), inc=sS)
    else:
        p.do("s", lambda e: e.activation(c_sc, c_a, AF.Exp, scale=-0.5))
        p.do("s", lambda e: e.activation(c_b, c_sc, AF.Square))
        p.do("s", lambda e: e.activation(c_b, c_b, AF.Copy, scale=c_c))
        p.do("s", lambda e: e.activation(c_a, c_b, AF.Ln, bias=eps_ap))
        p.do("s", lambda e: e.activation(c_b, c_a, AF.Exp, scale=-0.5))
        p.do("s", lambda e: e.activation(c_a, c_b, AF.Copy, scale=c_sc))
        p.do("s", lambda e: e.activation(c_sc, c_a, AF.Copy), inc=sS)




def emit_ln_smalls_batched(c, su0, sq0, mn0, sc0, n=NT, double=False):
    p = c.p
    sV, sS, sV2 = c.sV, c.sS, c.sV2
    sm = c.sm
    su = sm[:, su0:su0 + n]
    sq = sm[:, sq0:sq0 + n]
    mn = sm[:, mn0:mn0 + n]
    sc = sm[:, sc0:sc0 + n]
    p.wait("v", sS)
    p.wait("v", sV)
    p.do("v", lambda e: e.tensor_scalar(mn, su, 1.0 / D, None, ALU.mult))
    p.do("v", lambda e: e.tensor_scalar(sq, sq, 1.0 / D, None, ALU.mult), inc=sV2)
    p.wait("v", sV2)
    p.do("v", lambda e: e.tensor_mul(su, mn, mn), inc=sV2)
    p.wait("v", sV2)
    p.do("v", lambda e: e.tensor_sub(sq, sq, su), inc=sV2)
    p.wait("s", sV2)
    p.do("s", lambda e: e.activation(sc, sq, AF.Ln, bias=c.epsc[:]))
    p.do("s", lambda e: e.activation(sc, sc, AF.Exp, scale=-0.5), inc=sS)
    if double:
        p.wait("v", sS)
        p.do("v", lambda e: e.tensor_mul(su, sc, sc), inc=sV2)
        p.wait("v", sV2)
        p.do("v", lambda e: e.tensor_mul(su, su, sq), inc=sV2)
        p.wait("s", sV2)
        p.do("s", lambda e: e.activation(su, su, AF.Ln, bias=c.epsc[:]))
        p.do("s", lambda e: e.activation(su, su, AF.Exp, scale=-0.5), inc=sS)
        p.wait("v", sS)
        p.do("v", lambda e: e.tensor_mul(sc, sc, su), inc=sV2)
        return [sS.mark(), sV2.mark()]
    return [sS.mark()]


def emit_pre_ln_pass(c, s, x_in, xnT_dst, xslot_free, ps_ta_guard,
                     affine=False, sqdump=None):
    p = c.p
    sT, sV, sS = c.sT, c.sV, c.sS
    sqd = sqdump if sqdump is not None else c.scr[:]
    SU, SQ, MN, SC = 128, 144, 160, 176
    p.wait("s", c.sV2)
    for i in range(NT):
        sl = c.s_x[i % 2]
        p.wait_guard("y", xslot_free[i % 2])
        p.do("y", lambda e, i=i: e.dma_start(
            out=c.xt[i % 2][:], in_=x_in[s, 128 * i:128 * (i + 1), :]), inc=sl)
        p.wait("v", sl)
        p.do("v", lambda e, i=i: e.reduce_sum(
            c.sm[:, SU + i:SU + i + 1], c.xt[i % 2][:], axis=AX.X), inc=sV)
        p.wait("s", sl)
        p.do("s", lambda e, i=i, sqd=sqd: e.activation(
            sqd, c.xt[i % 2][:], AF.Square, accum_out=c.sm[:, SQ + i:SQ + i + 1]),
            inc=sS)
        xslot_free[i % 2] = [sV.mark(), sS.mark()]
    marks = emit_ln_smalls_batched(c, SU, SQ, MN, SC, double=affine)
    p.wait_guard("v", marks)
    xnb_free = [None, None]
    for i in range(NT):
        sl = c.s_x[i % 2]
        p.wait_guard("y", xslot_free[i % 2])
        p.do("y", lambda e, i=i: e.dma_start(
            out=c.xt[i % 2][:], in_=x_in[s, 128 * i:128 * (i + 1), :]), inc=sl)
        p.wait("v", sl)
        if affine:
            xnb = c.xn_bf[:]
            p.wait("v", sT)
            p.do("v", lambda e, i=i: e.tensor_scalar(
                c.xn_f[:], c.xt[i % 2][:], c.sm[:, MN + i:MN + i + 1],
                c.sm[:, SC + i:SC + i + 1], ALU.subtract, ALU.mult))
            p.do("v", lambda e: e.tensor_mul(c.xn_f[:], c.xn_f[:], c.gamma_b[:]))
            p.do("v", lambda e, xnb=xnb: e.tensor_add(
                xnb, c.xn_f[:], c.beta_b[:]), inc=sV)
        else:
            xnb = c.xn_bf[:] if i % 2 == 0 else c.xn_bf2
            p.wait_guard("v", xnb_free[i % 2])
            p.do("v", lambda e, i=i, xnb=xnb: e.tensor_scalar(
                xnb, c.xt[i % 2][:], c.sm[:, MN + i:MN + i + 1],
                c.sm[:, SC + i:SC + i + 1], ALU.subtract, ALU.mult), inc=sV)
        xslot_free[i % 2] = sV.mark()
        p.wait("t", sV)
        p.wait_guard("t", ps_ta_guard[0])
        for cc in range(8):
            p.do("t", lambda e, cc=cc, xnb=xnb: e.matmul(
                c.PS_TA[:, cc * 128:(cc + 1) * 128],
                xnb[:, cc * 128:(cc + 1) * 128],
                c.ident_bf[:], is_transpose=True, start=(cc == 0), stop=(cc == 7)),
                inc=sT if cc == 7 else None)
        xnb_free[i % 2] = sT.mark()
        p.wait("v", sT)
        p.do("v", lambda e, i=i, xnT_dst=xnT_dst: e.tensor_copy(
            xnT_dst[:, :, 128 * i:128 * (i + 1)],
            c.PS_TA.rearrange("p (c f) -> p c f", c=8)), inc=sV)
        ps_ta_guard[0] = sV.mark()


def emit_final_ln_pass(c, s, x_in, x_out, ozall, xslot_free):
    p = c.p
    sT, sV, sS = c.sT, c.sV, c.sS
    SU, SQ, MN, SC = 64, 80, 96, 112
    p.wait("s", sV)
    p.wait("s", c.sV2)
    for i in range(NT):
        oztile = ozall[i]
        p.do("v", lambda e, i=i, oztile=oztile: e.reduce_sum(
            c.sm[:, SU + i:SU + i + 1], oztile, axis=AX.X), inc=sV)
        p.do("s", lambda e, i=i, oztile=oztile: e.activation(
            c.scr[:], oztile, AF.Square, accum_out=c.sm[:, SQ + i:SQ + i + 1]),
            inc=sS)
    marks = emit_ln_smalls_batched(c, SU, SQ, MN, SC)
    p.wait_guard("v", marks)
    for i in range(NT):
        sl = c.s_x[i % 2]
        p.wait_guard("y", xslot_free[i % 2])
        p.do("y", lambda e, i=i: e.dma_start(
            out=c.xt[i % 2][:], in_=x_in[s, 128 * i:128 * (i + 1), :]), inc=sl)
        oztile = ozall[i]
        p.do("v", lambda e, i=i, oztile=oztile: e.tensor_scalar(
            c.oz[:], oztile, c.sm[:, MN + i:MN + i + 1],
            c.sm[:, SC + i:SC + i + 1], ALU.subtract, ALU.mult))
        p.wait("v", sl)
        p.do("v", lambda e, i=i: e.tensor_add(
            c.xt[i % 2][:], c.xt[i % 2][:], c.oz[:]), inc=sV)
        p.wait("y", sV)
        p.do("y", lambda e, i=i: e.dma_start(
            out=x_out[s, 128 * i:128 * (i + 1), :], in_=c.xt[i % 2][:]),
            inc=c.s_st[i % 2])
        xslot_free[i % 2] = (c.s_st[i % 2], c.s_st[i % 2].n)


def emit_final_pre_ln_pass(c, s0, s1, x_in, x_out, ozall, xslot_free, ps_ta_guard,
                           pre_x=None, xnT_dst=None, pre_wait_stores=False):
    p = c.p
    sT, sV, sS = c.sT, c.sV, c.sS
    if pre_x is None:
        pre_x = x_in
    if xnT_dst is None:
        xnT_dst = c.xnT
    ASU, ASQ, AMN, ASC = 64, 80, 96, 112
    BSU, BSQ, BMN, BSC = 128, 144, 160, 176
    xslot2_free = [None, None]
    p.wait("s", sV)
    p.wait("s", c.sV2)
    p.wait("y", sT)
    p.wait("y", sS)
    if pre_wait_stores:
        p.wait("y", c.s_st[0])
        p.wait("y", c.s_st[1])
    for i in range(NT):
        oztile = ozall[i]
        p.do("v", lambda e, i=i, oztile=oztile: e.reduce_sum(
            c.sm[:, ASU + i:ASU + i + 1], oztile, axis=AX.X), inc=sV)
        p.do("s", lambda e, i=i, oztile=oztile: e.activation(
            c.scr[:], oztile, AF.Square, accum_out=c.sm[:, ASQ + i:ASQ + i + 1]),
            inc=sS)
        sl2 = c.s_x2[i % 2]
        p.wait_guard("y", xslot2_free[i % 2])
        p.do("y", lambda e, i=i: e.dma_start(
            out=c.xt2[i % 2][:], in_=pre_x[s1, 128 * i:128 * (i + 1), :]), inc=sl2)
        p.wait("v", sl2)
        p.do("v", lambda e, i=i: e.reduce_sum(
            c.sm[:, BSU + i:BSU + i + 1], c.xt2[i % 2][:], axis=AX.X), inc=sV)
        p.wait("s", sl2)
        p.do("s", lambda e, i=i: e.activation(
            c.scr[:], c.xt2[i % 2][:], AF.Square,
            accum_out=c.sm[:, BSQ + i:BSQ + i + 1]), inc=sS)
        xslot2_free[i % 2] = [sV.mark(), sS.mark()]
    marksA = emit_ln_smalls_batched(c, ASU, ASQ, AMN, ASC)
    marksB = emit_ln_smalls_batched(c, BSU, BSQ, BMN, BSC)
    p.wait_guard("v", marksA)
    p.wait_guard("v", marksB)
    xnb_free = [None, None]
    for i in range(NT):
        sl = c.s_x[i % 2]
        p.wait_guard("y", xslot_free[i % 2])
        p.do("y", lambda e, i=i: e.dma_start(
            out=c.xt[i % 2][:], in_=x_in[s0, 128 * i:128 * (i + 1), :]), inc=sl)
        oztile = ozall[i]
        p.do("v", lambda e, i=i, oztile=oztile: e.tensor_scalar(
            c.oz[:], oztile, c.sm[:, AMN + i:AMN + i + 1],
            c.sm[:, ASC + i:ASC + i + 1], ALU.subtract, ALU.mult))
        p.wait("v", sl)
        p.do("v", lambda e, i=i: e.tensor_add(
            c.xt[i % 2][:], c.xt[i % 2][:], c.oz[:]), inc=sV)
        p.wait("y", sV)
        p.do("y", lambda e, i=i: e.dma_start(
            out=x_out[s0, 128 * i:128 * (i + 1), :], in_=c.xt[i % 2][:]),
            inc=c.s_st[i % 2])
        xslot_free[i % 2] = (c.s_st[i % 2], c.s_st[i % 2].n)
        sl2 = c.s_x2[i % 2]
        p.wait_guard("y", xslot2_free[i % 2])
        p.do("y", lambda e, i=i: e.dma_start(
            out=c.xt2[i % 2][:], in_=pre_x[s1, 128 * i:128 * (i + 1), :]), inc=sl2)
        xnb = c.xn_bf[:] if i % 2 == 0 else c.xn_bf2
        p.wait("v", sl2)
        p.wait_guard("v", xnb_free[i % 2])
        p.do("v", lambda e, i=i, xnb=xnb: e.tensor_scalar(
            xnb, c.xt2[i % 2][:], c.sm[:, BMN + i:BMN + i + 1],
            c.sm[:, BSC + i:BSC + i + 1], ALU.subtract, ALU.mult), inc=sV)
        xslot2_free[i % 2] = sV.mark()
        p.wait("t", sV)
        p.wait_guard("t", ps_ta_guard[0])
        for cc in range(8):
            p.do("t", lambda e, cc=cc, xnb=xnb: e.matmul(
                c.PS_TA[:, cc * 128:(cc + 1) * 128],
                xnb[:, cc * 128:(cc + 1) * 128],
                c.ident_bf[:], is_transpose=True, start=(cc == 0), stop=(cc == 7)),
                inc=sT if cc == 7 else None)
        xnb_free[i % 2] = sT.mark()
        p.wait("v", sT)
        p.do("v", lambda e, i=i, xnT_dst=xnT_dst: e.tensor_copy(
            xnT_dst[:, :, 128 * i:128 * (i + 1)],
            c.PS_TA.rearrange("p (c f) -> p c f", c=8)), inc=sV)
        ps_ta_guard[0] = sV.mark()



def build_l0_attn(debug=False):
    nc = bass.Bass("TRN2", target_bir_lowering=False)
    x_in = nc.dram_tensor("x_in", [NSEQ, T, D], F32, kind="ExternalInput")
    wqkvT = nc.dram_tensor("wqkvT", [D, 3 * D], F32, kind="ExternalInput")
    woutT = nc.dram_tensor("woutT", [D, D], F32, kind="ExternalInput")
    rows3 = nc.dram_tensor("rows3", [3, D], F32, kind="ExternalInput")
    ident = nc.dram_tensor("ident", [128, 128], F32, kind="ExternalInput")
    causal = nc.dram_tensor("causal", [128, 128], F32, kind="ExternalInput")
    ones1 = nc.dram_tensor("ones1", [1, 128], F32, kind="ExternalInput")
    x_out = nc.dram_tensor("x_out", [NSEQ, T, D], F32, kind="ExternalOutput")
    dbg = {}
    if debug:
        dbg["d_xnT"] = nc.dram_tensor("d_xnT", [128, 8, T], BF16, kind="ExternalOutput")
        dbg["d_Q"] = nc.dram_tensor("d_Q", [128, 8, T], BF16, kind="ExternalOutput")
        dbg["d_K"] = nc.dram_tensor("d_K", [128, 8, T], BF16, kind="ExternalOutput")
        dbg["d_V"] = nc.dram_tensor("d_V", [128, NT, 16 * 65], BF16, kind="ExternalOutput")

    c = Ctx = type("C", (), {})()
    c.dbg = dbg
    c.nc = nc
    A = nc.alloc_sbuf_tensor
    c.ident_bf = A("ident_bf", [128, 128], BF16)
    c.causal_bf = A("c_causal", [128, 128], BF16)
    c.ones1 = A("c_ones1", [1, 128], F32)
    c.onescol = A("onescol", [128, 1], BF16)
    c.gamma_b = A("gamma_b", [128, D], F32)
    c.beta_b = A("beta_b", [128, D], F32)
    c.bout_b = A("bout_b", [128, D], F32)
    c.xnT = A("xnT", [128, 8, T], BF16)
    c.w = A("w_sb", [128, 8, D], BF16)
    c.Q = A("Q_sb", [128, 8, T], BF16)
    c.K = A("K_sb", [128, 8, T], BF16)
    c.V = A("V_sb", [128, NT, 16 * 65], BF16)
    c.woutT = A("woutT_sb", [128, 8, D], BF16)
    c.PT = A("PT_sb", [128, T], BF16)
    c.PT2 = A("PT2_sb", [128, T], BF16)
    c.qq_sc = A("qq_sc", [128, T], BF16)
    c.att = A("att_sb", [128, D], BF16)
    c.oz = A("oz", [128, D], F32)
    c.xt = [A("xt0", [128, D], F32), A("xt1", [128, D], F32)]
    c.xn_bf = A("xn_bf", [128, D], BF16)
    c.scr = A("scr", [128, D], F32)
    c.xn_bf2 = c.scr.bitcast(BF16)[:, 0:D]
    c.xn_f = c.scr
    c.sm = A("sm", [128, 192], F32)
    c.epsc = A("epsc", [128, 1], F32)

    c.attT = c.PT
    c.PS_S = nc.alloc_psum_tensor("PS_S", [128, 2048], F32)
    c.PS_O = nc.alloc_psum_tensor("PS_O", [128, 1024], F32)
    c.PS_TA = nc.alloc_psum_tensor("PS_TA", [128, 1024], BF16)
    c.PS_TB = nc.alloc_psum_tensor("PS_TB", [128, 1024], BF16)

    p = Prog()
    c.p = p
    c.s_w = p.sem("s_w", 16)
    c.s_x = [p.sem("s_x0", 16), p.sem("s_x1", 16)]
    c.s_st = [p.sem("s_st0", 16), p.sem("s_st1", 16)]
    c.sT = p.sem("sT")
    c.sV = p.sem("sV")
    c.sS = p.sem("sS")
    c.sV2 = p.sem("sV2")

    p.do("v", lambda e: e.memset(c.epsc[:], EPS), inc=c.sV)
    p.do("v", lambda e: e.memset(c.onescol[:], 1.0), inc=c.sV)
    p.do("g", lambda e: e.dma_start(out=c.ident_bf[:], in_=ident[:]), inc=c.s_w)
    p.do("g", lambda e: e.dma_start(out=c.causal_bf[:], in_=causal[:]), inc=c.s_w)
    p.do("g", lambda e: e.dma_start(out=c.ones1[:], in_=ones1[:]), inc=c.s_w)
    p.do("g", lambda e: e.dma_start(
        out=c.woutT[:], in_=woutT.rearrange("(c p) d -> p c d", p=128)), inc=c.s_w)
    for i, dst in enumerate((c.gamma_b, c.beta_b, c.bout_b)):
        p.wait("g", c.sT)
        p.do("g", lambda e, i=i: e.dma_start(
            out=c.scr[0:1, :], in_=rows3[i:i + 1, :]), inc=c.s_w)
        p.wait("t", c.s_w)
        rsl = c.scr[0:1, :]
        p.do("t", lambda e, rsl=rsl: e.matmul(
            c.PS_O[:, 0:512], c.ones1[:], rsl[:, 0:512], start=True, stop=True))
        p.do("t", lambda e, rsl=rsl: e.matmul(
            c.PS_O[:, 512:1024], c.ones1[:], rsl[:, 512:1024], start=True, stop=True), inc=c.sT)
        p.wait("v", c.sT)
        p.do("v", lambda e, dst=dst: e.tensor_copy(dst[:], c.PS_O[:]), inc=c.sV)
        p.wait("t", c.sV)

    for s in range(NSEQ):
        emit_l0a_seq(c, s, x_in, wqkvT, x_out)
        if s == 0 and dbg:
            p.wait("y", c.sV)
            p.wait("y", c.sS)
            for nm, src in (("d_xnT", c.xnT), ("d_Q", c.Q), ("d_K", c.K), ("d_V", c.V)):
                p.do("y", lambda e, nm=nm, src=src: e.dma_start(
                    out=dbg[nm][:], in_=src[:]), inc=c.s_st[0])
    p.wait("y", c.s_st[0])
    p.wait("y", c.s_st[1])
    return nc, c


def emit_l0a_seq(c, s, x_in, wqkvT, x_out):
    p = c.p
    sT, sV, sS, sV2 = c.sT, c.sV, c.sS, c.sV2
    p.wait("s", sT)
    xslot_free = [None, None]

    def load_xtile(i, eng="y"):
        sl = c.s_x[i % 2]
        p.wait_guard(eng, xslot_free[i % 2])
        p.do(eng, lambda e, i=i: e.dma_start(
            out=c.xt[i % 2][:], in_=x_in[s, 128 * i:128 * (i + 1), :]), inc=sl)
        return sl

    ps_ta_free = [None, None]
    emit_pre_ln_pass(c, s, x_in, c.xnT, xslot_free, ps_ta_free,
                     affine=True, sqdump=c.oz[:])

    ps_half_free = [None, None]
    for e0, dst in ((0, "Q"), (D, "K"), (2 * D, "V")):
        p.wait("g", sT)
        p.do("g", lambda e, e0=e0: e.dma_start(
            out=c.w[:], in_=wqkvT.rearrange("(c p) n -> p c n", p=128)[:, :, e0:e0 + D]),
            inc=c.s_w)
        p.wait("t", c.s_w)
        p.wait("t", sV)
        if dst in ("Q", "K"):
            out_sb = c.Q if dst == "Q" else c.K
            for jg in range(4):
                for it in range(8):
                    half = (jg * 8 + it) % 2
                    ps = c.PS_S[:, half * 1024:half * 1024 + 512]
                    p.wait_guard("t", ps_half_free[half])
                    for kc in range(8):
                        p.do("t", lambda e, it=it, kc=kc, jg=jg, ps=ps: e.matmul(
                            ps, c.w[:, kc, it * 128:(it + 1) * 128],
                            c.xnT[:, kc, jg * 512:(jg + 1) * 512],
                            start=(kc == 0), stop=(kc == 7)),
                            inc=sT if kc == 7 else None)
                    if dst == "Q":
                        p.wait("s", sT)
                        p.do("s", lambda e, it=it, jg=jg, ps=ps, out_sb=out_sb: e.activation(
                            out_sb[:, it, jg * 512:(jg + 1) * 512], ps, AF.Copy,
                            scale=0.125), inc=sS)
                        ps_half_free[half] = sS.mark()
                    else:
                        p.wait("v", sT)
                        p.do("v", lambda e, it=it, jg=jg, ps=ps, out_sb=out_sb: e.tensor_copy(
                            out_sb[:, it, jg * 512:(jg + 1) * 512], ps), inc=sV)
                        ps_half_free[half] = sV.mark()
        else:
            vv = c.V.rearrange("p t (h e) -> p t h e", h=16)
            for i in range(NT):
                half = i % 2
                ps = c.PS_S[:, half * 1024:half * 1024 + 1024]
                p.wait_guard("t", ps_half_free[half])
                for kc in range(8):
                    for jh in range(2):
                        p.do("t", lambda e, i=i, kc=kc, jh=jh, ps=ps: e.matmul(
                            ps[:, jh * 512:(jh + 1) * 512],
                            c.xnT[:, kc, i * 128:(i + 1) * 128],
                            c.w[:, kc, jh * 512:(jh + 1) * 512],
                            start=(kc == 0), stop=(kc == 7)),
                            inc=sT if (kc == 7 and jh == 1) else None)
                p.wait("v", sT)
                p.do("v", lambda e, i=i, ps=ps, vv=vv: e.tensor_copy(
                    vv[:, i, :, 0:64], ps.rearrange("p (h e) -> p h e", h=16)))
                p.do("v", lambda e, i=i, vv=vv: e.memset(vv[:, i, :, 64:65], 1.0), inc=sV)
                ps_half_free[half] = sV.mark()

    sm = c.sm
    qsum_marks = []
    for pi, src in enumerate((c.Q, c.K)):
        for ch in range(8):
            p.wait("v", sT)
            p.do("v", lambda e, src=src, ch=ch: e.tensor_mul(
                c.qq_sc[:], src[:, ch, :], src[:, ch, :]), inc=sV)
            p.wait("t", sV)
            for j in range(4):
                p.do("t", lambda e, j=j: e.matmul(
                    c.PS_S[0:1, j * 512:(j + 1) * 512], c.onescol[:],
                    c.qq_sc[:, j * 512:(j + 1) * 512],
                    start=True, stop=True), inc=sT if j == 3 else None)
            p.wait("v", sT)
            for j in range(4):
                p.do("v", lambda e, ch=ch, j=j: e.reduce_max(
                    sm[0:1, 32 + ch * 4 + j:33 + ch * 4 + j],
                    c.PS_S[0:1, j * 512:(j + 1) * 512], axis=AX.X),
                    inc=sV2 if j == 3 else None)
        p.wait("v", sV2)
        p.do("v", lambda e, pi=pi: e.reduce_max(
            sm[0:1, 30 + pi:31 + pi], sm[0:1, 32:64], axis=AX.X), inc=sV2)
    p.wait("v", sV2)
    p.do("v", lambda e: e.tensor_mul(
        sm[0:1, 29:30], sm[0:1, 30:31], sm[0:1, 31:32]), inc=sV2)
    p.wait("s", sV2)
    p.do("s", lambda e: e.activation(sm[0:1, 28:29], sm[0:1, 29:30], AF.Ln))
    p.do("s", lambda e: e.activation(sm[0:1, 27:28], sm[0:1, 28:29], AF.Exp,
                                     scale=0.5), inc=sS)
    p.wait("t", sS)
    p.wait("t", sV)
    p.do("t", lambda e: e.matmul(
        c.PS_O[:, 0:1], c.ones1[:], sm[0:1, 27:28], start=True, stop=True), inc=sT)
    p.wait("v", sT)
    p.do("v", lambda e: e.tensor_scalar(
        sm[:, 13:14], c.PS_O[:, 0:1], -1.0, None, ALU.mult), inc=sV)
    negc_mark = sV.mark()
    c.l0a_qtr = [sV2.mark()] * 4

    ps_o_free = [None, None]
    pt_free = [None, None]
    ps_t_free = [ps_ta_free[0], None]
    sm_free = [None, None]
    for qi in range(NT):
        J = (qi + 1) * 128
        sl = load_xtile(qi)
        exp_s_mark = {}
        mask_v_mark = {}

        def emit_scores_exp(h):
            off = 64 * (h % 2)
            chk = h // 2
            PTb = c.PT if h % 2 == 0 else c.PT2
            base = 1024 * (h % 2) if J <= 1024 else 0
            csz = 4 if J <= 1024 else 8
            p.wait_guard("s", pt_free[h % 2])
            p.wait_guard("s", negc_mark)
            for pc in range((qi + csz) // csz):
                k0 = pc * csz
                k1 = min(qi + 1, k0 + csz)
                qas = range((base + k0 * 128) // 512,
                            (base + (k1 - 1) * 128) // 512 + 1)
                for qa in qas:
                    p.wait_guard("t", c.l0a_qtr[qa])
                for kc in range(k0, k1):
                    p.do("t", lambda e, kc=kc, off=off, chk=chk, qq=qi, base=base: e.matmul(
                        c.PS_S[:, base + kc * 128:base + (kc + 1) * 128],
                        c.K[off:off + 64, chk, kc * 128:(kc + 1) * 128],
                        c.Q[off:off + 64, chk, qq * 128:(qq + 1) * 128],
                        start=True, stop=True), inc=sT if kc == k1 - 1 else None)
                p.wait("s", sT)
                p.do("s", lambda e, k0=k0, k1=k1, PTb=PTb, base=base: e.activation(
                    PTb[:, k0 * 128:k1 * 128],
                    c.PS_S[:, base + k0 * 128:base + k1 * 128],
                    AF.Exp, bias=sm[:, 13:14]), inc=sS)
                for qa in qas:
                    c.l0a_qtr[qa] = sS.mark()
            exp_s_mark[h] = sS.mark()
            p.wait("v", sS)
            p.do("v", lambda e, J=J, PTb=PTb: e.tensor_mul(
                PTb[:, J - 128:J], PTb[:, J - 128:J], c.causal_bf[:]), inc=sV)
            mask_v_mark[h] = sV.mark()

        def emit_pv(h):
            o0 = 512 * (h % 2)
            PTb = c.PT if h % 2 == 0 else c.PT2
            p.wait_guard("t", exp_s_mark[h])
            p.wait_guard("t", mask_v_mark[h])
            p.wait_guard("t", ps_o_free[h % 2])
            for kc in range(qi + 1):
                p.do("t", lambda e, kc=kc, h=h, o0=o0, qq=qi, PTb=PTb: e.matmul(
                    c.PS_O[:, o0:o0 + 65],
                    PTb[:, kc * 128:(kc + 1) * 128],
                    c.V[:, kc, 65 * h:65 * h + 65],
                    start=(kc == 0), stop=(kc == qq)), inc=sT if kc == qi else None)
            pt_free[h % 2] = sT.mark()
            p.wait("v", sT)
            p.wait_guard("v", sm_free[h % 2])
            p.do("v", lambda e, o0=o0, h=h: e.reciprocal(
                c.sm[:, 8 + (h % 2):9 + (h % 2)], c.PS_O[:, o0 + 64:o0 + 65]),
                inc=c.sV2)
            p.wait("v", c.sV2)
            p.do("v", lambda e, h=h, o0=o0: e.tensor_scalar(
                c.att[:, 64 * h:64 * h + 64], c.PS_O[:, o0:o0 + 64],
                c.sm[:, 8 + (h % 2):9 + (h % 2)], None, ALU.mult), inc=sV)
            ps_o_free[h % 2] = sV.mark()
            sm_free[h % 2] = sV.mark()

        for hh in range(17):
            if hh < 16:
                emit_scores_exp(hh)
            if hh >= 1:
                emit_pv(hh - 1)
        p.wait("t", sV)
        p.wait_guard("t", ps_t_free[0])
        for cc in range(8):
            p.do("t", lambda e, cc=cc: e.matmul(
                c.PS_TA[:, cc * 128:(cc + 1) * 128], c.att[:, cc * 128:(cc + 1) * 128],
                c.ident_bf[:], is_transpose=True, start=(cc == 0), stop=(cc == 7)),
                inc=sT if cc == 7 else None)
        p.wait("v", sT)
        p.wait_guard("v", pt_free[0])
        p.do("v", lambda e: e.tensor_copy(c.attT[:, 0:1024], c.PS_TA[:]), inc=sV)
        ps_t_free[0] = sV.mark()
        p.wait("t", sV)
        p.wait_guard("t", ps_o_free[0])
        p.wait_guard("t", ps_o_free[1])
        for kc in range(8):
            for jh in range(2):
                p.do("t", lambda e, kc=kc, jh=jh: e.matmul(
                    c.PS_O[:, jh * 512:(jh + 1) * 512],
                    c.PT[:, kc * 128:(kc + 1) * 128],
                    c.woutT[:, kc, jh * 512:(jh + 1) * 512],
                    start=(kc == 0), stop=(kc == 7)),
                    inc=sT if (kc == 7 and jh == 1) else None)
        pt_free[0] = sT.mark()
        p.wait("v", sT)
        p.do("v", lambda e: e.tensor_add(c.oz[:], c.PS_O[:], c.bout_b[:]), inc=sV)
        ps_o_free[0] = ps_o_free[1] = sV.mark()
        p.do("v", lambda e: e.reduce_sum(c.sm[:, 0:1], c.oz[:], axis=AX.X), inc=sV)
        p.wait("s", sV)
        p.do("s", lambda e: e.activation(
            c.scr[:], c.oz[:], AF.Square, accum_out=c.sm[:, 1:2]), inc=sS)
        emit_ln_smalls(p, c.sm, sS, sV, c.epsc[:])
        p.wait("v", sS)
        p.do("v", lambda e: e.tensor_scalar(
            c.xn_f[:], c.oz[:], c.sm[:, 2:3], c.sm[:, 3:4], ALU.subtract, ALU.mult))
        p.wait("v", sl)
        p.do("v", lambda e, qi=qi: e.tensor_add(
            c.xt[qi % 2][:], c.xt[qi % 2][:], c.xn_f[:]), inc=sV)
        p.wait("y", sV)
        p.do("y", lambda e, qi=qi: e.dma_start(
            out=x_out[s, 128 * qi:128 * (qi + 1), :], in_=c.xt[qi % 2][:]),
            inc=c.s_st[qi % 2])
        xslot_free[qi % 2] = (c.s_st[qi % 2], c.s_st[qi % 2].n)


def finish_build(nc, c):
    with ExitStack() as st:
        for s in c.p.sems:
            s.h = st.enter_context(nc.semaphore(s.name))
        with nc.Block() as block:
            @block.sync
            def _(eng):
                c.p.replay("y", eng)

            @block.gpsimd
            def _(eng):
                c.p.replay("g", eng)

            @block.tensor
            def _(eng):
                c.p.replay("t", eng)

            @block.vector
            def _(eng):
                c.p.replay("v", eng)

            @block.scalar
            def _(eng):
                c.p.replay("s", eng)
    return nc



def build_l0_ffn():
    nc = bass.Bass("TRN2", target_bir_lowering=False)
    x_in = nc.dram_tensor("x_in", [NSEQ, T, D], F32, kind="ExternalInput")
    U1 = nc.dram_tensor("U1", [D, 256], F32, kind="ExternalInput")
    V1 = nc.dram_tensor("V1", [256, 4096], F32, kind="ExternalInput")
    U2 = nc.dram_tensor("U2", [4096, 256], F32, kind="ExternalInput")
    V2 = nc.dram_tensor("V2", [256, D], F32, kind="ExternalInput")
    b1T = nc.dram_tensor("b1T", [128, 32], F32, kind="ExternalInput")
    b2row = nc.dram_tensor("b2row", [1, D], F32, kind="ExternalInput")
    ident = nc.dram_tensor("ident", [128, 128], F32, kind="ExternalInput")
    ones1 = nc.dram_tensor("ones1", [1, 128], F32, kind="ExternalInput")
    x_out = nc.dram_tensor("x_out", [NSEQ, T, D], F32, kind="ExternalOutput")

    c = type("C", (), {})()
    c.nc = nc
    A = nc.alloc_sbuf_tensor
    c.ident_bf = A("ident_bf", [128, 128], BF16)
    c.ones1 = A("c_ones1", [1, 128], F32)
    c.b1fm = A("b1fm", [128, 32], F32)
    c.b2b = A("b2b", [128, D], F32)
    c.xnT = A("xnT", [128, 8, T], BF16)
    c.w1 = A("w1", [128, 8, 256], BF16)
    c.wv1 = A("wv1", [128, 2, 4096], BF16)
    c.w2 = A("w2", [128, 32, 256], BF16)
    c.wv2 = A("wv2", [128, 2, D], BF16)
    c.P2 = A("P2", [128, 2, T], BF16)
    c.H = A("H_sb", [128, 32, 512], BF16)
    c.Q2 = A("Q2", [128, 2, 512], BF16)
    c.ozall = A("ozall", [128, NT, D], BF16)
    c.oztiles = [c.ozall[:, i, :] for i in range(NT)]
    c.oz = A("oz", [128, D], F32)
    c.xt = [A("xt0", [128, D], F32), A("xt1", [128, D], F32)]
    c.xt2 = [A("xt2a", [128, D], F32), A("xt2b", [128, D], F32)]
    c.xn_bf = A("xn_bf", [128, D], BF16)
    c.scr = A("scr", [128, D], F32)
    c.xn_bf2 = c.scr.bitcast(BF16)[:, 0:D]
    c.sm = A("sm", [128, 192], F32)
    c.epsc = A("epsc", [128, 1], F32)

    c.PS_S = nc.alloc_psum_tensor("PS_S", [128, 2048], F32)
    c.PS_O = nc.alloc_psum_tensor("PS_O", [128, 1024], F32)
    c.PS_TA = nc.alloc_psum_tensor("PS_TA", [128, 1024], BF16)
    c.PS_TB = nc.alloc_psum_tensor("PS_TB", [128, 1024], BF16)

    p = Prog()
    c.p = p
    c.s_w = p.sem("s_w", 16)
    c.s_x = [p.sem("s_x0", 16), p.sem("s_x1", 16)]
    c.s_x2 = [p.sem("s_x2a", 16), p.sem("s_x2b", 16)]
    c.s_st = [p.sem("s_st0", 16), p.sem("s_st1", 16)]
    sT = c.sT = p.sem("sT")
    sV = c.sV = p.sem("sV")
    sS = c.sS = p.sem("sS")

    p.do("v", lambda e: e.memset(c.epsc[:], EPS), inc=sV)
    p.do("g", lambda e: e.dma_start(out=c.ident_bf[:], in_=ident[:]), inc=c.s_w)
    p.do("g", lambda e: e.dma_start(out=c.ones1[:], in_=ones1[:]), inc=c.s_w)
    p.do("g", lambda e: e.dma_start(out=c.b1fm[:], in_=b1T[:]), inc=c.s_w)
    p.do("g", lambda e: e.dma_start(
        out=c.w1[:], in_=U1.rearrange("(c p) r -> p c r", p=128)), inc=c.s_w)
    p.do("g", lambda e: e.dma_start(
        out=c.wv1[:], in_=V1.rearrange("(c p) f -> p c f", p=128)), inc=c.s_w)
    p.do("g", lambda e: e.dma_start(
        out=c.w2[:], in_=U2.rearrange("(c p) r -> p c r", p=128)), inc=c.s_w)
    p.do("g", lambda e: e.dma_start(
        out=c.wv2[:], in_=V2.rearrange("(c p) d -> p c d", p=128)), inc=c.s_w)
    p.do("g", lambda e: e.dma_start(out=c.scr[0:1, :], in_=b2row[:]), inc=c.s_w)
    p.wait("t", c.s_w)
    p.do("t", lambda e: e.matmul(c.PS_O[:, 0:512], c.ones1[:], c.scr[0:1, 0:512], start=True, stop=True))
    p.do("t", lambda e: e.matmul(c.PS_O[:, 512:1024], c.ones1[:], c.scr[0:1, 512:1024], start=True, stop=True), inc=sT)
    p.wait("v", sT)
    p.do("v", lambda e: e.tensor_copy(c.b2b[:], c.PS_O[:]), inc=sV)
    p.wait("t", sV)

    emit_l0f_block(c, x_in, x_out)
    p.wait("y", c.s_st[0])
    p.wait("y", c.s_st[1])
    return nc, c


def emit_l0f_block(c, x_in, x_out):
    xslot_free = [None, None]
    ps_ta_free = [None]
    emit_pre_ln_pass(c, 0, x_in, c.xnT, xslot_free, ps_ta_free)
    emit_l0f_core(c)
    emit_final_pre_ln_pass(c, 0, 1, x_in, x_out, c.oztiles, xslot_free, ps_ta_free)
    emit_l0f_core(c)
    emit_final_ln_pass(c, 1, x_in, x_out, c.oztiles, xslot_free)


def emit_l0f_core(c):
    p = c.p
    sT, sV, sS = c.sT, c.sV, c.sS
    if not hasattr(c, "f1_ps_o"):
        c.f1_ps_o = None

    q_free = [None, None, None, None]
    qi_rr = [0]

    def quarter():
        i = qi_rr[0] % 4
        qi_rr[0] += 1
        return i, c.PS_S[:, i * 512:(i + 1) * 512]

    p.wait("t", sV)
    for jg in range(4):
        for it in range(2):
            qi, ps = quarter()
            p.wait_guard("t", q_free[qi])
            for kc in range(8):
                p.do("t", lambda e, it=it, kc=kc, jg=jg, ps=ps: e.matmul(
                    ps, c.w1[:, kc, it * 128:(it + 1) * 128],
                    c.xnT[:, kc, jg * 512:(jg + 1) * 512],
                    start=(kc == 0), stop=(kc == 7)),
                    inc=sT if kc == 7 else None)
            p.wait("v", sT)
            p.do("v", lambda e, it=it, jg=jg, ps=ps: e.tensor_copy(
                c.P2[:, it, jg * 512:(jg + 1) * 512], ps), inc=sV)
            q_free[qi] = sV.mark()

    for g in range(4):
        for it in range(32):
            qi, ps = quarter()
            p.wait_guard("t", q_free[qi])
            for kc in range(2):
                p.do("t", lambda e, it=it, kc=kc, g=g, ps=ps: e.matmul(
                    ps, c.wv1[:, kc, it * 128:(it + 1) * 128],
                    c.P2[:, kc, g * 512:(g + 1) * 512],
                    start=(kc == 0), stop=(kc == 1)),
                    inc=sT if kc == 1 else None)
            p.wait("s", sT)
            p.do("s", lambda e, it=it, ps=ps: e.activation(
                c.H[:, it, :], ps, AF.Gelu_apprx_tanh, bias=c.b1fm[:, it:it + 1]), inc=sS)
            q_free[qi] = sS.mark()
        for it in range(2):
            qi, ps = quarter()
            p.wait_guard("t", q_free[qi])
            p.wait("t", sS)
            for kc in range(32):
                p.do("t", lambda e, it=it, kc=kc, ps=ps: e.matmul(
                    ps, c.w2[:, kc, it * 128:(it + 1) * 128],
                    c.H[:, kc, :],
                    start=(kc == 0), stop=(kc == 31)),
                    inc=sT if kc == 31 else None)
            p.wait("v", sT)
            p.do("v", lambda e, it=it, ps=ps: e.tensor_copy(
                c.Q2[:, it, :], ps), inc=sV)
            q_free[qi] = sV.mark()
        for tt in range(4):
            i = g * 4 + tt
            p.wait_guard("t", c.f1_ps_o)
            p.wait("t", sV)
            for kc in range(2):
                for jh in range(2):
                    p.do("t", lambda e, i=i, kc=kc, jh=jh, tt=tt: e.matmul(
                        c.PS_O[:, jh * 512:(jh + 1) * 512],
                        c.Q2[:, kc, tt * 128:(tt + 1) * 128],
                        c.wv2[:, kc, jh * 512:(jh + 1) * 512],
                        start=(kc == 0), stop=(kc == 1)),
                        inc=sT if (kc == 1 and jh == 1) else None)
            p.wait("v", sT)
            p.do("v", lambda e, i=i: e.tensor_add(
                c.oztiles[i], c.PS_O[:], c.b2b[:]), inc=sV)
            c.f1_ps_o = sV.mark()



H1 = 8
DH1 = 128
VD = 2048
VH = 256


def carve(arena, off_bytes, dtype, shape):
    t = arena.bitcast(dtype)
    esz = mybir.dt.size(dtype)
    start = off_bytes // esz
    n = int(np.prod(shape[1:]))
    ap = t[:, start:start + n]
    if len(shape) == 3:
        ap = ap.rearrange("p (a b) -> p a b", a=shape[1])
    return ap



L1A_ARENA_BYTES = 1280 + 131328 + 59392


def alloc_l1a(c, base=0):
    ar = c.ar
    c.causal_bf = carve(ar, base + 0, BF16, [128, 128])
    c.Btab = carve(ar, base + 256, F32, [128, NT, H1])
    c.Atab = carve(ar, base + 768, F32, [128, NT, H1])
    k0 = base + 1280
    c.krT = carve(ar, k0, BF16, [128, H1, T])
    c.qrT = carve(ar, k0 + 32768, BF16, [128, H1, T])
    c.V = carve(ar, k0 + 65536, BF16, [128, NT, H1 * 257])
    sub = k0 + 65536 + 65792
    c.xnT = carve(ar, sub, BF16, [128, 8, T])
    c.wo = carve(ar, sub, BF16, [128, 16, D])
    c.wbuf = carve(ar, sub + 32 * 1024, BF16, [128, 8, D])
    c.qsh = carve(ar, sub + 32 * 1024, BF16, [128, T])
    c.gstr = [carve(ar, sub + 32 * 1024, BF16, [128, VD]),
              carve(ar, sub + 36 * 1024, BF16, [128, VD])]
    c.Oall = carve(ar, sub + 40 * 1024, BF16, [128, H1 * 256])
    c.PT2 = carve(ar, sub + 44 * 1024, BF16, [128, T])
    c.cos = carve(ar, sub + 48 * 1024, BF16, [128, T])
    c.PT = carve(ar, sub + 48 * 1024, BF16, [128, T])
    c.sin = carve(ar, sub + 52 * 1024, BF16, [128, T])
    c.att = carve(ar, sub + 52 * 1024, BF16, [128, T])
    c.gb = [carve(ar, sub + 48 * 1024, BF16, [128, VD]),
            carve(ar, sub + 52 * 1024, BF16, [128, VD])]
    c.qh = carve(ar, sub + 56 * 1024, BF16, [128, D])
    c.xn_bf = c.xn_bf_t[:]
    c.scrA = c.scr[:]
    c.sqd = c.scr[:]
    c.scrB = c.oz[:]


def build_l1_attn(debug=False):
    nc = bass.Bass("TRN2", target_bir_lowering=False)
    x_in = nc.dram_tensor("x_in", [NSEQ, T, D], F32, kind="ExternalInput")
    wq = nc.dram_tensor("wq", [D, D], F32, kind="ExternalInput")
    wk = nc.dram_tensor("wk", [D, D], F32, kind="ExternalInput")
    wv = nc.dram_tensor("wv", [D, VD], F32, kind="ExternalInput")
    wg = nc.dram_tensor("wg", [D, VD], F32, kind="ExternalInput")
    wo = nc.dram_tensor("wo", [VD, D], F32, kind="ExternalInput")
    cosf = nc.dram_tensor("cosf", [128, T], F32, kind="ExternalInput")
    sinf = nc.dram_tensor("sinf", [128, T], F32, kind="ExternalInput")
    Btab_d = nc.dram_tensor("Btab", [128, NT, H1], F32, kind="ExternalInput")
    Atab_d = nc.dram_tensor("Atab", [128, NT, H1], F32, kind="ExternalInput")
    causal01 = nc.dram_tensor("causal01", [128, 128], F32, kind="ExternalInput")
    ident = nc.dram_tensor("ident", [128, 128], F32, kind="ExternalInput")
    x_out = nc.dram_tensor("x_out", [NSEQ, T, D], F32, kind="ExternalOutput")
    g_d = nc.dram_tensor("g_d", [NT, 128, VD], BF16)
    dbg = {}
    if debug:
        for nm, shp in (("d_xnT", [128, 8, T]), ("d_qrT", [128, H1, T]),
                        ("d_krT", [128, H1, T]), ("d_V", [128, NT, H1 * 257]),
                        ("d_g", [NT, 128, VD])):
            dbg[nm] = nc.dram_tensor(nm, shp, BF16, kind="ExternalOutput")

    c = type("C", (), {})()
    c.nc = nc
    A = nc.alloc_sbuf_tensor
    c.ident_bf = A("ident_bf", [128, 128], BF16)
    c.epsc = A("epsc", [128, 1], F32)
    c.cm1 = A("cm1", [128, 1], F32)
    c.cp1 = A("cp1", [128, 1], F32)
    c.xt = [A("xt0", [128, D], F32), A("xt1", [128, D], F32)]
    c.scr = A("scr", [128, D], F32)
    c.xn_bf2 = c.scr.bitcast(BF16)[:, 0:D]
    c.oz = A("oz", [128, D], F32)
    c.xn_bf_t = A("xn_bf_t", [128, D], BF16)
    c.sm = A("sm", [128, 192], F32)
    c.ar = A("arena", [128, L1A_ARENA_BYTES], mybir.dt.uint8)
    alloc_l1a(c, 0)

    c.PS_S = nc.alloc_psum_tensor("PS_S", [128, 2048], F32)
    c.PS_O = nc.alloc_psum_tensor("PS_O", [128, 1024], F32)
    c.PS_TA = nc.alloc_psum_tensor("PS_TA", [128, 1024], BF16)
    c.PS_TB = nc.alloc_psum_tensor("PS_TB", [128, 1024], BF16)

    p = Prog()
    c.p = p
    c.s_w = p.sem("s_w", 16)
    c.s_x = [p.sem("s_x0", 16), p.sem("s_x1", 16)]
    c.s_st = [p.sem("s_st0", 16), p.sem("s_st1", 16)]
    c.s_g = [p.sem("s_g0", 16), p.sem("s_g1", 16)]
    sT = c.sT = p.sem("sT")
    sV = c.sV = p.sem("sV")
    sS = c.sS = p.sem("sS")
    c.sV2 = p.sem("sV2")

    p.do("v", lambda e: e.memset(c.epsc[:], EPS), inc=sV)
    p.do("v", lambda e: e.memset(c.cm1[:], -1.0), inc=sV)
    p.do("v", lambda e: e.memset(c.cp1[:], 1.0), inc=sV)
    p.do("g", lambda e: e.dma_start(out=c.ident_bf[:], in_=ident[:]), inc=c.s_w)
    p.do("g", lambda e: e.dma_start(out=c.causal_bf, in_=causal01[:]), inc=c.s_w)
    p.do("g", lambda e: e.dma_start(out=c.Btab, in_=Btab_d[:]), inc=c.s_w)
    p.do("g", lambda e: e.dma_start(out=c.Atab, in_=Atab_d[:]), inc=c.s_w)

    for s in range(NSEQ):
        emit_l1a_seq(c, s, x_in, x_out, (wq, wk, wv, wg, wo), (cosf, sinf), g_d)
        if s == 0 and dbg:
            p.wait("y", c.sV)
            p.wait("y", c.sS)
            for nm, src in (("d_qrT", c.qrT[:]), ("d_krT", c.krT[:]), ("d_V", c.V[:])):
                p.do("y", lambda e, nm=nm, src=src: e.dma_start(
                    out=dbg[nm][:], in_=src), inc=c.s_st[0])
            p.do("y", lambda e: e.dma_start(out=dbg["d_g"][:], in_=g_d[:]), inc=c.s_st[0])
    p.wait("y", c.s_st[0])
    p.wait("y", c.s_st[1])
    c.dbg = dbg
    return nc, c


def emit_l1a_seq(c, s, x_in, x_out, ws, trig, g_d, pre_done=False):
    p = c.p
    sT, sV, sS, sV2 = c.sT, c.sV, c.sS, c.sV2
    wq, wk, wv, wg, wo = ws
    cosf, sinf = trig
    xslot_free = [None, None]

    def load_xtile(i):
        sl = c.s_x[i % 2]
        p.wait_guard("y", xslot_free[i % 2])
        p.do("y", lambda e, i=i: e.dma_start(
            out=c.xt[i % 2][:], in_=x_in[s, 128 * i:128 * (i + 1), :]), inc=sl)
        return sl

    ps_ta_free = [None, None]
    if not pre_done:
        emit_pre_ln_pass(c, s, x_in, c.xnT, xslot_free, ps_ta_free, sqdump=c.sqd)

    p.wait("g", sT)
    p.wait("g", sV)
    p.do("g", lambda e: e.dma_start(out=c.cos, in_=cosf[:]), inc=c.s_w)
    p.do("g", lambda e: e.dma_start(out=c.sin, in_=sinf[:]), inc=c.s_w)

    ps_half_free = [None, None]
    for which in ("q", "k"):
        wsl = wq if which == "q" else wk
        dst = c.qrT if which == "q" else c.krT
        p.wait("g", sT)
        p.wait("g", sV)
        p.do("g", lambda e, wsl=wsl: e.dma_start(
            out=c.wbuf, in_=wsl.rearrange("(c p) n -> p c n", p=128)), inc=c.s_w)
        p.wait("t", c.s_w)
        p.wait("t", sV)
        for i in range(NT):
            half = i % 2
            ps = c.PS_S[:, half * 1024:half * 1024 + 1024]
            p.wait_guard("t", ps_half_free[half])
            for kc in range(8):
                for jh in range(2):
                    p.do("t", lambda e, i=i, kc=kc, jh=jh, ps=ps: e.matmul(
                        ps[:, jh * 512:(jh + 1) * 512],
                        c.xnT[:, kc, i * 128:(i + 1) * 128],
                        c.wbuf[:, kc, jh * 512:(jh + 1) * 512],
                        start=(kc == 0), stop=(kc == 7)),
                        inc=sT if (kc == 7 and jh == 1) else None)
            p.wait("s", sT)
            if which == "k":
                for h in range(H1):
                    p.do("s", lambda e, i=i, h=h, ps=ps: e.activation(
                        c.qh[:, h * 128:(h + 1) * 128], ps[:, h * 128:(h + 1) * 128],
                        AF.Copy, scale=c.Btab[:, i, h:h + 1]), inc=sS if h == H1 - 1 else None)
            else:
                p.do("s", lambda e, i=i, ps=ps: e.activation(
                    c.qh[:, 0:1024], ps, AF.Copy), inc=sS)
            ps_half_free[half] = sS.mark()
            p.wait("t", sS)
            p.wait_guard("t", ps_ta_free[1])
            for cc in range(8):
                p.do("t", lambda e, cc=cc: e.matmul(
                    c.PS_TB[:, cc * 128:(cc + 1) * 128],
                    c.qh[:, cc * 128:(cc + 1) * 128],
                    c.ident_bf[:], is_transpose=True, start=(cc == 0), stop=(cc == 7)),
                    inc=sT if cc == 7 else None)
            p.wait("v", sT)
            p.do("v", lambda e, i=i, dst=dst: e.tensor_copy(
                dst[:, :, 128 * i:128 * (i + 1)],
                c.PS_TB.rearrange("p (c f) -> p c f", c=8)), inc=sV)
            ps_ta_free[1] = sV.mark()
        p.wait("v", c.s_w)
        p.wait("v", sT)
        for h in range(H1):
            row = dst[:, h, :]
            p.do("v", lambda e, row=row: e.stream_shuffle(
                c.qsh, row, [i + 1 - 2 * (i % 2) for i in range(32)]))
            p.do("v", lambda e: e.tensor_mul(c.qsh, c.qsh, c.sin))
            p.do("v", lambda e, row=row: e.tensor_mul(row, row, c.cos))
            p.do("v", lambda e, row=row: e.tensor_add(row, row, c.qsh), inc=sV)

    vv = c.V.rearrange("p t (h e) -> p t h e", h=H1)
    for vh in range(2):
        p.wait("g", sT)
        p.wait("g", sV)
        p.do("g", lambda e, vh=vh: e.dma_start(
            out=c.wbuf, in_=wv.rearrange("(c p) n -> p c n", p=128)[:, :, vh * 1024:(vh + 1) * 1024]),
            inc=c.s_w)
        p.wait("t", c.s_w)
        p.wait("t", sV)
        for i in range(NT):
            half = i % 2
            ps = c.PS_S[:, half * 1024:half * 1024 + 1024]
            p.wait_guard("t", ps_half_free[half])
            for kc in range(8):
                for jh in range(2):
                    p.do("t", lambda e, i=i, kc=kc, jh=jh, ps=ps: e.matmul(
                        ps[:, jh * 512:(jh + 1) * 512],
                        c.xnT[:, kc, i * 128:(i + 1) * 128],
                        c.wbuf[:, kc, jh * 512:(jh + 1) * 512],
                        start=(kc == 0), stop=(kc == 7)),
                        inc=sT if (kc == 7 and jh == 1) else None)
            p.wait("v", sT)
            p.do("v", lambda e, i=i, vh=vh, ps=ps: e.tensor_copy(
                vv[:, i, 4 * vh:4 * vh + 4, 0:256],
                ps.rearrange("p (h e) -> p h e", h=4)), inc=sV)
            ps_half_free[half] = sV.mark()
    p.do("v", lambda e: e.memset(vv[:, :, :, 256:257], 1.0), inc=sV)

    gslot_free = [None, None]
    for vh in range(2):
        p.wait("g", sT)
        p.wait("g", sV)
        p.do("g", lambda e, vh=vh: e.dma_start(
            out=c.wbuf, in_=wg.rearrange("(c p) n -> p c n", p=128)[:, :, vh * 1024:(vh + 1) * 1024]),
            inc=c.s_w)
        p.wait("t", c.s_w)
        for i in range(NT):
            half = i % 2
            ps = c.PS_S[:, half * 1024:half * 1024 + 1024]
            p.wait_guard("t", ps_half_free[half])
            for kc in range(8):
                for jh in range(2):
                    p.do("t", lambda e, i=i, kc=kc, jh=jh, ps=ps: e.matmul(
                        ps[:, jh * 512:(jh + 1) * 512],
                        c.xnT[:, kc, i * 128:(i + 1) * 128],
                        c.wbuf[:, kc, jh * 512:(jh + 1) * 512],
                        start=(kc == 0), stop=(kc == 7)),
                        inc=sT if (kc == 7 and jh == 1) else None)
            p.wait("s", sT)
            gb = c.gb[i % 2]
            p.wait_guard("s", gslot_free[i % 2])
            p.do("s", lambda e, ps=ps, gb=gb, vh=vh: e.activation(
                gb[:, vh * 1024:(vh + 1) * 1024], ps, AF.Silu), inc=sS)
            ps_half_free[half] = sS.mark()
            p.wait("y", sS)
            p.do("y", lambda e, i=i, vh=vh, gb=gb: e.dma_start(
                out=g_d[i][:, vh * 1024:(vh + 1) * 1024],
                in_=gb[:, vh * 1024:(vh + 1) * 1024]), inc=c.s_g[i % 2])
            gslot_free[i % 2] = c.s_g[i % 2].mark()

    p.wait("g", sT)
    p.do("g", lambda e: e.dma_start(
        out=c.wo, in_=wo.rearrange("(c p) d -> p c d", p=128)), inc=c.s_w)
    for ss_ in (0, 1):
        p.wait("s", c.s_g[ss_])
        p.wait("v", c.s_g[ss_])
        p.wait("y", c.s_g[ss_])
    p.wait("y", sV)
    p.wait("y", sT)
    c.b4_qtr = [ps_half_free[0], ps_half_free[1],
                ps_half_free[0], ps_half_free[1]]
    ps_o_free = [None, None]
    ps_t_free = [None, None]
    gstream_free = [gslot_free[0], gslot_free[1]]
    oall_free = None
    for qi in range(NT):
        J = (qi + 1) * 128
        sl = load_xtile(qi)
        sg = c.s_g[qi % 2]
        p.wait_guard("y", gstream_free[qi % 2])
        p.do("y", lambda e, qi=qi: e.dma_start(
            out=c.gstr[qi % 2], in_=g_d[qi][:]), inc=sg)
        p.wait("t", c.s_w)
        evac_mark = {}
        pt_bufs = [c.PT, c.PT2]

        def b4_scores_evac(h, qq, JJ):
            PTb = pt_bufs[h % 2]
            eng = "s" if h % 2 == 0 else "v"
            sE = sS if h % 2 == 0 else sV
            base = 1024 * (h % 2) if JJ <= 1024 else 0
            csz = 4 if JJ <= 1024 else 8
            for jq in range((qq + csz) // csz):
                k0 = jq * csz
                k1 = min(qq + 1, k0 + csz)
                qas = range((base + k0 * 128) // 512,
                            (base + (k1 - 1) * 128) // 512 + 1)
                for qa in qas:
                    p.wait_guard("t", c.b4_qtr[qa])
                for kc in range(k0, k1):
                    p.do("t", lambda e, h=h, qq=qq, kc=kc, base=base: e.matmul(
                        c.PS_S[:, base + kc * 128:base + (kc + 1) * 128],
                        c.krT[:, h, kc * 128:(kc + 1) * 128],
                        c.qrT[:, h, qq * 128:(qq + 1) * 128],
                        start=True, stop=True), inc=sT if kc == k1 - 1 else None)
                p.wait(eng, sT)
                if eng == "s":
                    p.do("s", lambda e, k0=k0, k1=k1, PTb=PTb, base=base: e.activation(
                        PTb[:, k0 * 128:k1 * 128],
                        c.PS_S[:, base + k0 * 128:base + k1 * 128],
                        AF.Copy), inc=sS)
                else:
                    p.do("v", lambda e, k0=k0, k1=k1, PTb=PTb, base=base: e.tensor_copy(
                        PTb[:, k0 * 128:k1 * 128],
                        c.PS_S[:, base + k0 * 128:base + k1 * 128]),
                        inc=sV)
                for qa in qas:
                    c.b4_qtr[qa] = sE.mark()
            if eng == "s":
                p.wait("v", sS)
            p.do("v", lambda e, JJ=JJ, PTb=PTb: e.tensor_mul(
                PTb[:, JJ - 128:JJ], PTb[:, JJ - 128:JJ], c.causal_bf[:]), inc=sV)
            evac_mark[h] = [sE.mark(), sV.mark()]

        def b4_pv(h, qq, JJ):
            nonlocal oall_free
            PTb = pt_bufs[h % 2]
            o0 = 512 * (h % 2)
            p.wait_guard("t", evac_mark[h])
            p.wait_guard("t", ps_o_free[h % 2])
            for kc in range(qq + 1):
                p.do("t", lambda e, kc=kc, h=h, o0=o0, qq=qq, PTb=PTb: e.matmul(
                    c.PS_O[:, o0:o0 + 257],
                    PTb[:, kc * 128:(kc + 1) * 128],
                    c.V[:, kc, 257 * h:257 * h + 257],
                    start=(kc == 0), stop=(kc == qq)), inc=sT if kc == qq else None)
            p.wait("v", sT)
            p.wait_guard("v", oall_free)
            p.do("v", lambda e, h=h, o0=o0, qq=qq: e.tensor_scalar(
                c.Oall[:, 256 * h:256 * (h + 1)], c.PS_O[:, o0:o0 + 256],
                c.Atab[:, qq, h:h + 1], None, ALU.mult))
            p.do("v", lambda e, h=h, o0=o0, qq=qq: e.tensor_scalar(
                c.sm[:, 40 + h:41 + h], c.PS_O[:, o0 + 256:o0 + 257],
                c.Atab[:, qq, h:h + 1], None, ALU.mult), inc=sV)
            ps_o_free[h % 2] = sV.mark()
            p.wait("s", sV)
            p.do("s", lambda e, h=h: e.activation(
                c.sqd[:, 0:256], c.Oall[:, 256 * h:256 * (h + 1)], AF.Square,
                accum_out=c.sm[:, 32 + h:33 + h]), inc=sS)

        for hh in range(H1 + 1):
            if hh < H1:
                b4_scores_evac(hh, qi, J)
            if hh >= 1:
                b4_pv(hh - 1, qi, J)
        ooall = c.Oall.rearrange("p (h e) -> p h e", h=H1)
        sm = c.sm
        p.wait("s", sS)
        p.wait("s", sV)
        _se = float(16.0 * EPS ** 0.5)
        p.do("s", lambda e: e.activation(sm[:, 8:16], sm[:, 40:48], AF.Abs))
        p.do("s", lambda e: e.activation(sm[:, 8:16], sm[:, 8:16], AF.Relu, bias=c.cm1[:]))
        p.do("s", lambda e, _se=_se: e.activation(
            sm[:, 16:24], sm[:, 8:16], AF.Copy, scale=_se, bias=_se))
        p.do("s", lambda e: e.activation(
            sm[:, 16:24], sm[:, 16:24], AF.Square), inc=sS)
        p.wait("v", sS)
        p.do("v", lambda e: e.tensor_add(
            sm[:, 24:32], sm[:, 32:40], sm[:, 16:24]), inc=sV2)
        p.wait("s", sV2)
        p.do("s", lambda e: e.activation(
            sm[:, 16:24], sm[:, 24:32], AF.Ln, scale=1.0 / 256.0))
        p.do("s", lambda e: e.activation(
            sm[:, 24:32], sm[:, 16:24], AF.Exp, scale=-0.5), inc=sS)
        p.wait("s", sS)
        for h in range(H1):
            p.do("s", lambda e, h=h: e.activation(
                c.att[:, VH * h:VH * (h + 1)], ooall[:, h, :], AF.Copy,
                scale=sm[:, 24 + h:25 + h]), inc=sS if h == H1 - 1 else None)
        oall_free = sS.mark()
        p.wait("v", sS)
        p.wait("v", sg)
        p.do("v", lambda e, qi=qi: e.tensor_mul(c.att, c.att, c.gstr[qi % 2]), inc=sV)
        gstream_free[qi % 2] = sV.mark()
        p.wait("t", sV)
        for half in range(2):
            pst = c.PS_TA if half == 0 else c.PS_TB
            p.wait_guard("t", ps_t_free[half])
            for cc in range(8):
                k = half * 8 + cc
                p.do("t", lambda e, k=k, cc=cc, pst=pst: e.matmul(
                    pst[:, cc * 128:(cc + 1) * 128],
                    c.att[:, k * 128:(k + 1) * 128],
                    c.ident_bf[:], is_transpose=True, start=(cc == 0), stop=(cc == 7)),
                    inc=sT if cc == 7 else None)
            p.wait("v", sT)
            p.do("v", lambda e, half=half, pst=pst: e.tensor_copy(
                c.PT[:, half * 1024:(half + 1) * 1024], pst[:]), inc=sV)
            ps_t_free[half] = sV.mark()
        p.wait("t", sV)
        p.wait_guard("t", ps_o_free[0])
        p.wait_guard("t", ps_o_free[1])
        for kc in range(16):
            for jh in range(2):
                p.do("t", lambda e, kc=kc, jh=jh: e.matmul(
                    c.PS_O[:, jh * 512:(jh + 1) * 512],
                    c.PT[:, kc * 128:(kc + 1) * 128],
                    c.wo[:, kc, jh * 512:(jh + 1) * 512],
                    start=(kc == 0), stop=(kc == 15)),
                    inc=sT if (kc == 15 and jh == 1) else None)
        p.wait("v", sT)
        p.do("v", lambda e: e.reduce_sum(c.sm[:, 0:1], c.PS_O[:], axis=AX.X), inc=sV)
        p.wait("s", sV)
        p.do("s", lambda e: e.activation(
            c.scrA, c.PS_O[:], AF.Square, accum_out=c.sm[:, 1:2]), inc=sS)
        emit_ln_smalls(p, c.sm, sS, sV, c.epsc[:])
        p.wait("v", sS)
        p.do("v", lambda e: e.tensor_scalar(
            c.scrB, c.PS_O[:], c.sm[:, 2:3], c.sm[:, 3:4], ALU.subtract, ALU.mult), inc=sV)
        ps_o_free[0] = ps_o_free[1] = sV.mark()
        p.wait("v", sl)
        p.do("v", lambda e, qi=qi: e.tensor_add(
            c.xt[qi % 2][:], c.xt[qi % 2][:], c.scrB), inc=sV)
        p.wait("y", sV)
        p.do("y", lambda e, qi=qi: e.dma_start(
            out=x_out[s, 128 * qi:128 * (qi + 1), :], in_=c.xt[qi % 2][:]),
            inc=c.s_st[qi % 2])
        xslot_free[qi % 2] = (c.s_st[qi % 2], c.s_st[qi % 2].n)



def build_l1_ffn():
    nc = bass.Bass("TRN2", target_bir_lowering=False)
    x_in = nc.dram_tensor("x_in", [NSEQ, T, D], F32, kind="ExternalInput")
    WrT = nc.dram_tensor("WrT", [D, D], F32, kind="ExternalInput")
    WkT = nc.dram_tensor("WkT", [D, D], F32, kind="ExternalInput")
    WvT = nc.dram_tensor("WvT", [D, D], F32, kind="ExternalInput")
    muT = nc.dram_tensor("muT", [128, 16], F32, kind="ExternalInput")
    ident = nc.dram_tensor("ident", [128, 128], F32, kind="ExternalInput")
    x_out = nc.dram_tensor("x_out", [NSEQ, T, D], F32, kind="ExternalOutput")

    c = type("C", (), {})()
    c.nc = nc
    A = nc.alloc_sbuf_tensor
    c.ident_bf = A("ident_bf", [128, 128], BF16)
    c.mu = A("mu_sb", [128, 16], F32)
    c.epsc = A("epsc", [128, 1], F32)
    c.xt = [A("xt0", [128, D], F32), A("xt1", [128, D], F32)]
    c.xt2 = [A("xt2a", [128, D], F32), A("xt2b", [128, D], F32)]
    c.scr = A("scr", [128, D], F32)
    c.xn_bf2 = c.scr.bitcast(BF16)[:, 0:D]
    c.oz = A("oz", [128, D], F32)
    c.xn_bf = A("xn_bf", [128, D], BF16)
    c.sm = A("sm", [128, 192], F32)
    c.xnT = A("xnT", [128, 8, T], BF16)
    c.wr = A("wr", [128, 8, D], BF16)
    c.wk = A("wk", [128, 8, D], BF16)
    c.wv = A("wv", [128, 8, D], BF16)
    c.xr = [A("xr0", [128, 8, 512], BF16), A("xr1", [128, 8, 512], BF16)]
    c.xk = [A("xk0", [128, 8, 512], BF16), A("xk1", [128, 8, 512], BF16)]
    c.ka = [A("ka0", [128, 8, 512], BF16), A("ka1", [128, 8, 512], BF16)]
    c.kre = A("kre", [128, 512], F32)
    c.rg = A("rg", [128, 4, D], BF16)
    c.ozall = A("ozall", [128, NT, D], BF16)
    c.oztiles = [c.ozall[:, i, :] for i in range(NT)]

    c.PS_S = nc.alloc_psum_tensor("PS_S", [128, 2048], F32)
    c.PS_O = nc.alloc_psum_tensor("PS_O", [128, 1024], F32)
    c.PS_TA = nc.alloc_psum_tensor("PS_TA", [128, 1024], BF16)
    c.PS_TB = nc.alloc_psum_tensor("PS_TB", [128, 1024], BF16)

    p = Prog()
    c.p = p
    c.s_w = p.sem("s_w", 16)
    c.s_x = [p.sem("s_x0", 16), p.sem("s_x1", 16)]
    c.s_x2 = [p.sem("s_x2a", 16), p.sem("s_x2b", 16)]
    c.s_st = [p.sem("s_st0", 16), p.sem("s_st1", 16)]
    sT = c.sT = p.sem("sT")
    sV = c.sV = p.sem("sV")
    sS = c.sS = p.sem("sS")

    p.do("v", lambda e: e.memset(c.epsc[:], EPS), inc=sV)
    p.do("g", lambda e: e.dma_start(out=c.ident_bf[:], in_=ident[:]), inc=c.s_w)
    p.do("g", lambda e: e.dma_start(out=c.mu[:], in_=muT[:]), inc=c.s_w)
    p.do("g", lambda e: e.dma_start(
        out=c.wr[:], in_=WrT.rearrange("(c p) n -> p c n", p=128)), inc=c.s_w)
    p.do("g", lambda e: e.dma_start(
        out=c.wk[:], in_=WkT.rearrange("(c p) n -> p c n", p=128)), inc=c.s_w)
    p.do("g", lambda e: e.dma_start(
        out=c.wv[:], in_=WvT.rearrange("(c p) n -> p c n", p=128)), inc=c.s_w)

    emit_l1f_block(c, x_in, x_out)
    p.wait("y", c.s_st[0])
    p.wait("y", c.s_st[1])
    return nc, c


def emit_l1f_block(c, x_in, x_out):
    xslot_free = [None, None]
    ps_ta_free = [None]
    emit_pre_ln_pass(c, 0, x_in, c.xnT, xslot_free, ps_ta_free)
    emit_l1f_core(c)
    emit_final_pre_ln_pass(c, 0, 1, x_in, x_out, c.oztiles, xslot_free, ps_ta_free)
    emit_l1f_core(c)
    emit_final_ln_pass(c, 1, x_in, x_out, c.oztiles, xslot_free)


def emit_l1f_core(c):
    p = c.p
    sT, sV, sS = c.sT, c.sV, c.sS

    q_free = [None, None, None, None]
    qi_rr = [0]

    def quarter():
        i = qi_rr[0] % 4
        qi_rr[0] += 1
        return i, c.PS_S[:, i * 512:(i + 1) * 512]

    p.wait("t", c.s_w)
    p.wait("t", sV)
    xr_free = [None, None]
    xk_free = [None, None]
    ka_free = [None, None]
    shift_done = [None, None]

    def emit_shift(g):
        par = g % 2
        xr, xk = c.xr[par], c.xk[par]
        p.wait_guard("v", xr_free[par])
        p.wait_guard("v", xk_free[par])
        last = [None]
        for cc in range(8):
            a = c.xnT[:, cc, g * 512:(g + 1) * 512]
            if g == 0:
                ap_ = c.xnT[:, cc, 0:511]
                for dst, mcol in ((xr, 0), (xk, 8)):
                    p.do("v", lambda e, a=a, ap_=ap_, dst=dst, cc=cc, mcol=mcol: e.tensor_sub(
                        c.scr[:, 0:511], a[:, 1:512], ap_))
                    p.do("v", lambda e, dst=dst, cc=cc, mcol=mcol, ap_=ap_: e.scalar_tensor_tensor(
                        dst[:, cc, 1:512], c.scr[:, 0:511], c.mu[:, mcol + cc:mcol + cc + 1],
                        ap_, ALU.mult, ALU.add))
                    p.do("v", lambda e, a=a, dst=dst, cc=cc, mcol=mcol: e.tensor_scalar(
                        dst[:, cc, 0:1], a[:, 0:1], c.mu[:, mcol + cc:mcol + cc + 1],
                        None, ALU.mult),
                        inc=sV if (cc == 7 and mcol == 8) else None)
            else:
                ap_ = c.xnT[:, cc, g * 512 - 1:(g + 1) * 512 - 1]
                for dst, mcol in ((xr, 0), (xk, 8)):
                    p.do("v", lambda e, a=a, ap_=ap_, dst=dst, cc=cc, mcol=mcol: e.tensor_sub(
                        c.scr[:, 0:512], a, ap_))
                    p.do("v", lambda e, dst=dst, cc=cc, mcol=mcol, ap_=ap_: e.scalar_tensor_tensor(
                        dst[:, cc, :], c.scr[:, 0:512], c.mu[:, mcol + cc:mcol + cc + 1],
                        ap_, ALU.mult, ALU.add),
                        inc=sV if (cc == 7 and mcol == 8) else None)
        shift_done[par] = sV.mark()

    emit_shift(0)
    for g in range(4):
        par = g % 2
        xr, xk, ka = c.xr[par], c.xk[par], c.ka[par]
        p.wait_guard("t", shift_done[par])
        p.wait_guard("s", ka_free[par])
        for it in range(8):
            qi, ps = quarter()
            p.wait_guard("t", q_free[qi])
            for kc in range(8):
                p.do("t", lambda e, it=it, kc=kc, ps=ps, xk=xk: e.matmul(
                    ps, c.wk[:, kc, it * 128:(it + 1) * 128],
                    xk[:, kc, :],
                    start=(kc == 0), stop=(kc == 7)),
                    inc=sT if kc == 7 else None)
            p.wait("s", sT)
            p.do("s", lambda e, it=it, ps=ps: e.activation(c.kre[:], ps, AF.Relu))
            p.do("s", lambda e, it=it, ka=ka: e.activation(
                ka[:, it, :], c.kre[:], AF.Square), inc=sS)
            q_free[qi] = sS.mark()
        xk_free[par] = sT.mark()
        if g + 1 < 4:
            emit_shift(g + 1)
        for tt in range(4):
            i = g * 4 + tt
            for jh in range(2):
                qi, ps = quarter()
                p.wait_guard("t", q_free[qi])
                p.wait("t", sS)
                for kc in range(8):
                    p.do("t", lambda e, tt=tt, kc=kc, jh=jh, ps=ps, xr=xr: e.matmul(
                        ps, xr[:, kc, tt * 128:(tt + 1) * 128],
                        c.wr[:, kc, jh * 512:(jh + 1) * 512],
                        start=(kc == 0), stop=(kc == 7)),
                        inc=sT if kc == 7 else None)
                p.wait("s", sT)
                p.wait("s", sV)
                p.do("s", lambda e, tt=tt, jh=jh, ps=ps: e.activation(
                    c.rg[:, tt, jh * 512:(jh + 1) * 512], ps, AF.Sigmoid), inc=sS)
                q_free[qi] = sS.mark()
            for jh in range(2):
                qi, ps = quarter()
                p.wait_guard("t", q_free[qi])
                for kc in range(8):
                    p.do("t", lambda e, tt=tt, kc=kc, jh=jh, ps=ps, ka=ka: e.matmul(
                        ps, ka[:, kc, tt * 128:(tt + 1) * 128],
                        c.wv[:, kc, jh * 512:(jh + 1) * 512],
                        start=(kc == 0), stop=(kc == 7)),
                        inc=sT if kc == 7 else None)
                p.wait("v", sT)
                p.wait("v", sS)
                p.do("v", lambda e, i=i, tt=tt, jh=jh, ps=ps: e.tensor_mul(
                    c.oztiles[i][:, jh * 512:(jh + 1) * 512], ps,
                    c.rg[:, tt, jh * 512:(jh + 1) * 512]), inc=sV)
                q_free[qi] = sV.mark()
        xr_free[par] = sT.mark()
        ka_free[par] = sT.mark()



def _barrier(p, engs=ENGINES):
    for e in engs:
        for s in p.sems:
            p.wait(e, s)


def _sub_ctx(c):
    n = type("C", (), {})()
    n.__dict__.update(c.__dict__)
    return n


def build_fused(debug=False, nblocks=4, start_block=0):
    nc = bass.Bass("TRN2", target_bir_lowering=False)
    x_in = nc.dram_tensor("x_in", [NSEQ, T, D], F32, kind="ExternalInput")
    y_out = nc.dram_tensor("y_out", [NSEQ, T, D], F32, kind="ExternalOutput")
    xA = nc.dram_tensor("xA", [NSEQ, T, D], F32)
    xB = nc.dram_tensor("xB", [NSEQ, T, D], F32)
    dstage = {}
    if debug:
        for i in range(3):
            dstage[i] = nc.dram_tensor(f"dstage{i}", [NSEQ, T, D], F32, kind="ExternalOutput")
    g_d = nc.dram_tensor("g_d", [NT, 128, VD], BF16)
    ident = nc.dram_tensor("ident", [128, 128], F32, kind="ExternalInput")
    ones1 = nc.dram_tensor("ones1", [1, 128], F32, kind="ExternalInput")
    causal01 = nc.dram_tensor("causal01", [128, 128], F32, kind="ExternalInput")
    cosf = nc.dram_tensor("cosf", [128, T], F32, kind="ExternalInput")
    sinf = nc.dram_tensor("sinf", [128, T], F32, kind="ExternalInput")
    Btab_d = nc.dram_tensor("Btab", [128, NT, H1], F32, kind="ExternalInput")
    Atab_d = nc.dram_tensor("Atab", [128, NT, H1], F32, kind="ExternalInput")
    wqkvT = nc.dram_tensor("wqkvT", [D, 3 * D], F32, kind="ExternalInput")
    woutT = nc.dram_tensor("woutT", [D, D], F32, kind="ExternalInput")
    rows3 = nc.dram_tensor("rows3", [3, D], F32, kind="ExternalInput")
    U1 = nc.dram_tensor("U1", [D, 256], F32, kind="ExternalInput")
    V1 = nc.dram_tensor("V1", [256, 4096], F32, kind="ExternalInput")
    U2 = nc.dram_tensor("U2", [4096, 256], F32, kind="ExternalInput")
    V2 = nc.dram_tensor("V2", [256, D], F32, kind="ExternalInput")
    b1T = nc.dram_tensor("b1T", [128, 32], F32, kind="ExternalInput")
    b2row = nc.dram_tensor("b2row", [1, D], F32, kind="ExternalInput")
    wq = nc.dram_tensor("wq", [D, D], F32, kind="ExternalInput")
    wk = nc.dram_tensor("wk", [D, D], F32, kind="ExternalInput")
    wv = nc.dram_tensor("wv", [D, VD], F32, kind="ExternalInput")
    wg = nc.dram_tensor("wg", [D, VD], F32, kind="ExternalInput")
    wo = nc.dram_tensor("wo", [VD, D], F32, kind="ExternalInput")
    WrT = nc.dram_tensor("WrT", [D, D], F32, kind="ExternalInput")
    WkT = nc.dram_tensor("WkT", [D, D], F32, kind="ExternalInput")
    WvT = nc.dram_tensor("WvT", [D, D], F32, kind="ExternalInput")
    muT = nc.dram_tensor("muT", [128, 16], F32, kind="ExternalInput")

    c = type("C", (), {})()
    c.nc = nc
    A = nc.alloc_sbuf_tensor
    c.ident_bf = A("ident_bf", [128, 128], BF16)
    c.ones1 = A("c_ones1", [1, 128], F32)
    c.onescol = A("onescol", [128, 1], BF16)
    c.epsc = A("epsc", [128, 1], F32)
    c.cm1 = A("cm1", [128, 1], F32)
    c.cp1 = A("cp1", [128, 1], F32)
    c.xt = [A("xt0", [128, D], F32), A("xt1", [128, D], F32)]
    c.scr = A("scr", [128, D], F32)
    c.xn_bf2 = c.scr.bitcast(BF16)[:, 0:D]
    c.oz = A("oz", [128, D], F32)
    c.xn_bf_t = A("xn_bf_t", [128, D], BF16)
    c.sm = A("sm", [128, 192], F32)
    AB = L1A_ARENA_BYTES
    c.ar = A("arena", [128, AB], mybir.dt.uint8)

    c.PS_S = nc.alloc_psum_tensor("PS_S", [128, 2048], F32)
    c.PS_O = nc.alloc_psum_tensor("PS_O", [128, 1024], F32)
    c.PS_TA = nc.alloc_psum_tensor("PS_TA", [128, 1024], BF16)
    c.PS_TB = nc.alloc_psum_tensor("PS_TB", [128, 1024], BF16)

    p = Prog()
    c.p = p
    c.s_w = p.sem("s_w", 16)
    c.s_x = [p.sem("s_x0", 16), p.sem("s_x1", 16)]
    c.s_x2 = [p.sem("s_x2a", 16), p.sem("s_x2b", 16)]
    c.s_st = [p.sem("s_st0", 16), p.sem("s_st1", 16)]
    c.s_g = [p.sem("s_g0", 16), p.sem("s_g1", 16)]
    sT = c.sT = p.sem("sT")
    sV = c.sV = p.sem("sV")
    sS = c.sS = p.sem("sS")
    c.sV2 = p.sem("sV2")

    p.do("v", lambda e: e.memset(c.epsc[:], EPS), inc=sV)
    p.do("v", lambda e: e.memset(c.cm1[:], -1.0), inc=sV)
    p.do("v", lambda e: e.memset(c.cp1[:], 1.0), inc=sV)
    p.do("v", lambda e: e.memset(c.onescol[:], 1.0), inc=sV)
    p.do("g", lambda e: e.dma_start(out=c.ident_bf[:], in_=ident[:]), inc=c.s_w)
    p.do("g", lambda e: e.dma_start(out=c.ones1[:], in_=ones1[:]), inc=c.s_w)

    def bcast_row(src_row_ap, dst):
        p.wait("g", sT)
        p.do("g", lambda e: e.dma_start(out=c.scr[0:1, :], in_=src_row_ap), inc=c.s_w)
        p.wait("t", c.s_w)
        p.do("t", lambda e: e.matmul(c.PS_O[:, 0:512], c.ones1[:], c.scr[0:1, 0:512], start=True, stop=True))
        p.do("t", lambda e: e.matmul(c.PS_O[:, 512:1024], c.ones1[:], c.scr[0:1, 512:1024], start=True, stop=True), inc=sT)
        p.wait("v", sT)
        p.do("v", lambda e, dst=dst: e.tensor_copy(dst, c.PS_O[:]), inc=sV)
        p.wait("t", sV)

    xio = [(x_in, xA), (xA, xB), (xB, xA), (xA, y_out)]
    bufs = [x_in, xA, xB, xA, y_out]
    xio = [(bufs[i], bufs[i + 1]) for i in range(4)]
    xio[nblocks - 1] = (xio[nblocks - 1][0], y_out)
    xio[start_block] = (x_in, xio[start_block][1])

    skip0 = start_block > 0
    c0 = _sub_ctx(c)
    o = 0
    c0.causal_bf = carve(c.ar, 0, BF16, [128, 128]); o = 256
    c0.gamma_b = carve(c.ar, o, F32, [128, D]); o += 4096
    c0.beta_b = carve(c.ar, o, F32, [128, D]); o += 4096
    c0.bout_b = carve(c.ar, o, F32, [128, D]); o += 4096
    c0.xnT = carve(c.ar, o, BF16, [128, 8, T]); o += 32768
    c0.w = carve(c.ar, o, BF16, [128, 8, D]); o += 16384
    c0.Q = carve(c.ar, o, BF16, [128, 8, T]); o += 32768
    c0.K = carve(c.ar, o, BF16, [128, 8, T]); o += 32768
    c0.V = carve(c.ar, o, BF16, [128, NT, 16 * 65]); o += 33280
    c0.woutT = carve(c.ar, o, BF16, [128, 8, D]); o += 16384
    c0.PT = carve(c.ar, o, BF16, [128, T]); o += 4096
    c0.PT2 = carve(c.ar, o, BF16, [128, T]); o += 4096
    c0.qq_sc = carve(c.ar, o, BF16, [128, T]); o += 4096
    c0.att = carve(c.ar, o, BF16, [128, D]); o += 2048
    c0.attT = c0.PT
    c0.xn_bf = c.xn_bf_t[:]
    c0.xn_f = c.scr[:]
    assert o <= AB
    if not skip0:
        p.do("g", lambda e: e.dma_start(out=c0.causal_bf, in_=causal01[:]), inc=c.s_w)
        p.do("g", lambda e: e.dma_start(
            out=c0.woutT, in_=woutT.rearrange("(c p) d -> p c d", p=128)), inc=c.s_w)
        for i, dst in enumerate((c0.gamma_b, c0.beta_b, c0.bout_b)):
            bcast_row(rows3[i:i + 1, :], dst)
        p.wait("s", sT)
        for s in range(NSEQ):
            emit_l0a_seq(c0, s, xio[0][0], wqkvT, xio[0][1])
    if dstage:
        p.wait("y", c.s_st[0]); p.wait("y", c.s_st[1])
        p.do("y", lambda e: e.dma_start(out=dstage[0][:], in_=xA[:]), inc=c.s_st[0])
    _barrier(p)

    if nblocks < 2:
        p.wait("y", c.s_st[0])
        p.wait("y", c.s_st[1])
        return nc, c
    c1 = _sub_ctx(c)
    o = 0
    c1.b1fm = carve(c.ar, o, F32, [128, 32]); o += 128
    c1.b2b = carve(c.ar, o, F32, [128, D]); o += 4096
    c1.xnT = carve(c.ar, o, BF16, [128, 8, T]); o += 32768
    c1.w1 = carve(c.ar, o, BF16, [128, 8, 256]); o += 4096
    c1.wv1 = carve(c.ar, o, BF16, [128, 2, 4096]); o += 16384
    c1.w2 = carve(c.ar, o, BF16, [128, 32, 256]); o += 16384
    c1.wv2 = carve(c.ar, o, BF16, [128, 2, D]); o += 4096
    c1.P2 = carve(c.ar, o, BF16, [128, 2, T]); o += 8192
    c1.Q2 = carve(c.ar, o, BF16, [128, 2, 512]); o += 2048
    hH = o
    c1.H = carve(c.ar, o, BF16, [128, 32, 512]); o += 32768
    c1.xt2 = [carve(c.ar, hH, F32, [128, D]), carve(c.ar, hH + 4096, F32, [128, D])]
    assert o == 120960
    c1.oztiles = (
        [carve(c.ar, 165376 + i * 2048, BF16, [128, D]) for i in range(11)]
        + [carve(c.ar, 120960 + (i - 11) * 2048, BF16, [128, D])
           for i in range(11, 16)])
    c1.xn_bf = c.xn_bf_t
    c1.scr = c.scr
    c1.oz = c.oz
    assert o <= AB
    c1.f1_ps_o = None
    p.do("g", lambda e: e.dma_start(out=c1.b1fm, in_=b1T[:]), inc=c.s_w)
    p.do("g", lambda e: e.dma_start(
        out=c1.w1, in_=U1.rearrange("(c p) r -> p c r", p=128)), inc=c.s_w)
    p.do("g", lambda e: e.dma_start(
        out=c1.wv1, in_=V1.rearrange("(c p) f -> p c f", p=128)), inc=c.s_w)
    p.do("g", lambda e: e.dma_start(
        out=c1.w2, in_=U2.rearrange("(c p) r -> p c r", p=128)), inc=c.s_w)
    p.do("g", lambda e: e.dma_start(
        out=c1.wv2, in_=V2.rearrange("(c p) d -> p c d", p=128)), inc=c.s_w)
    bcast_row(b2row[0:1, :], c1.b2b)
    p.wait("s", sT)
    if nblocks < 3:
        emit_l0f_block(c1, xio[1][0], xio[1][1])
        p.wait("y", c.s_st[0])
        p.wait("y", c.s_st[1])
        return nc, c
    c2 = _sub_ctx(c)
    alloc_l1a(c2, 0)
    assert L1A_ARENA_BYTES <= AB
    xsf1 = [None, None]
    ptf1 = [None]
    emit_pre_ln_pass(c1, 0, xio[1][0], c1.xnT, xsf1, ptf1)
    emit_l0f_core(c1)
    emit_final_pre_ln_pass(c1, 0, 1, xio[1][0], xio[1][1], c1.oztiles, xsf1, ptf1)
    emit_l0f_core(c1)
    emit_final_pre_ln_pass(c1, 1, 0, xio[1][0], xio[1][1], c1.oztiles, xsf1,
                           ptf1, pre_x=xio[2][0], xnT_dst=c2.xnT,
                           pre_wait_stores=True)
    if dstage:
        p.wait("y", c.s_st[0]); p.wait("y", c.s_st[1])
        p.do("y", lambda e: e.dma_start(out=dstage[1][:], in_=xB[:]), inc=c.s_st[0])
    _barrier(p)

    p.do("g", lambda e: e.dma_start(out=c2.causal_bf, in_=causal01[:]), inc=c.s_w)
    p.do("g", lambda e: e.dma_start(out=c2.Btab, in_=Btab_d[:]), inc=c.s_w)
    p.do("g", lambda e: e.dma_start(out=c2.Atab, in_=Atab_d[:]), inc=c.s_w)
    for s in range(NSEQ):
        emit_l1a_seq(c2, s, xio[2][0], xio[2][1], (wq, wk, wv, wg, wo),
                     (cosf, sinf), g_d, pre_done=(s == 0))
    if dstage:
        p.wait("y", c.s_st[0]); p.wait("y", c.s_st[1])
        p.do("y", lambda e: e.dma_start(out=dstage[2][:], in_=xA[:]), inc=c.s_st[0])
    _barrier(p)

    if nblocks < 4:
        p.wait("y", c.s_st[0])
        p.wait("y", c.s_st[1])
        return nc, c
    c3 = _sub_ctx(c)
    o = 0
    c3.mu = carve(c.ar, o, F32, [128, 16]); o += 64
    c3.xnT = carve(c.ar, o, BF16, [128, 8, T]); o += 32768
    c3.wr = carve(c.ar, o, BF16, [128, 8, D]); o += 16384
    c3.wk = carve(c.ar, o, BF16, [128, 8, D]); o += 16384
    c3.wv = carve(c.ar, o, BF16, [128, 8, D]); o += 16384
    c3.xr = [carve(c.ar, o, BF16, [128, 8, 512]),
             carve(c.ar, o + 8192, BF16, [128, 8, 512])]; o += 16384
    c3.xk = [carve(c.ar, o, BF16, [128, 8, 512]),
             carve(c.ar, o + 8192, BF16, [128, 8, 512])]; o += 16384
    c3.ka = [carve(c.ar, o, BF16, [128, 8, 512]),
             carve(c.ar, o + 8192, BF16, [128, 8, 512])]; o += 16384
    c3.kre = carve(c.ar, o, F32, [128, 512]); o += 2048
    c3.rg = carve(c.ar, o, BF16, [128, 4, D]); o += 8192
    c3.ozall = carve(c.ar, o, BF16, [128, NT, D]); o += 32768
    c3.oztiles = [c3.ozall[:, i, :] for i in range(NT)]
    c3.xt2 = [carve(c.ar, o, F32, [128, D]), carve(c.ar, o + 4096, F32, [128, D])]
    o += 8192
    c3.xn_bf = c.xn_bf_t
    assert o <= AB
    p.do("g", lambda e: e.dma_start(out=c3.mu, in_=muT[:]), inc=c.s_w)
    p.do("g", lambda e: e.dma_start(
        out=c3.wr, in_=WrT.rearrange("(c p) n -> p c n", p=128)), inc=c.s_w)
    p.do("g", lambda e: e.dma_start(
        out=c3.wk, in_=WkT.rearrange("(c p) n -> p c n", p=128)), inc=c.s_w)
    p.do("g", lambda e: e.dma_start(
        out=c3.wv, in_=WvT.rearrange("(c p) n -> p c n", p=128)), inc=c.s_w)
    emit_l1f_block(c3, xio[3][0], xio[3][1])
    p.wait("y", c.s_st[0])
    p.wait("y", c.s_st[1])
    return nc, c


def _fused_inputs(inputs, core):
    cst = _CONSTS
    mu = np.stack([inputs["L1_ffn_mu_r"], inputs["L1_ffn_mu_k"]])
    muT = np.ascontiguousarray(
        mu.reshape(2, 8, 128).transpose(2, 0, 1).reshape(128, 16))
    return {
        "x_in": np.ascontiguousarray(inputs["L0_attn_x"][2 * core:2 * core + 2]),
        "ident": cst["ident"],
        "ones1": cst["ones1"],
        "causal01": cst["causal01"],
        "cosf": cst["cos_fm"],
        "sinf": cst["sin_fm"],
        "Btab": cst["Btab"],
        "Atab": cst["Atab"],
        "wqkvT": np.ascontiguousarray(inputs["L0_attn_w_qkv"].T),
        "woutT": np.ascontiguousarray(inputs["L0_attn_w_out"].T),
        "rows3": np.ascontiguousarray(np.stack([
            inputs["L0_attn_ln_gamma"], inputs["L0_attn_ln_beta"],
            inputs["L0_attn_b_out"]])),
        "U1": inputs["L0_ffn_U1"],
        "V1": inputs["L0_ffn_V1"],
        "U2": inputs["L0_ffn_U2"],
        "V2": inputs["L0_ffn_V2"],
        "b1T": np.ascontiguousarray(inputs["L0_ffn_b1"].reshape(32, 128).T),
        "b2row": np.ascontiguousarray(inputs["L0_ffn_b2"][None, :]),
        "wq": inputs["L1_attn_wq"],
        "wk": inputs["L1_attn_wk"],
        "wv": inputs["L1_attn_wv"],
        "wg": inputs["L1_attn_wg"],
        "wo": inputs["L1_attn_wo"],
        "WrT": np.ascontiguousarray(inputs["L1_ffn_W_r"].T),
        "WkT": np.ascontiguousarray(inputs["L1_ffn_W_k"].T),
        "WvT": np.ascontiguousarray(inputs["L1_ffn_W_v"].T),
        "muT": muT,
    }


def kernel_fused(debug=False, nblocks=4, start_block=0, **inputs):
    nc, c = build_fused(debug, nblocks, start_block)
    finish_build(nc, c)
    in_maps = [_fused_inputs(inputs, core) for core in range(NCORES)]
    res = run_bass_kernel_spmd(nc, in_maps, core_ids=list(range(NCORES)))
    y = np.concatenate([r["y_out"] for r in res.results], axis=0)
    if debug:
        return y, res.results
    return y



def _l0f_inputs(inputs, xcur, core):
    cst = _CONSTS
    return {
        "x_in": np.ascontiguousarray(xcur[2 * core:2 * core + 2]),
        "U1": inputs["L0_ffn_U1"],
        "V1": inputs["L0_ffn_V1"],
        "U2": inputs["L0_ffn_U2"],
        "V2": inputs["L0_ffn_V2"],
        "b1T": np.ascontiguousarray(inputs["L0_ffn_b1"].reshape(32, 128).T),
        "b2row": np.ascontiguousarray(inputs["L0_ffn_b2"][None, :]),
        "ident": cst["ident"],
        "ones1": cst["ones1"],
    }


def _l1a_inputs(inputs, xcur, core):
    cst = _CONSTS
    return {
        "x_in": np.ascontiguousarray(xcur[2 * core:2 * core + 2]),
        "wq": inputs["L1_attn_wq"],
        "wk": inputs["L1_attn_wk"],
        "wv": inputs["L1_attn_wv"],
        "wg": inputs["L1_attn_wg"],
        "wo": inputs["L1_attn_wo"],
        "cosf": cst["cos_fm"],
        "sinf": cst["sin_fm"],
        "Btab": cst["Btab"],
        "Atab": cst["Atab"],
        "causal01": cst["causal01"],
        "ident": cst["ident"],
    }


def _l1f_inputs(inputs, xcur, core):
    cst = _CONSTS
    mu = np.stack([inputs["L1_ffn_mu_r"], inputs["L1_ffn_mu_k"]])
    muT = np.ascontiguousarray(
        mu.reshape(2, 8, 128).transpose(2, 0, 1).reshape(128, 16))
    return {
        "x_in": np.ascontiguousarray(xcur[2 * core:2 * core + 2]),
        "WrT": np.ascontiguousarray(inputs["L1_ffn_W_r"].T),
        "WkT": np.ascontiguousarray(inputs["L1_ffn_W_k"].T),
        "WvT": np.ascontiguousarray(inputs["L1_ffn_W_v"].T),
        "muT": muT,
        "ident": cst["ident"],
    }


def _l0a_inputs(inputs, xcur, core):
    cst = _CONSTS
    return {
        "x_in": np.ascontiguousarray(xcur[2 * core:2 * core + 2]),
        "wqkvT": np.ascontiguousarray(inputs["L0_attn_w_qkv"].T),
        "woutT": np.ascontiguousarray(inputs["L0_attn_w_out"].T),
        "rows3": np.ascontiguousarray(np.stack([
            inputs["L0_attn_ln_gamma"], inputs["L0_attn_ln_beta"],
            inputs["L0_attn_b_out"]])),
        "ident": cst["ident"],
        "causal": cst["causal01"],
        "ones1": cst["ones1"],
    }


def kernel(**inputs):
    return kernel_fused(**inputs)


def kernel_multilaunch(debug=False, upto=4, **inputs):
    xcur = np.ascontiguousarray(inputs["L0_attn_x"], dtype=np.float32)
    dbg0 = None
    stages = [(build_l0_attn, _l0a_inputs), (build_l0_ffn, _l0f_inputs),
              (build_l1_attn, _l1a_inputs), (build_l1_ffn, _l1f_inputs)][:upto]
    for bi, (build, mk_in) in enumerate(stages):
        if bi == 0 and debug:
            nc, c = build(True)
        else:
            nc, c = build()
        finish_build(nc, c)
        in_maps = [mk_in(inputs, xcur, core) for core in range(NCORES)]
        res = run_bass_kernel_spmd(nc, in_maps, core_ids=list(range(NCORES)))
        xcur = np.concatenate([r["x_out"] for r in res.results], axis=0)
        if bi == 0 and debug:
            dbg0 = res.results[0]
    if debug:
        return xcur, dbg0
    return xcur


# version 17
# speedup vs baseline: 1.0166x; 1.0166x over previous
import numpy as np
import concourse.bass as bass
import concourse.mybir as mybir
from concourse.bass_utils import run_bass_kernel_spmd
from contextlib import ExitStack

F32 = mybir.dt.float32
BF16 = mybir.dt.bfloat16
I32 = mybir.dt.int32
AF = mybir.ActivationFunctionType
ALU = mybir.AluOpType
AX = mybir.AxisListType

NCORES = 8
NSEQ = 2
T = 2048
NT = 16
D = 1024
EPS = 1e-5


def host_constants():
    c = {}
    c["ident"] = np.eye(128, dtype=np.float32)
    m = np.zeros((128, 128), np.float32)
    m[np.triu_indices(128, 1)] = -1e30
    c["causal_add"] = np.ascontiguousarray(m)
    m2 = np.ones((128, 128), np.float32)
    m2[np.tril_indices(128, -1)] = 0.0
    c["causal01"] = np.ascontiguousarray(m2)
    c["ones1"] = np.ones((1, 128), np.float32)

    KEY_DIM = 128
    lin = np.linspace(0.0, 1.0, KEY_DIM // 2, dtype=np.float64)
    angle = np.repeat(1.0 / 10000.0 ** lin, 2)
    pos = np.arange(T, dtype=np.float64)
    grid = pos[None, :] * angle[:, None]
    c["cos_fm"] = np.cos(grid).astype(np.float32)
    sin = np.sin(grid).astype(np.float32)
    sgn = np.where(np.arange(KEY_DIM) % 2 == 0, -1.0, 1.0).astype(np.float32)
    c["sin_fm"] = np.ascontiguousarray(sin * sgn[:, None])

    H1 = 8
    d = np.log1p(-2.0 ** (-5.0 - np.arange(H1, dtype=np.float64)))
    k = np.arange(T, dtype=np.float64)
    B = np.exp(-k[:, None] * d[None, :]) * (128.0 ** -0.5)
    ed = np.exp(d)
    rs = (1.0 - np.exp((k[:, None] + 1.0) * d[None, :])) / (1.0 - ed[None, :])
    A = np.exp(k[:, None] * d[None, :]) / np.sqrt(rs)
    c["Btab"] = np.ascontiguousarray(
        B.astype(np.float32).reshape(NT, 128, H1).transpose(1, 0, 2))
    c["Atab"] = np.ascontiguousarray(
        A.astype(np.float32).reshape(NT, 128, H1).transpose(1, 0, 2))
    return c

_CONSTS = host_constants()


ENGINES = ("y", "g", "t", "v", "s")


class Sem:
    __slots__ = ("name", "h", "n", "step")

    def __init__(self, name, step=1):
        self.name = name
        self.h = None
        self.n = 0
        self.step = step

    def mark(self):
        return (self, self.n)


class Prog:
    def __init__(self):
        self.q = {e: [] for e in ENGINES}
        self.sems = []
        self.last_wait = {}

    def sem(self, name, step=1):
        s = Sem(name, step)
        self.sems.append(s)
        return s

    def wait(self, eng, sem, val=None):
        v = sem.n if val is None else val
        if v <= 0:
            return
        key = (eng, sem.name)
        if self.last_wait.get(key, -1) >= v:
            return
        self.last_wait[key] = v
        self.q[eng].append(("w", sem, v))

    def wait_guard(self, eng, guard):
        if guard is None:
            return
        if isinstance(guard, list):
            for g in guard:
                self.wait_guard(eng, g)
            return
        self.wait(eng, guard[0], guard[1])

    def do(self, eng, fn, inc=None):
        n = 0
        if inc is not None:
            n = inc.step
            inc.n += n
        self.q[eng].append(("i", fn, inc, n))

    def replay(self, eng_name, eng):
        for item in self.q[eng_name]:
            if item[0] == "w":
                eng.wait_ge(item[1].h, item[2])
            else:
                ins = item[1](eng)
                if item[2] is not None:
                    ins.then_inc(item[2].h, item[3])


def _as_i32(ap):
    t = ap.tensor.bitcast(I32)
    return bass.AP(tensor=t, offset=ap.offset, ap=ap.ap)



def emit_rsqrt_act(p, out_ap, in_ap, tmp_ap, sS, eps_ap):
    p.do("s", lambda e: e.activation(tmp_ap, in_ap, AF.Ln, bias=eps_ap))
    p.do("s", lambda e: e.activation(out_ap, tmp_ap, AF.Exp, scale=-0.5), inc=sS)


def emit_rsqrt_dve(p, out_ap, in_ap, v_ap, t0_ap, t1_ap, eps=EPS):
    p.do("v", lambda e: e.tensor_single_scalar(v_ap, in_ap, float(eps), ALU.add))
    vi = _as_i32(v_ap)
    oi = _as_i32(out_ap)
    p.do("v", lambda e: e.tensor_single_scalar(oi, vi, 1, ALU.arith_shift_right))
    p.do("v", lambda e: e.tensor_scalar(oi, oi, -1, 0x5F3759DF, ALU.mult, ALU.add))
    for _ in range(3):
        p.do("v", lambda e: e.tensor_mul(t0_ap, out_ap, out_ap))
        p.do("v", lambda e: e.tensor_mul(t0_ap, t0_ap, v_ap))
        p.do("v", lambda e: e.tensor_scalar(t0_ap, t0_ap, -0.5, 1.5, ALU.mult, ALU.add))
        p.do("v", lambda e: e.tensor_mul(out_ap, out_ap, t0_ap))


def emit_ln_smalls(p, sm, sS, sV, eps_ap, double_ln=False, base=0, s1_guard=None):
    c_s1, c_s2, c_m, c_sc, c_a, c_b, c_c = (
        sm[:, base + i:base + i + 1] for i in range(7))
    if s1_guard is not None:
        p.wait_guard("s", s1_guard)
    else:
        p.wait("s", sV)
    p.do("s", lambda e: e.activation(c_m, c_s1, AF.Copy, scale=1.0 / D))
    p.do("s", lambda e: e.activation(c_a, c_s2, AF.Copy, scale=1.0 / D))
    p.do("s", lambda e: e.activation(c_b, c_m, AF.Square))
    p.do("s", lambda e: e.activation(c_c, c_b, AF.Relu, scale=-1.0, bias=c_a))
    p.do("s", lambda e: e.activation(c_a, c_c, AF.Ln, bias=eps_ap))
    if not double_ln:
        p.do("s", lambda e: e.activation(c_sc, c_a, AF.Exp, scale=-0.5), inc=sS)
    else:
        p.do("s", lambda e: e.activation(c_sc, c_a, AF.Exp, scale=-0.5))
        p.do("s", lambda e: e.activation(c_b, c_sc, AF.Square))
        p.do("s", lambda e: e.activation(c_b, c_b, AF.Copy, scale=c_c))
        p.do("s", lambda e: e.activation(c_a, c_b, AF.Ln, bias=eps_ap))
        p.do("s", lambda e: e.activation(c_b, c_a, AF.Exp, scale=-0.5))
        p.do("s", lambda e: e.activation(c_a, c_b, AF.Copy, scale=c_sc))
        p.do("s", lambda e: e.activation(c_sc, c_a, AF.Copy), inc=sS)




def emit_ln_smalls_batched(c, su0, sq0, mn0, sc0, n=NT, double=False):
    p = c.p
    sV, sS, sV2 = c.sV, c.sS, c.sV2
    sm = c.sm
    su = sm[:, su0:su0 + n]
    sq = sm[:, sq0:sq0 + n]
    mn = sm[:, mn0:mn0 + n]
    sc = sm[:, sc0:sc0 + n]
    p.wait("v", sS)
    p.wait("v", sV)
    p.do("v", lambda e: e.tensor_scalar(mn, su, 1.0 / D, None, ALU.mult))
    p.do("v", lambda e: e.tensor_scalar(sq, sq, 1.0 / D, None, ALU.mult), inc=sV2)
    p.wait("v", sV2)
    p.do("v", lambda e: e.tensor_mul(su, mn, mn), inc=sV2)
    p.wait("v", sV2)
    p.do("v", lambda e: e.tensor_sub(sq, sq, su), inc=sV2)
    p.wait("s", sV2)
    p.do("s", lambda e: e.activation(sc, sq, AF.Ln, bias=c.epsc[:]))
    p.do("s", lambda e: e.activation(sc, sc, AF.Exp, scale=-0.5), inc=sS)
    if double:
        p.wait("v", sS)
        p.do("v", lambda e: e.tensor_mul(su, sc, sc), inc=sV2)
        p.wait("v", sV2)
        p.do("v", lambda e: e.tensor_mul(su, su, sq), inc=sV2)
        p.wait("s", sV2)
        p.do("s", lambda e: e.activation(su, su, AF.Ln, bias=c.epsc[:]))
        p.do("s", lambda e: e.activation(su, su, AF.Exp, scale=-0.5), inc=sS)
        p.wait("v", sS)
        p.do("v", lambda e: e.tensor_mul(sc, sc, su), inc=sV2)
        return [sS.mark(), sV2.mark()]
    return [sS.mark()]


def emit_pre_ln_pass(c, s, x_in, xnT_dst, xslot_free, ps_ta_guard,
                     affine=False, sqdump=None):
    p = c.p
    sT, sV, sS = c.sT, c.sV, c.sS
    sqd = sqdump if sqdump is not None else c.scr[:]
    SU, SQ, MN, SC = 128, 144, 160, 176
    p.wait("s", c.sV2)
    for i in range(NT):
        sl = c.s_x[i % 2]
        p.wait_guard("y", xslot_free[i % 2])
        p.do("y", lambda e, i=i: e.dma_start(
            out=c.xt[i % 2][:], in_=x_in[s, 128 * i:128 * (i + 1), :]), inc=sl)
        p.wait("v", sl)
        p.do("v", lambda e, i=i: e.reduce_sum(
            c.sm[:, SU + i:SU + i + 1], c.xt[i % 2][:], axis=AX.X), inc=sV)
        p.wait("s", sl)
        p.do("s", lambda e, i=i, sqd=sqd: e.activation(
            sqd, c.xt[i % 2][:], AF.Square, accum_out=c.sm[:, SQ + i:SQ + i + 1]),
            inc=sS)
        xslot_free[i % 2] = [sV.mark(), sS.mark()]
    marks = emit_ln_smalls_batched(c, SU, SQ, MN, SC, double=affine)
    p.wait_guard("v", marks)
    xnb_free = [None, None]
    for i in range(NT):
        sl = c.s_x[i % 2]
        p.wait_guard("y", xslot_free[i % 2])
        p.do("y", lambda e, i=i: e.dma_start(
            out=c.xt[i % 2][:], in_=x_in[s, 128 * i:128 * (i + 1), :]), inc=sl)
        p.wait("v", sl)
        if affine:
            xnb = c.xn_bf[:]
            p.wait("v", sT)
            p.do("v", lambda e, i=i: e.tensor_scalar(
                c.xn_f[:], c.xt[i % 2][:], c.sm[:, MN + i:MN + i + 1],
                c.sm[:, SC + i:SC + i + 1], ALU.subtract, ALU.mult))
            p.do("v", lambda e: e.tensor_mul(c.xn_f[:], c.xn_f[:], c.gamma_b[:]))
            p.do("v", lambda e, xnb=xnb: e.tensor_add(
                xnb, c.xn_f[:], c.beta_b[:]), inc=sV)
        else:
            xnb = c.xn_bf[:] if i % 2 == 0 else c.xn_bf2
            p.wait_guard("v", xnb_free[i % 2])
            p.do("v", lambda e, i=i, xnb=xnb: e.tensor_scalar(
                xnb, c.xt[i % 2][:], c.sm[:, MN + i:MN + i + 1],
                c.sm[:, SC + i:SC + i + 1], ALU.subtract, ALU.mult), inc=sV)
        xslot_free[i % 2] = sV.mark()
        pst = c.PS_TA if i % 2 == 0 else c.PS_TB
        p.wait("t", sV)
        p.wait_guard("t", ps_ta_guard[i % 2])
        for cc in range(8):
            p.do("t", lambda e, cc=cc, xnb=xnb, pst=pst: e.matmul(
                pst[:, cc * 128:(cc + 1) * 128],
                xnb[:, cc * 128:(cc + 1) * 128],
                c.ident_bf[:], is_transpose=True, start=(cc == 0), stop=(cc == 7)),
                inc=sT if cc == 7 else None)
        xnb_free[i % 2] = sT.mark()
        p.wait("v", sT)
        p.do("v", lambda e, i=i, xnT_dst=xnT_dst, pst=pst: e.tensor_copy(
            xnT_dst[:, :, 128 * i:128 * (i + 1)],
            pst.rearrange("p (c f) -> p c f", c=8)), inc=sV)
        ps_ta_guard[i % 2] = sV.mark()


def emit_final_ln_pass(c, s, x_in, x_out, ozall, xslot_free):
    p = c.p
    sT, sV, sS = c.sT, c.sV, c.sS
    SU, SQ, MN, SC = 64, 80, 96, 112
    p.wait("s", sV)
    p.wait("s", c.sV2)
    for i in range(NT):
        oztile = ozall[i]
        p.do("v", lambda e, i=i, oztile=oztile: e.reduce_sum(
            c.sm[:, SU + i:SU + i + 1], oztile, axis=AX.X), inc=sV)
        p.do("s", lambda e, i=i, oztile=oztile: e.activation(
            c.scr[:], oztile, AF.Square, accum_out=c.sm[:, SQ + i:SQ + i + 1]),
            inc=sS)
    marks = emit_ln_smalls_batched(c, SU, SQ, MN, SC)
    p.wait_guard("v", marks)
    for i in range(NT):
        sl = c.s_x[i % 2]
        p.wait_guard("y", xslot_free[i % 2])
        p.do("y", lambda e, i=i: e.dma_start(
            out=c.xt[i % 2][:], in_=x_in[s, 128 * i:128 * (i + 1), :]), inc=sl)
        oztile = ozall[i]
        p.do("v", lambda e, i=i, oztile=oztile: e.tensor_scalar(
            c.oz[:], oztile, c.sm[:, MN + i:MN + i + 1],
            c.sm[:, SC + i:SC + i + 1], ALU.subtract, ALU.mult))
        p.wait("v", sl)
        p.do("v", lambda e, i=i: e.tensor_add(
            c.xt[i % 2][:], c.xt[i % 2][:], c.oz[:]), inc=sV)
        p.wait("y", sV)
        p.do("y", lambda e, i=i: e.dma_start(
            out=x_out[s, 128 * i:128 * (i + 1), :], in_=c.xt[i % 2][:]),
            inc=c.s_st[i % 2])
        xslot_free[i % 2] = (c.s_st[i % 2], c.s_st[i % 2].n)


def emit_final_pre_ln_pass(c, s0, s1, x_in, x_out, ozall, xslot_free, ps_ta_guard,
                           pre_x=None, xnT_dst=None, pre_wait_stores=False):
    p = c.p
    sT, sV, sS = c.sT, c.sV, c.sS
    if pre_x is None:
        pre_x = x_in
    if xnT_dst is None:
        xnT_dst = c.xnT
    ASU, ASQ, AMN, ASC = 64, 80, 96, 112
    BSU, BSQ, BMN, BSC = 128, 144, 160, 176
    xslot2_free = [None, None]
    p.wait("s", sV)
    p.wait("s", c.sV2)
    p.wait("y", sT)
    p.wait("y", sS)
    if pre_wait_stores:
        p.wait("y", c.s_st[0])
        p.wait("y", c.s_st[1])
    for i in range(NT):
        oztile = ozall[i]
        p.do("v", lambda e, i=i, oztile=oztile: e.reduce_sum(
            c.sm[:, ASU + i:ASU + i + 1], oztile, axis=AX.X), inc=sV)
        p.do("s", lambda e, i=i, oztile=oztile: e.activation(
            c.scr[:], oztile, AF.Square, accum_out=c.sm[:, ASQ + i:ASQ + i + 1]),
            inc=sS)
        sl2 = c.s_x2[i % 2]
        p.wait_guard("y", xslot2_free[i % 2])
        p.do("y", lambda e, i=i: e.dma_start(
            out=c.xt2[i % 2][:], in_=pre_x[s1, 128 * i:128 * (i + 1), :]), inc=sl2)
        p.wait("v", sl2)
        p.do("v", lambda e, i=i: e.reduce_sum(
            c.sm[:, BSU + i:BSU + i + 1], c.xt2[i % 2][:], axis=AX.X), inc=sV)
        p.wait("s", sl2)
        p.do("s", lambda e, i=i: e.activation(
            c.scr[:], c.xt2[i % 2][:], AF.Square,
            accum_out=c.sm[:, BSQ + i:BSQ + i + 1]), inc=sS)
        xslot2_free[i % 2] = [sV.mark(), sS.mark()]
    marksA = emit_ln_smalls_batched(c, ASU, ASQ, AMN, ASC)
    marksB = emit_ln_smalls_batched(c, BSU, BSQ, BMN, BSC)
    p.wait_guard("v", marksA)
    p.wait_guard("v", marksB)
    xnb_free = [None, None]
    for i in range(NT):
        sl = c.s_x[i % 2]
        p.wait_guard("y", xslot_free[i % 2])
        p.do("y", lambda e, i=i: e.dma_start(
            out=c.xt[i % 2][:], in_=x_in[s0, 128 * i:128 * (i + 1), :]), inc=sl)
        oztile = ozall[i]
        p.do("v", lambda e, i=i, oztile=oztile: e.tensor_scalar(
            c.oz[:], oztile, c.sm[:, AMN + i:AMN + i + 1],
            c.sm[:, ASC + i:ASC + i + 1], ALU.subtract, ALU.mult))
        p.wait("v", sl)
        p.do("v", lambda e, i=i: e.tensor_add(
            c.xt[i % 2][:], c.xt[i % 2][:], c.oz[:]), inc=sV)
        p.wait("y", sV)
        p.do("y", lambda e, i=i: e.dma_start(
            out=x_out[s0, 128 * i:128 * (i + 1), :], in_=c.xt[i % 2][:]),
            inc=c.s_st[i % 2])
        xslot_free[i % 2] = (c.s_st[i % 2], c.s_st[i % 2].n)
        sl2 = c.s_x2[i % 2]
        p.wait_guard("y", xslot2_free[i % 2])
        p.do("y", lambda e, i=i: e.dma_start(
            out=c.xt2[i % 2][:], in_=pre_x[s1, 128 * i:128 * (i + 1), :]), inc=sl2)
        xnb = c.xn_bf[:] if i % 2 == 0 else c.xn_bf2
        p.wait("v", sl2)
        p.wait_guard("v", xnb_free[i % 2])
        p.do("v", lambda e, i=i, xnb=xnb: e.tensor_scalar(
            xnb, c.xt2[i % 2][:], c.sm[:, BMN + i:BMN + i + 1],
            c.sm[:, BSC + i:BSC + i + 1], ALU.subtract, ALU.mult), inc=sV)
        xslot2_free[i % 2] = sV.mark()
        pst = c.PS_TA if i % 2 == 0 else c.PS_TB
        p.wait("t", sV)
        p.wait_guard("t", ps_ta_guard[i % 2])
        for cc in range(8):
            p.do("t", lambda e, cc=cc, xnb=xnb, pst=pst: e.matmul(
                pst[:, cc * 128:(cc + 1) * 128],
                xnb[:, cc * 128:(cc + 1) * 128],
                c.ident_bf[:], is_transpose=True, start=(cc == 0), stop=(cc == 7)),
                inc=sT if cc == 7 else None)
        xnb_free[i % 2] = sT.mark()
        p.wait("v", sT)
        p.do("v", lambda e, i=i, xnT_dst=xnT_dst, pst=pst: e.tensor_copy(
            xnT_dst[:, :, 128 * i:128 * (i + 1)],
            pst.rearrange("p (c f) -> p c f", c=8)), inc=sV)
        ps_ta_guard[i % 2] = sV.mark()



def build_l0_attn(debug=False):
    nc = bass.Bass("TRN2", target_bir_lowering=False)
    x_in = nc.dram_tensor("x_in", [NSEQ, T, D], F32, kind="ExternalInput")
    wqkvT = nc.dram_tensor("wqkvT", [D, 3 * D], F32, kind="ExternalInput")
    woutT = nc.dram_tensor("woutT", [D, D], F32, kind="ExternalInput")
    rows3 = nc.dram_tensor("rows3", [3, D], F32, kind="ExternalInput")
    ident = nc.dram_tensor("ident", [128, 128], F32, kind="ExternalInput")
    causal = nc.dram_tensor("causal", [128, 128], F32, kind="ExternalInput")
    ones1 = nc.dram_tensor("ones1", [1, 128], F32, kind="ExternalInput")
    x_out = nc.dram_tensor("x_out", [NSEQ, T, D], F32, kind="ExternalOutput")
    dbg = {}
    if debug:
        dbg["d_xnT"] = nc.dram_tensor("d_xnT", [128, 8, T], BF16, kind="ExternalOutput")
        dbg["d_Q"] = nc.dram_tensor("d_Q", [128, 8, T], BF16, kind="ExternalOutput")
        dbg["d_K"] = nc.dram_tensor("d_K", [128, 8, T], BF16, kind="ExternalOutput")
        dbg["d_V"] = nc.dram_tensor("d_V", [128, NT, 16 * 65], BF16, kind="ExternalOutput")

    c = Ctx = type("C", (), {})()
    c.dbg = dbg
    c.nc = nc
    A = nc.alloc_sbuf_tensor
    c.ident_bf = A("ident_bf", [128, 128], BF16)
    c.causal_bf = A("c_causal", [128, 128], BF16)
    c.ones1 = A("c_ones1", [1, 128], F32)
    c.onescol = A("onescol", [128, 1], BF16)
    c.gamma_b = A("gamma_b", [128, D], F32)
    c.beta_b = A("beta_b", [128, D], F32)
    c.bout_b = A("bout_b", [128, D], F32)
    c.xnT = A("xnT", [128, 8, T], BF16)
    c.w = A("w_sb", [128, 8, D], BF16)
    c.Q = A("Q_sb", [128, 8, T], BF16)
    c.K = A("K_sb", [128, 8, T], BF16)
    c.V = A("V_sb", [128, NT, 16 * 65], BF16)
    c.woutT = A("woutT_sb", [128, 8, D], BF16)
    c.PT = A("PT_sb", [128, T], BF16)
    c.PT2 = A("PT2_sb", [128, T], BF16)
    c.qq_sc = A("qq_sc", [128, T], BF16)
    c.att = A("att_sb", [128, D], BF16)
    c.oz = A("oz", [128, D], F32)
    c.xt = [A("xt0", [128, D], F32), A("xt1", [128, D], F32)]
    c.xn_bf = A("xn_bf", [128, D], BF16)
    c.scr = A("scr", [128, D], F32)
    c.xn_bf2 = c.scr.bitcast(BF16)[:, 0:D]
    c.xn_f = c.scr
    c.sm = A("sm", [128, 192], F32)
    c.epsc = A("epsc", [128, 1], F32)

    c.attT = c.PT
    c.PS_S = nc.alloc_psum_tensor("PS_S", [128, 2048], F32)
    c.PS_O = nc.alloc_psum_tensor("PS_O", [128, 1024], F32)
    c.PS_TA = nc.alloc_psum_tensor("PS_TA", [128, 1024], BF16)
    c.PS_TB = nc.alloc_psum_tensor("PS_TB", [128, 1024], BF16)

    p = Prog()
    c.p = p
    c.s_w = p.sem("s_w", 16)
    c.s_x = [p.sem("s_x0", 16), p.sem("s_x1", 16)]
    c.s_st = [p.sem("s_st0", 16), p.sem("s_st1", 16)]
    c.sT = p.sem("sT")
    c.sV = p.sem("sV")
    c.sS = p.sem("sS")
    c.sV2 = p.sem("sV2")

    p.do("v", lambda e: e.memset(c.epsc[:], EPS), inc=c.sV)
    p.do("v", lambda e: e.memset(c.onescol[:], 1.0), inc=c.sV)
    p.do("g", lambda e: e.dma_start(out=c.ident_bf[:], in_=ident[:]), inc=c.s_w)
    p.do("g", lambda e: e.dma_start(out=c.causal_bf[:], in_=causal[:]), inc=c.s_w)
    p.do("g", lambda e: e.dma_start(out=c.ones1[:], in_=ones1[:]), inc=c.s_w)
    p.do("g", lambda e: e.dma_start(
        out=c.woutT[:], in_=woutT.rearrange("(c p) d -> p c d", p=128)), inc=c.s_w)
    for i, dst in enumerate((c.gamma_b, c.beta_b, c.bout_b)):
        p.wait("g", c.sT)
        p.do("g", lambda e, i=i: e.dma_start(
            out=c.scr[0:1, :], in_=rows3[i:i + 1, :]), inc=c.s_w)
        p.wait("t", c.s_w)
        rsl = c.scr[0:1, :]
        p.do("t", lambda e, rsl=rsl: e.matmul(
            c.PS_O[:, 0:512], c.ones1[:], rsl[:, 0:512], start=True, stop=True))
        p.do("t", lambda e, rsl=rsl: e.matmul(
            c.PS_O[:, 512:1024], c.ones1[:], rsl[:, 512:1024], start=True, stop=True), inc=c.sT)
        p.wait("v", c.sT)
        p.do("v", lambda e, dst=dst: e.tensor_copy(dst[:], c.PS_O[:]), inc=c.sV)
        p.wait("t", c.sV)

    for s in range(NSEQ):
        emit_l0a_seq(c, s, x_in, wqkvT, x_out)
        if s == 0 and dbg:
            p.wait("y", c.sV)
            p.wait("y", c.sS)
            for nm, src in (("d_xnT", c.xnT), ("d_Q", c.Q), ("d_K", c.K), ("d_V", c.V)):
                p.do("y", lambda e, nm=nm, src=src: e.dma_start(
                    out=dbg[nm][:], in_=src[:]), inc=c.s_st[0])
    p.wait("y", c.s_st[0])
    p.wait("y", c.s_st[1])
    return nc, c


def emit_l0a_seq(c, s, x_in, wqkvT, x_out):
    p = c.p
    sT, sV, sS, sV2 = c.sT, c.sV, c.sS, c.sV2
    p.wait("s", sT)
    xslot_free = [None, None]

    def load_xtile(i, eng="y"):
        sl = c.s_x[i % 2]
        p.wait_guard(eng, xslot_free[i % 2])
        p.do(eng, lambda e, i=i: e.dma_start(
            out=c.xt[i % 2][:], in_=x_in[s, 128 * i:128 * (i + 1), :]), inc=sl)
        return sl

    ps_ta_free = [None, None]
    emit_pre_ln_pass(c, s, x_in, c.xnT, xslot_free, ps_ta_free,
                     affine=True, sqdump=c.oz[:])

    ps_half_free = [None, None]
    for e0, dst in ((0, "Q"), (D, "K"), (2 * D, "V")):
        p.wait("g", sT)
        p.do("g", lambda e, e0=e0: e.dma_start(
            out=c.w[:], in_=wqkvT.rearrange("(c p) n -> p c n", p=128)[:, :, e0:e0 + D]),
            inc=c.s_w)
        p.wait("t", c.s_w)
        p.wait("t", sV)
        if dst in ("Q", "K"):
            out_sb = c.Q if dst == "Q" else c.K
            for jg in range(4):
                for it in range(8):
                    half = (jg * 8 + it) % 2
                    ps = c.PS_S[:, half * 1024:half * 1024 + 512]
                    p.wait_guard("t", ps_half_free[half])
                    for kc in range(8):
                        p.do("t", lambda e, it=it, kc=kc, jg=jg, ps=ps: e.matmul(
                            ps, c.w[:, kc, it * 128:(it + 1) * 128],
                            c.xnT[:, kc, jg * 512:(jg + 1) * 512],
                            start=(kc == 0), stop=(kc == 7)),
                            inc=sT if kc == 7 else None)
                    if dst == "Q":
                        p.wait("s", sT)
                        p.do("s", lambda e, it=it, jg=jg, ps=ps, out_sb=out_sb: e.activation(
                            out_sb[:, it, jg * 512:(jg + 1) * 512], ps, AF.Copy,
                            scale=0.125), inc=sS)
                        ps_half_free[half] = sS.mark()
                    else:
                        p.wait("v", sT)
                        p.do("v", lambda e, it=it, jg=jg, ps=ps, out_sb=out_sb: e.tensor_copy(
                            out_sb[:, it, jg * 512:(jg + 1) * 512], ps), inc=sV)
                        ps_half_free[half] = sV.mark()
        else:
            vv = c.V.rearrange("p t (h e) -> p t h e", h=16)
            for i in range(NT):
                half = i % 2
                ps = c.PS_S[:, half * 1024:half * 1024 + 1024]
                p.wait_guard("t", ps_half_free[half])
                for kc in range(8):
                    for jh in range(2):
                        p.do("t", lambda e, i=i, kc=kc, jh=jh, ps=ps: e.matmul(
                            ps[:, jh * 512:(jh + 1) * 512],
                            c.xnT[:, kc, i * 128:(i + 1) * 128],
                            c.w[:, kc, jh * 512:(jh + 1) * 512],
                            start=(kc == 0), stop=(kc == 7)),
                            inc=sT if (kc == 7 and jh == 1) else None)
                p.wait("v", sT)
                p.do("v", lambda e, i=i, ps=ps, vv=vv: e.tensor_copy(
                    vv[:, i, :, 0:64], ps.rearrange("p (h e) -> p h e", h=16)))
                p.do("v", lambda e, i=i, vv=vv: e.memset(vv[:, i, :, 64:65], 1.0), inc=sV)
                ps_half_free[half] = sV.mark()

    sm = c.sm
    qsum_marks = []
    for pi, src in enumerate((c.Q, c.K)):
        for ch in range(8):
            p.wait("v", sT)
            p.do("v", lambda e, src=src, ch=ch: e.tensor_mul(
                c.qq_sc[:], src[:, ch, :], src[:, ch, :]), inc=sV)
            p.wait("t", sV)
            for j in range(4):
                p.do("t", lambda e, j=j: e.matmul(
                    c.PS_S[0:1, j * 512:(j + 1) * 512], c.onescol[:],
                    c.qq_sc[:, j * 512:(j + 1) * 512],
                    start=True, stop=True), inc=sT if j == 3 else None)
            p.wait("v", sT)
            for j in range(4):
                p.do("v", lambda e, ch=ch, j=j: e.reduce_max(
                    sm[0:1, 32 + ch * 4 + j:33 + ch * 4 + j],
                    c.PS_S[0:1, j * 512:(j + 1) * 512], axis=AX.X),
                    inc=sV2 if j == 3 else None)
        p.wait("v", sV2)
        p.do("v", lambda e, pi=pi: e.reduce_max(
            sm[0:1, 30 + pi:31 + pi], sm[0:1, 32:64], axis=AX.X), inc=sV2)
    p.wait("v", sV2)
    p.do("v", lambda e: e.tensor_mul(
        sm[0:1, 29:30], sm[0:1, 30:31], sm[0:1, 31:32]), inc=sV2)
    p.wait("s", sV2)
    p.do("s", lambda e: e.activation(sm[0:1, 28:29], sm[0:1, 29:30], AF.Ln))
    p.do("s", lambda e: e.activation(sm[0:1, 27:28], sm[0:1, 28:29], AF.Exp,
                                     scale=0.5), inc=sS)
    p.wait("t", sS)
    p.wait("t", sV)
    p.do("t", lambda e: e.matmul(
        c.PS_O[:, 0:1], c.ones1[:], sm[0:1, 27:28], start=True, stop=True), inc=sT)
    p.wait("v", sT)
    p.do("v", lambda e: e.tensor_scalar(
        sm[:, 13:14], c.PS_O[:, 0:1], -1.0, None, ALU.mult), inc=sV)
    negc_mark = sV.mark()
    c.l0a_qtr = [sV2.mark()] * 4

    ps_o_free = [None, None]
    pt_free = [None, None]
    ps_t_free = [ps_ta_free[0], None]
    sm_free = [None, None]
    for qi in range(NT):
        J = (qi + 1) * 128
        sl = load_xtile(qi)
        exp_s_mark = {}
        mask_v_mark = {}

        def emit_scores_exp(h):
            off = 64 * (h % 2)
            chk = h // 2
            PTb = c.PT if h % 2 == 0 else c.PT2
            base = 1024 * (h % 2) if J <= 1024 else 0
            p.wait_guard("s", pt_free[h % 2])
            p.wait_guard("s", negc_mark)
            for pc in range((qi + 4) // 4):
                k0 = pc * 4
                k1 = min(qi + 1, k0 + 4)
                qa = base // 512 + pc
                p.wait_guard("t", c.l0a_qtr[qa])
                for kc in range(k0, k1):
                    p.do("t", lambda e, kc=kc, off=off, chk=chk, qq=qi, base=base: e.matmul(
                        c.PS_S[:, base + kc * 128:base + (kc + 1) * 128],
                        c.K[off:off + 64, chk, kc * 128:(kc + 1) * 128],
                        c.Q[off:off + 64, chk, qq * 128:(qq + 1) * 128],
                        start=True, stop=True), inc=sT if kc == k1 - 1 else None)
                p.wait("s", sT)
                p.do("s", lambda e, k0=k0, k1=k1, PTb=PTb, base=base: e.activation(
                    PTb[:, k0 * 128:k1 * 128],
                    c.PS_S[:, base + k0 * 128:base + k1 * 128],
                    AF.Exp, bias=sm[:, 13:14]), inc=sS)
                c.l0a_qtr[qa] = sS.mark()
            exp_s_mark[h] = sS.mark()
            p.wait("v", sS)
            p.do("v", lambda e, J=J, PTb=PTb: e.tensor_mul(
                PTb[:, J - 128:J], PTb[:, J - 128:J], c.causal_bf[:]), inc=sV)
            mask_v_mark[h] = sV.mark()

        def emit_pv(h):
            o0 = 512 * (h % 2)
            PTb = c.PT if h % 2 == 0 else c.PT2
            p.wait_guard("t", exp_s_mark[h])
            p.wait_guard("t", mask_v_mark[h])
            p.wait_guard("t", ps_o_free[h % 2])
            for kc in range(qi + 1):
                p.do("t", lambda e, kc=kc, h=h, o0=o0, qq=qi, PTb=PTb: e.matmul(
                    c.PS_O[:, o0:o0 + 65],
                    PTb[:, kc * 128:(kc + 1) * 128],
                    c.V[:, kc, 65 * h:65 * h + 65],
                    start=(kc == 0), stop=(kc == qq)), inc=sT if kc == qi else None)
            pt_free[h % 2] = sT.mark()
            p.wait("v", sT)
            p.wait_guard("v", sm_free[h % 2])
            p.do("v", lambda e, o0=o0, h=h: e.reciprocal(
                c.sm[:, 8 + (h % 2):9 + (h % 2)], c.PS_O[:, o0 + 64:o0 + 65]),
                inc=c.sV2)
            p.wait("v", c.sV2)
            p.do("v", lambda e, h=h, o0=o0: e.tensor_scalar(
                c.att[:, 64 * h:64 * h + 64], c.PS_O[:, o0:o0 + 64],
                c.sm[:, 8 + (h % 2):9 + (h % 2)], None, ALU.mult), inc=sV)
            ps_o_free[h % 2] = sV.mark()
            sm_free[h % 2] = sV.mark()

        for hh in range(17):
            if hh < 16:
                emit_scores_exp(hh)
            if hh >= 1:
                emit_pv(hh - 1)
        p.wait("t", sV)
        p.wait_guard("t", ps_t_free[0])
        for cc in range(8):
            p.do("t", lambda e, cc=cc: e.matmul(
                c.PS_TA[:, cc * 128:(cc + 1) * 128], c.att[:, cc * 128:(cc + 1) * 128],
                c.ident_bf[:], is_transpose=True, start=(cc == 0), stop=(cc == 7)),
                inc=sT if cc == 7 else None)
        p.wait("v", sT)
        p.wait_guard("v", pt_free[0])
        p.do("v", lambda e: e.tensor_copy(c.attT[:, 0:1024], c.PS_TA[:]), inc=sV)
        ps_t_free[0] = sV.mark()
        p.wait("t", sV)
        p.wait_guard("t", ps_o_free[0])
        p.wait_guard("t", ps_o_free[1])
        for kc in range(8):
            for jh in range(2):
                p.do("t", lambda e, kc=kc, jh=jh: e.matmul(
                    c.PS_O[:, jh * 512:(jh + 1) * 512],
                    c.PT[:, kc * 128:(kc + 1) * 128],
                    c.woutT[:, kc, jh * 512:(jh + 1) * 512],
                    start=(kc == 0), stop=(kc == 7)),
                    inc=sT if (kc == 7 and jh == 1) else None)
        pt_free[0] = sT.mark()
        p.wait("v", sT)
        p.do("v", lambda e: e.tensor_add(c.oz[:], c.PS_O[:], c.bout_b[:]), inc=sV)
        ps_o_free[0] = ps_o_free[1] = sV.mark()
        p.do("v", lambda e: e.reduce_sum(c.sm[:, 0:1], c.oz[:], axis=AX.X), inc=sV)
        p.wait("s", sV)
        p.do("s", lambda e: e.activation(
            c.scr[:], c.oz[:], AF.Square, accum_out=c.sm[:, 1:2]), inc=sS)
        emit_ln_smalls(p, c.sm, sS, sV, c.epsc[:])
        p.wait("v", sS)
        p.do("v", lambda e: e.tensor_scalar(
            c.xn_f[:], c.oz[:], c.sm[:, 2:3], c.sm[:, 3:4], ALU.subtract, ALU.mult))
        p.wait("v", sl)
        p.do("v", lambda e, qi=qi: e.tensor_add(
            c.xt[qi % 2][:], c.xt[qi % 2][:], c.xn_f[:]), inc=sV)
        p.wait("y", sV)
        p.do("y", lambda e, qi=qi: e.dma_start(
            out=x_out[s, 128 * qi:128 * (qi + 1), :], in_=c.xt[qi % 2][:]),
            inc=c.s_st[qi % 2])
        xslot_free[qi % 2] = (c.s_st[qi % 2], c.s_st[qi % 2].n)


def finish_build(nc, c):
    with ExitStack() as st:
        for s in c.p.sems:
            s.h = st.enter_context(nc.semaphore(s.name))
        with nc.Block() as block:
            @block.sync
            def _(eng):
                c.p.replay("y", eng)

            @block.gpsimd
            def _(eng):
                c.p.replay("g", eng)

            @block.tensor
            def _(eng):
                c.p.replay("t", eng)

            @block.vector
            def _(eng):
                c.p.replay("v", eng)

            @block.scalar
            def _(eng):
                c.p.replay("s", eng)
    return nc



def build_l0_ffn():
    nc = bass.Bass("TRN2", target_bir_lowering=False)
    x_in = nc.dram_tensor("x_in", [NSEQ, T, D], F32, kind="ExternalInput")
    U1 = nc.dram_tensor("U1", [D, 256], F32, kind="ExternalInput")
    V1 = nc.dram_tensor("V1", [256, 4096], F32, kind="ExternalInput")
    U2 = nc.dram_tensor("U2", [4096, 256], F32, kind="ExternalInput")
    V2 = nc.dram_tensor("V2", [256, D], F32, kind="ExternalInput")
    b1T = nc.dram_tensor("b1T", [128, 32], F32, kind="ExternalInput")
    b2row = nc.dram_tensor("b2row", [1, D], F32, kind="ExternalInput")
    ident = nc.dram_tensor("ident", [128, 128], F32, kind="ExternalInput")
    ones1 = nc.dram_tensor("ones1", [1, 128], F32, kind="ExternalInput")
    x_out = nc.dram_tensor("x_out", [NSEQ, T, D], F32, kind="ExternalOutput")

    c = type("C", (), {})()
    c.nc = nc
    A = nc.alloc_sbuf_tensor
    c.ident_bf = A("ident_bf", [128, 128], BF16)
    c.ones1 = A("c_ones1", [1, 128], F32)
    c.b1fm = A("b1fm", [128, 32], F32)
    c.b2b = A("b2b", [128, D], F32)
    c.xnT = A("xnT", [128, 8, T], BF16)
    c.w1 = A("w1", [128, 8, 256], BF16)
    c.wv1 = A("wv1", [128, 2, 4096], BF16)
    c.w2 = A("w2", [128, 32, 256], BF16)
    c.wv2 = A("wv2", [128, 2, D], BF16)
    c.P2 = A("P2", [128, 2, T], BF16)
    c.H = A("H_sb", [128, 32, 512], BF16)
    c.Q2 = A("Q2", [128, 2, 512], BF16)
    c.ozall = A("ozall", [128, NT, D], BF16)
    c.oztiles = [c.ozall[:, i, :] for i in range(NT)]
    c.oz = A("oz", [128, D], F32)
    c.xt = [A("xt0", [128, D], F32), A("xt1", [128, D], F32)]
    c.xt2 = [A("xt2a", [128, D], F32), A("xt2b", [128, D], F32)]
    c.xn_bf = A("xn_bf", [128, D], BF16)
    c.scr = A("scr", [128, D], F32)
    c.xn_bf2 = c.scr.bitcast(BF16)[:, 0:D]
    c.sm = A("sm", [128, 192], F32)
    c.epsc = A("epsc", [128, 1], F32)

    c.PS_S = nc.alloc_psum_tensor("PS_S", [128, 2048], F32)
    c.PS_O = nc.alloc_psum_tensor("PS_O", [128, 1024], F32)
    c.PS_TA = nc.alloc_psum_tensor("PS_TA", [128, 1024], BF16)
    c.PS_TB = nc.alloc_psum_tensor("PS_TB", [128, 1024], BF16)

    p = Prog()
    c.p = p
    c.s_w = p.sem("s_w", 16)
    c.s_x = [p.sem("s_x0", 16), p.sem("s_x1", 16)]
    c.s_x2 = [p.sem("s_x2a", 16), p.sem("s_x2b", 16)]
    c.s_st = [p.sem("s_st0", 16), p.sem("s_st1", 16)]
    sT = c.sT = p.sem("sT")
    sV = c.sV = p.sem("sV")
    sS = c.sS = p.sem("sS")

    p.do("v", lambda e: e.memset(c.epsc[:], EPS), inc=sV)
    p.do("g", lambda e: e.dma_start(out=c.ident_bf[:], in_=ident[:]), inc=c.s_w)
    p.do("g", lambda e: e.dma_start(out=c.ones1[:], in_=ones1[:]), inc=c.s_w)
    p.do("g", lambda e: e.dma_start(out=c.b1fm[:], in_=b1T[:]), inc=c.s_w)
    p.do("g", lambda e: e.dma_start(
        out=c.w1[:], in_=U1.rearrange("(c p) r -> p c r", p=128)), inc=c.s_w)
    p.do("g", lambda e: e.dma_start(
        out=c.wv1[:], in_=V1.rearrange("(c p) f -> p c f", p=128)), inc=c.s_w)
    p.do("g", lambda e: e.dma_start(
        out=c.w2[:], in_=U2.rearrange("(c p) r -> p c r", p=128)), inc=c.s_w)
    p.do("g", lambda e: e.dma_start(
        out=c.wv2[:], in_=V2.rearrange("(c p) d -> p c d", p=128)), inc=c.s_w)
    p.do("g", lambda e: e.dma_start(out=c.scr[0:1, :], in_=b2row[:]), inc=c.s_w)
    p.wait("t", c.s_w)
    p.do("t", lambda e: e.matmul(c.PS_O[:, 0:512], c.ones1[:], c.scr[0:1, 0:512], start=True, stop=True))
    p.do("t", lambda e: e.matmul(c.PS_O[:, 512:1024], c.ones1[:], c.scr[0:1, 512:1024], start=True, stop=True), inc=sT)
    p.wait("v", sT)
    p.do("v", lambda e: e.tensor_copy(c.b2b[:], c.PS_O[:]), inc=sV)
    p.wait("t", sV)

    emit_l0f_block(c, x_in, x_out)
    p.wait("y", c.s_st[0])
    p.wait("y", c.s_st[1])
    return nc, c


def emit_l0f_block(c, x_in, x_out):
    xslot_free = [None, None]
    ps_ta_free = [None, None]
    emit_pre_ln_pass(c, 0, x_in, c.xnT, xslot_free, ps_ta_free)
    emit_l0f_core(c)
    emit_final_pre_ln_pass(c, 0, 1, x_in, x_out, c.oztiles, xslot_free, ps_ta_free)
    emit_l0f_core(c)
    emit_final_ln_pass(c, 1, x_in, x_out, c.oztiles, xslot_free)


def emit_l0f_core(c):
    p = c.p
    sT, sV, sS = c.sT, c.sV, c.sS
    if not hasattr(c, "f1_ps_o"):
        c.f1_ps_o = None

    q_free = [None, None, None, None]
    qi_rr = [0]

    def quarter():
        i = qi_rr[0] % 4
        qi_rr[0] += 1
        return i, c.PS_S[:, i * 512:(i + 1) * 512]

    p.wait("t", sV)
    for jg in range(4):
        for it in range(2):
            qi, ps = quarter()
            p.wait_guard("t", q_free[qi])
            for kc in range(8):
                p.do("t", lambda e, it=it, kc=kc, jg=jg, ps=ps: e.matmul(
                    ps, c.w1[:, kc, it * 128:(it + 1) * 128],
                    c.xnT[:, kc, jg * 512:(jg + 1) * 512],
                    start=(kc == 0), stop=(kc == 7)),
                    inc=sT if kc == 7 else None)
            p.wait("v", sT)
            p.do("v", lambda e, it=it, jg=jg, ps=ps: e.tensor_copy(
                c.P2[:, it, jg * 512:(jg + 1) * 512], ps), inc=sV)
            q_free[qi] = sV.mark()

    for g in range(4):
        for it in range(32):
            qi, ps = quarter()
            p.wait_guard("t", q_free[qi])
            for kc in range(2):
                p.do("t", lambda e, it=it, kc=kc, g=g, ps=ps: e.matmul(
                    ps, c.wv1[:, kc, it * 128:(it + 1) * 128],
                    c.P2[:, kc, g * 512:(g + 1) * 512],
                    start=(kc == 0), stop=(kc == 1)),
                    inc=sT if kc == 1 else None)
            p.wait("s", sT)
            p.do("s", lambda e, it=it, ps=ps: e.activation(
                c.H[:, it, :], ps, AF.Gelu_apprx_tanh, bias=c.b1fm[:, it:it + 1]), inc=sS)
            q_free[qi] = sS.mark()
        for it in range(2):
            qi, ps = quarter()
            p.wait_guard("t", q_free[qi])
            p.wait("t", sS)
            for kc in range(32):
                p.do("t", lambda e, it=it, kc=kc, ps=ps: e.matmul(
                    ps, c.w2[:, kc, it * 128:(it + 1) * 128],
                    c.H[:, kc, :],
                    start=(kc == 0), stop=(kc == 31)),
                    inc=sT if kc == 31 else None)
            p.wait("v", sT)
            p.do("v", lambda e, it=it, ps=ps: e.tensor_copy(
                c.Q2[:, it, :], ps), inc=sV)
            q_free[qi] = sV.mark()
        for tt in range(4):
            i = g * 4 + tt
            p.wait_guard("t", c.f1_ps_o)
            p.wait("t", sV)
            for kc in range(2):
                for jh in range(2):
                    p.do("t", lambda e, i=i, kc=kc, jh=jh, tt=tt: e.matmul(
                        c.PS_O[:, jh * 512:(jh + 1) * 512],
                        c.Q2[:, kc, tt * 128:(tt + 1) * 128],
                        c.wv2[:, kc, jh * 512:(jh + 1) * 512],
                        start=(kc == 0), stop=(kc == 1)),
                        inc=sT if (kc == 1 and jh == 1) else None)
            p.wait("v", sT)
            p.do("v", lambda e, i=i: e.tensor_add(
                c.oztiles[i], c.PS_O[:], c.b2b[:]), inc=sV)
            c.f1_ps_o = sV.mark()



H1 = 8
DH1 = 128
VD = 2048
VH = 256


def carve(arena, off_bytes, dtype, shape):
    t = arena.bitcast(dtype)
    esz = mybir.dt.size(dtype)
    start = off_bytes // esz
    n = int(np.prod(shape[1:]))
    ap = t[:, start:start + n]
    if len(shape) == 3:
        ap = ap.rearrange("p (a b) -> p a b", a=shape[1])
    return ap



L1A_ARENA_BYTES = 1280 + 131328 + 59392


def alloc_l1a(c, base=0):
    ar = c.ar
    c.causal_bf = carve(ar, base + 0, BF16, [128, 128])
    c.Btab = carve(ar, base + 256, F32, [128, NT, H1])
    c.Atab = carve(ar, base + 768, F32, [128, NT, H1])
    k0 = base + 1280
    c.krT = carve(ar, k0, BF16, [128, H1, T])
    c.qrT = carve(ar, k0 + 32768, BF16, [128, H1, T])
    c.V = carve(ar, k0 + 65536, BF16, [128, NT, H1 * 257])
    sub = k0 + 65536 + 65792
    c.xnT = carve(ar, sub, BF16, [128, 8, T])
    c.wo = carve(ar, sub, BF16, [128, 16, D])
    c.wbuf = carve(ar, sub + 32 * 1024, BF16, [128, 8, D])
    c.qsh = carve(ar, sub + 32 * 1024, BF16, [128, T])
    c.gstr = [carve(ar, sub + 32 * 1024, BF16, [128, VD]),
              carve(ar, sub + 36 * 1024, BF16, [128, VD])]
    c.Oall = carve(ar, sub + 40 * 1024, BF16, [128, H1 * 256])
    c.PT2 = carve(ar, sub + 44 * 1024, BF16, [128, T])
    c.cos = carve(ar, sub + 48 * 1024, BF16, [128, T])
    c.PT = carve(ar, sub + 48 * 1024, BF16, [128, T])
    c.sin = carve(ar, sub + 52 * 1024, BF16, [128, T])
    c.att = carve(ar, sub + 52 * 1024, BF16, [128, T])
    c.gb = [carve(ar, sub + 48 * 1024, BF16, [128, VD]),
            carve(ar, sub + 52 * 1024, BF16, [128, VD])]
    c.qh = carve(ar, sub + 56 * 1024, BF16, [128, D])
    c.xn_bf = c.xn_bf_t[:]
    c.scrA = c.scr[:]
    c.sqd = c.scr[:]
    c.scrB = c.oz[:]


def build_l1_attn(debug=False):
    nc = bass.Bass("TRN2", target_bir_lowering=False)
    x_in = nc.dram_tensor("x_in", [NSEQ, T, D], F32, kind="ExternalInput")
    wq = nc.dram_tensor("wq", [D, D], F32, kind="ExternalInput")
    wk = nc.dram_tensor("wk", [D, D], F32, kind="ExternalInput")
    wv = nc.dram_tensor("wv", [D, VD], F32, kind="ExternalInput")
    wg = nc.dram_tensor("wg", [D, VD], F32, kind="ExternalInput")
    wo = nc.dram_tensor("wo", [VD, D], F32, kind="ExternalInput")
    cosf = nc.dram_tensor("cosf", [128, T], F32, kind="ExternalInput")
    sinf = nc.dram_tensor("sinf", [128, T], F32, kind="ExternalInput")
    Btab_d = nc.dram_tensor("Btab", [128, NT, H1], F32, kind="ExternalInput")
    Atab_d = nc.dram_tensor("Atab", [128, NT, H1], F32, kind="ExternalInput")
    causal01 = nc.dram_tensor("causal01", [128, 128], F32, kind="ExternalInput")
    ident = nc.dram_tensor("ident", [128, 128], F32, kind="ExternalInput")
    x_out = nc.dram_tensor("x_out", [NSEQ, T, D], F32, kind="ExternalOutput")
    g_d = nc.dram_tensor("g_d", [NT, 128, VD], BF16)
    dbg = {}
    if debug:
        for nm, shp in (("d_xnT", [128, 8, T]), ("d_qrT", [128, H1, T]),
                        ("d_krT", [128, H1, T]), ("d_V", [128, NT, H1 * 257]),
                        ("d_g", [NT, 128, VD])):
            dbg[nm] = nc.dram_tensor(nm, shp, BF16, kind="ExternalOutput")

    c = type("C", (), {})()
    c.nc = nc
    A = nc.alloc_sbuf_tensor
    c.ident_bf = A("ident_bf", [128, 128], BF16)
    c.epsc = A("epsc", [128, 1], F32)
    c.cm1 = A("cm1", [128, 1], F32)
    c.cp1 = A("cp1", [128, 1], F32)
    c.xt = [A("xt0", [128, D], F32), A("xt1", [128, D], F32)]
    c.scr = A("scr", [128, D], F32)
    c.xn_bf2 = c.scr.bitcast(BF16)[:, 0:D]
    c.oz = A("oz", [128, D], F32)
    c.xn_bf_t = A("xn_bf_t", [128, D], BF16)
    c.sm = A("sm", [128, 192], F32)
    c.ar = A("arena", [128, L1A_ARENA_BYTES], mybir.dt.uint8)
    alloc_l1a(c, 0)

    c.PS_S = nc.alloc_psum_tensor("PS_S", [128, 2048], F32)
    c.PS_O = nc.alloc_psum_tensor("PS_O", [128, 1024], F32)
    c.PS_TA = nc.alloc_psum_tensor("PS_TA", [128, 1024], BF16)
    c.PS_TB = nc.alloc_psum_tensor("PS_TB", [128, 1024], BF16)

    p = Prog()
    c.p = p
    c.s_w = p.sem("s_w", 16)
    c.s_x = [p.sem("s_x0", 16), p.sem("s_x1", 16)]
    c.s_st = [p.sem("s_st0", 16), p.sem("s_st1", 16)]
    c.s_g = [p.sem("s_g0", 16), p.sem("s_g1", 16)]
    sT = c.sT = p.sem("sT")
    sV = c.sV = p.sem("sV")
    sS = c.sS = p.sem("sS")
    c.sV2 = p.sem("sV2")

    p.do("v", lambda e: e.memset(c.epsc[:], EPS), inc=sV)
    p.do("v", lambda e: e.memset(c.cm1[:], -1.0), inc=sV)
    p.do("v", lambda e: e.memset(c.cp1[:], 1.0), inc=sV)
    p.do("g", lambda e: e.dma_start(out=c.ident_bf[:], in_=ident[:]), inc=c.s_w)
    p.do("g", lambda e: e.dma_start(out=c.causal_bf, in_=causal01[:]), inc=c.s_w)
    p.do("g", lambda e: e.dma_start(out=c.Btab, in_=Btab_d[:]), inc=c.s_w)
    p.do("g", lambda e: e.dma_start(out=c.Atab, in_=Atab_d[:]), inc=c.s_w)

    for s in range(NSEQ):
        emit_l1a_seq(c, s, x_in, x_out, (wq, wk, wv, wg, wo), (cosf, sinf), g_d)
        if s == 0 and dbg:
            p.wait("y", c.sV)
            p.wait("y", c.sS)
            for nm, src in (("d_qrT", c.qrT[:]), ("d_krT", c.krT[:]), ("d_V", c.V[:])):
                p.do("y", lambda e, nm=nm, src=src: e.dma_start(
                    out=dbg[nm][:], in_=src), inc=c.s_st[0])
            p.do("y", lambda e: e.dma_start(out=dbg["d_g"][:], in_=g_d[:]), inc=c.s_st[0])
    p.wait("y", c.s_st[0])
    p.wait("y", c.s_st[1])
    c.dbg = dbg
    return nc, c


def emit_l1a_seq(c, s, x_in, x_out, ws, trig, g_d, pre_done=False):
    p = c.p
    sT, sV, sS, sV2 = c.sT, c.sV, c.sS, c.sV2
    wq, wk, wv, wg, wo = ws
    cosf, sinf = trig
    xslot_free = [None, None]

    def load_xtile(i):
        sl = c.s_x[i % 2]
        p.wait_guard("y", xslot_free[i % 2])
        p.do("y", lambda e, i=i: e.dma_start(
            out=c.xt[i % 2][:], in_=x_in[s, 128 * i:128 * (i + 1), :]), inc=sl)
        return sl

    ps_ta_free = [None, None]
    if not pre_done:
        emit_pre_ln_pass(c, s, x_in, c.xnT, xslot_free, ps_ta_free, sqdump=c.sqd)

    p.wait("g", sT)
    p.wait("g", sV)
    p.do("g", lambda e: e.dma_start(out=c.cos, in_=cosf[:]), inc=c.s_w)
    p.do("g", lambda e: e.dma_start(out=c.sin, in_=sinf[:]), inc=c.s_w)

    ps_half_free = [None, None]
    for which in ("q", "k"):
        wsl = wq if which == "q" else wk
        dst = c.qrT if which == "q" else c.krT
        p.wait("g", sT)
        p.wait("g", sV)
        p.do("g", lambda e, wsl=wsl: e.dma_start(
            out=c.wbuf, in_=wsl.rearrange("(c p) n -> p c n", p=128)), inc=c.s_w)
        p.wait("t", c.s_w)
        p.wait("t", sV)
        for i in range(NT):
            half = i % 2
            ps = c.PS_S[:, half * 1024:half * 1024 + 1024]
            p.wait_guard("t", ps_half_free[half])
            for kc in range(8):
                for jh in range(2):
                    p.do("t", lambda e, i=i, kc=kc, jh=jh, ps=ps: e.matmul(
                        ps[:, jh * 512:(jh + 1) * 512],
                        c.xnT[:, kc, i * 128:(i + 1) * 128],
                        c.wbuf[:, kc, jh * 512:(jh + 1) * 512],
                        start=(kc == 0), stop=(kc == 7)),
                        inc=sT if (kc == 7 and jh == 1) else None)
            p.wait("s", sT)
            if which == "k":
                for h in range(H1):
                    p.do("s", lambda e, i=i, h=h, ps=ps: e.activation(
                        c.qh[:, h * 128:(h + 1) * 128], ps[:, h * 128:(h + 1) * 128],
                        AF.Copy, scale=c.Btab[:, i, h:h + 1]), inc=sS if h == H1 - 1 else None)
            else:
                p.do("s", lambda e, i=i, ps=ps: e.activation(
                    c.qh[:, 0:1024], ps, AF.Copy), inc=sS)
            ps_half_free[half] = sS.mark()
            p.wait("t", sS)
            p.wait_guard("t", ps_ta_free[1])
            for cc in range(8):
                p.do("t", lambda e, cc=cc: e.matmul(
                    c.PS_TB[:, cc * 128:(cc + 1) * 128],
                    c.qh[:, cc * 128:(cc + 1) * 128],
                    c.ident_bf[:], is_transpose=True, start=(cc == 0), stop=(cc == 7)),
                    inc=sT if cc == 7 else None)
            p.wait("v", sT)
            p.do("v", lambda e, i=i, dst=dst: e.tensor_copy(
                dst[:, :, 128 * i:128 * (i + 1)],
                c.PS_TB.rearrange("p (c f) -> p c f", c=8)), inc=sV)
            ps_ta_free[1] = sV.mark()
        p.wait("v", c.s_w)
        p.wait("v", sT)
        for h in range(H1):
            row = dst[:, h, :]
            p.do("v", lambda e, row=row: e.stream_shuffle(
                c.qsh, row, [i + 1 - 2 * (i % 2) for i in range(32)]))
            p.do("v", lambda e: e.tensor_mul(c.qsh, c.qsh, c.sin))
            p.do("v", lambda e, row=row: e.tensor_mul(row, row, c.cos))
            p.do("v", lambda e, row=row: e.tensor_add(row, row, c.qsh), inc=sV)

    vv = c.V.rearrange("p t (h e) -> p t h e", h=H1)
    for vh in range(2):
        p.wait("g", sT)
        p.wait("g", sV)
        p.do("g", lambda e, vh=vh: e.dma_start(
            out=c.wbuf, in_=wv.rearrange("(c p) n -> p c n", p=128)[:, :, vh * 1024:(vh + 1) * 1024]),
            inc=c.s_w)
        p.wait("t", c.s_w)
        p.wait("t", sV)
        for i in range(NT):
            half = i % 2
            ps = c.PS_S[:, half * 1024:half * 1024 + 1024]
            p.wait_guard("t", ps_half_free[half])
            for kc in range(8):
                for jh in range(2):
                    p.do("t", lambda e, i=i, kc=kc, jh=jh, ps=ps: e.matmul(
                        ps[:, jh * 512:(jh + 1) * 512],
                        c.xnT[:, kc, i * 128:(i + 1) * 128],
                        c.wbuf[:, kc, jh * 512:(jh + 1) * 512],
                        start=(kc == 0), stop=(kc == 7)),
                        inc=sT if (kc == 7 and jh == 1) else None)
            p.wait("v", sT)
            p.do("v", lambda e, i=i, vh=vh, ps=ps: e.tensor_copy(
                vv[:, i, 4 * vh:4 * vh + 4, 0:256],
                ps.rearrange("p (h e) -> p h e", h=4)), inc=sV)
            ps_half_free[half] = sV.mark()
    p.do("v", lambda e: e.memset(vv[:, :, :, 256:257], 1.0), inc=sV)

    gslot_free = [None, None]
    for vh in range(2):
        p.wait("g", sT)
        p.wait("g", sV)
        p.do("g", lambda e, vh=vh: e.dma_start(
            out=c.wbuf, in_=wg.rearrange("(c p) n -> p c n", p=128)[:, :, vh * 1024:(vh + 1) * 1024]),
            inc=c.s_w)
        p.wait("t", c.s_w)
        for i in range(NT):
            half = i % 2
            ps = c.PS_S[:, half * 1024:half * 1024 + 1024]
            p.wait_guard("t", ps_half_free[half])
            for kc in range(8):
                for jh in range(2):
                    p.do("t", lambda e, i=i, kc=kc, jh=jh, ps=ps: e.matmul(
                        ps[:, jh * 512:(jh + 1) * 512],
                        c.xnT[:, kc, i * 128:(i + 1) * 128],
                        c.wbuf[:, kc, jh * 512:(jh + 1) * 512],
                        start=(kc == 0), stop=(kc == 7)),
                        inc=sT if (kc == 7 and jh == 1) else None)
            p.wait("s", sT)
            gb = c.gb[i % 2]
            p.wait_guard("s", gslot_free[i % 2])
            p.do("s", lambda e, ps=ps, gb=gb, vh=vh: e.activation(
                gb[:, vh * 1024:(vh + 1) * 1024], ps, AF.Silu), inc=sS)
            ps_half_free[half] = sS.mark()
            p.wait("y", sS)
            p.do("y", lambda e, i=i, vh=vh, gb=gb: e.dma_start(
                out=g_d[i][:, vh * 1024:(vh + 1) * 1024],
                in_=gb[:, vh * 1024:(vh + 1) * 1024]), inc=c.s_g[i % 2])
            gslot_free[i % 2] = c.s_g[i % 2].mark()

    p.wait("g", sT)
    p.do("g", lambda e: e.dma_start(
        out=c.wo, in_=wo.rearrange("(c p) d -> p c d", p=128)), inc=c.s_w)
    for ss_ in (0, 1):
        p.wait("s", c.s_g[ss_])
        p.wait("v", c.s_g[ss_])
        p.wait("y", c.s_g[ss_])
    p.wait("y", sV)
    p.wait("y", sT)
    c.b4_qtr = [ps_half_free[0], ps_half_free[1],
                ps_half_free[0], ps_half_free[1]]
    ps_o_free = [None, None]
    ps_t_free = [None, None]
    gstream_free = [gslot_free[0], gslot_free[1]]
    oall_free = None
    for qi in range(NT):
        J = (qi + 1) * 128
        sl = load_xtile(qi)
        sg = c.s_g[qi % 2]
        p.wait_guard("y", gstream_free[qi % 2])
        p.do("y", lambda e, qi=qi: e.dma_start(
            out=c.gstr[qi % 2], in_=g_d[qi][:]), inc=sg)
        p.wait("t", c.s_w)
        evac_mark = {}
        pt_bufs = [c.PT, c.PT2]

        def b4_scores_evac(h, qq, JJ):
            PTb = pt_bufs[h % 2]
            eng = "s" if h % 2 == 0 else "v"
            sE = sS if h % 2 == 0 else sV
            base = 1024 * (h % 2) if JJ <= 1024 else 0
            for jq in range((qq + 4) // 4):
                k0 = jq * 4
                k1 = min(qq + 1, k0 + 4)
                qa = base // 512 + jq
                p.wait_guard("t", c.b4_qtr[qa])
                for kc in range(k0, k1):
                    p.do("t", lambda e, h=h, qq=qq, kc=kc, base=base: e.matmul(
                        c.PS_S[:, base + kc * 128:base + (kc + 1) * 128],
                        c.krT[:, h, kc * 128:(kc + 1) * 128],
                        c.qrT[:, h, qq * 128:(qq + 1) * 128],
                        start=True, stop=True), inc=sT if kc == k1 - 1 else None)
                p.wait(eng, sT)
                if eng == "s":
                    p.do("s", lambda e, k0=k0, k1=k1, PTb=PTb, base=base: e.activation(
                        PTb[:, k0 * 128:k1 * 128],
                        c.PS_S[:, base + k0 * 128:base + k1 * 128],
                        AF.Copy), inc=sS)
                else:
                    p.do("v", lambda e, k0=k0, k1=k1, PTb=PTb, base=base: e.tensor_copy(
                        PTb[:, k0 * 128:k1 * 128],
                        c.PS_S[:, base + k0 * 128:base + k1 * 128]),
                        inc=sV)
                c.b4_qtr[qa] = sE.mark()
            if eng == "s":
                p.wait("v", sS)
            p.do("v", lambda e, JJ=JJ, PTb=PTb: e.tensor_mul(
                PTb[:, JJ - 128:JJ], PTb[:, JJ - 128:JJ], c.causal_bf[:]), inc=sV)
            evac_mark[h] = [sE.mark(), sV.mark()]

        def b4_pv(h, qq, JJ):
            nonlocal oall_free
            PTb = pt_bufs[h % 2]
            o0 = 512 * (h % 2)
            p.wait_guard("t", evac_mark[h])
            p.wait_guard("t", ps_o_free[h % 2])
            for kc in range(qq + 1):
                p.do("t", lambda e, kc=kc, h=h, o0=o0, qq=qq, PTb=PTb: e.matmul(
                    c.PS_O[:, o0:o0 + 257],
                    PTb[:, kc * 128:(kc + 1) * 128],
                    c.V[:, kc, 257 * h:257 * h + 257],
                    start=(kc == 0), stop=(kc == qq)), inc=sT if kc == qq else None)
            p.wait("v", sT)
            p.wait_guard("v", oall_free)
            p.do("v", lambda e, h=h, o0=o0, qq=qq: e.tensor_scalar(
                c.Oall[:, 256 * h:256 * (h + 1)], c.PS_O[:, o0:o0 + 256],
                c.Atab[:, qq, h:h + 1], None, ALU.mult))
            p.do("v", lambda e, h=h, o0=o0, qq=qq: e.tensor_scalar(
                c.sm[:, 40 + h:41 + h], c.PS_O[:, o0 + 256:o0 + 257],
                c.Atab[:, qq, h:h + 1], None, ALU.mult), inc=sV)
            ps_o_free[h % 2] = sV.mark()
            p.wait("s", sV)
            p.do("s", lambda e, h=h: e.activation(
                c.sqd[:, 0:256], c.Oall[:, 256 * h:256 * (h + 1)], AF.Square,
                accum_out=c.sm[:, 32 + h:33 + h]), inc=sS)

        for hh in range(H1 + 1):
            if hh < H1:
                b4_scores_evac(hh, qi, J)
            if hh >= 1:
                b4_pv(hh - 1, qi, J)
        ooall = c.Oall.rearrange("p (h e) -> p h e", h=H1)
        sm = c.sm
        p.wait("s", sS)
        p.wait("s", sV)
        _se = float(16.0 * EPS ** 0.5)
        p.do("s", lambda e: e.activation(sm[:, 8:16], sm[:, 40:48], AF.Abs))
        p.do("s", lambda e: e.activation(sm[:, 8:16], sm[:, 8:16], AF.Relu, bias=c.cm1[:]))
        p.do("s", lambda e, _se=_se: e.activation(
            sm[:, 16:24], sm[:, 8:16], AF.Copy, scale=_se, bias=_se))
        p.do("s", lambda e: e.activation(
            sm[:, 16:24], sm[:, 16:24], AF.Square), inc=sS)
        p.wait("v", sS)
        p.do("v", lambda e: e.tensor_add(
            sm[:, 24:32], sm[:, 32:40], sm[:, 16:24]), inc=sV2)
        p.wait("s", sV2)
        p.do("s", lambda e: e.activation(
            sm[:, 16:24], sm[:, 24:32], AF.Ln, scale=1.0 / 256.0))
        p.do("s", lambda e: e.activation(
            sm[:, 24:32], sm[:, 16:24], AF.Exp, scale=-0.5), inc=sS)
        p.wait("s", sS)
        for h in range(H1):
            p.do("s", lambda e, h=h: e.activation(
                c.att[:, VH * h:VH * (h + 1)], ooall[:, h, :], AF.Copy,
                scale=sm[:, 24 + h:25 + h]), inc=sS if h == H1 - 1 else None)
        oall_free = sS.mark()
        p.wait("v", sS)
        p.wait("v", sg)
        p.do("v", lambda e, qi=qi: e.tensor_mul(c.att, c.att, c.gstr[qi % 2]), inc=sV)
        gstream_free[qi % 2] = sV.mark()
        p.wait("t", sV)
        for half in range(2):
            pst = c.PS_TA if half == 0 else c.PS_TB
            p.wait_guard("t", ps_t_free[half])
            for cc in range(8):
                k = half * 8 + cc
                p.do("t", lambda e, k=k, cc=cc, pst=pst: e.matmul(
                    pst[:, cc * 128:(cc + 1) * 128],
                    c.att[:, k * 128:(k + 1) * 128],
                    c.ident_bf[:], is_transpose=True, start=(cc == 0), stop=(cc == 7)),
                    inc=sT if cc == 7 else None)
            p.wait("v", sT)
            p.do("v", lambda e, half=half, pst=pst: e.tensor_copy(
                c.PT[:, half * 1024:(half + 1) * 1024], pst[:]), inc=sV)
            ps_t_free[half] = sV.mark()
        p.wait("t", sV)
        p.wait_guard("t", ps_o_free[0])
        p.wait_guard("t", ps_o_free[1])
        for kc in range(16):
            for jh in range(2):
                p.do("t", lambda e, kc=kc, jh=jh: e.matmul(
                    c.PS_O[:, jh * 512:(jh + 1) * 512],
                    c.PT[:, kc * 128:(kc + 1) * 128],
                    c.wo[:, kc, jh * 512:(jh + 1) * 512],
                    start=(kc == 0), stop=(kc == 15)),
                    inc=sT if (kc == 15 and jh == 1) else None)
        p.wait("v", sT)
        p.do("v", lambda e: e.reduce_sum(c.sm[:, 0:1], c.PS_O[:], axis=AX.X), inc=sV)
        p.wait("s", sV)
        p.do("s", lambda e: e.activation(
            c.scrA, c.PS_O[:], AF.Square, accum_out=c.sm[:, 1:2]), inc=sS)
        emit_ln_smalls(p, c.sm, sS, sV, c.epsc[:])
        p.wait("v", sS)
        p.do("v", lambda e: e.tensor_scalar(
            c.scrB, c.PS_O[:], c.sm[:, 2:3], c.sm[:, 3:4], ALU.subtract, ALU.mult), inc=sV)
        ps_o_free[0] = ps_o_free[1] = sV.mark()
        p.wait("v", sl)
        p.do("v", lambda e, qi=qi: e.tensor_add(
            c.xt[qi % 2][:], c.xt[qi % 2][:], c.scrB), inc=sV)
        p.wait("y", sV)
        p.do("y", lambda e, qi=qi: e.dma_start(
            out=x_out[s, 128 * qi:128 * (qi + 1), :], in_=c.xt[qi % 2][:]),
            inc=c.s_st[qi % 2])
        xslot_free[qi % 2] = (c.s_st[qi % 2], c.s_st[qi % 2].n)



def build_l1_ffn():
    nc = bass.Bass("TRN2", target_bir_lowering=False)
    x_in = nc.dram_tensor("x_in", [NSEQ, T, D], F32, kind="ExternalInput")
    WrT = nc.dram_tensor("WrT", [D, D], F32, kind="ExternalInput")
    WkT = nc.dram_tensor("WkT", [D, D], F32, kind="ExternalInput")
    WvT = nc.dram_tensor("WvT", [D, D], F32, kind="ExternalInput")
    muT = nc.dram_tensor("muT", [128, 16], F32, kind="ExternalInput")
    ident = nc.dram_tensor("ident", [128, 128], F32, kind="ExternalInput")
    x_out = nc.dram_tensor("x_out", [NSEQ, T, D], F32, kind="ExternalOutput")

    c = type("C", (), {})()
    c.nc = nc
    A = nc.alloc_sbuf_tensor
    c.ident_bf = A("ident_bf", [128, 128], BF16)
    c.mu = A("mu_sb", [128, 16], F32)
    c.epsc = A("epsc", [128, 1], F32)
    c.xt = [A("xt0", [128, D], F32), A("xt1", [128, D], F32)]
    c.xt2 = [A("xt2a", [128, D], F32), A("xt2b", [128, D], F32)]
    c.scr = A("scr", [128, D], F32)
    c.xn_bf2 = c.scr.bitcast(BF16)[:, 0:D]
    c.oz = A("oz", [128, D], F32)
    c.xn_bf = A("xn_bf", [128, D], BF16)
    c.sm = A("sm", [128, 192], F32)
    c.xnT = A("xnT", [128, 8, T], BF16)
    c.wr = A("wr", [128, 8, D], BF16)
    c.wk = A("wk", [128, 8, D], BF16)
    c.wv = A("wv", [128, 8, D], BF16)
    c.xr = [A("xr0", [128, 8, 512], BF16), A("xr1", [128, 8, 512], BF16)]
    c.xk = [A("xk0", [128, 8, 512], BF16), A("xk1", [128, 8, 512], BF16)]
    c.ka = [A("ka0", [128, 8, 512], BF16), A("ka1", [128, 8, 512], BF16)]
    c.kre = A("kre", [128, 512], F32)
    c.rg = A("rg", [128, 4, D], BF16)
    c.ozall = A("ozall", [128, NT, D], BF16)
    c.oztiles = [c.ozall[:, i, :] for i in range(NT)]

    c.PS_S = nc.alloc_psum_tensor("PS_S", [128, 2048], F32)
    c.PS_O = nc.alloc_psum_tensor("PS_O", [128, 1024], F32)
    c.PS_TA = nc.alloc_psum_tensor("PS_TA", [128, 1024], BF16)
    c.PS_TB = nc.alloc_psum_tensor("PS_TB", [128, 1024], BF16)

    p = Prog()
    c.p = p
    c.s_w = p.sem("s_w", 16)
    c.s_x = [p.sem("s_x0", 16), p.sem("s_x1", 16)]
    c.s_x2 = [p.sem("s_x2a", 16), p.sem("s_x2b", 16)]
    c.s_st = [p.sem("s_st0", 16), p.sem("s_st1", 16)]
    sT = c.sT = p.sem("sT")
    sV = c.sV = p.sem("sV")
    sS = c.sS = p.sem("sS")

    p.do("v", lambda e: e.memset(c.epsc[:], EPS), inc=sV)
    p.do("g", lambda e: e.dma_start(out=c.ident_bf[:], in_=ident[:]), inc=c.s_w)
    p.do("g", lambda e: e.dma_start(out=c.mu[:], in_=muT[:]), inc=c.s_w)
    p.do("g", lambda e: e.dma_start(
        out=c.wr[:], in_=WrT.rearrange("(c p) n -> p c n", p=128)), inc=c.s_w)
    p.do("g", lambda e: e.dma_start(
        out=c.wk[:], in_=WkT.rearrange("(c p) n -> p c n", p=128)), inc=c.s_w)
    p.do("g", lambda e: e.dma_start(
        out=c.wv[:], in_=WvT.rearrange("(c p) n -> p c n", p=128)), inc=c.s_w)

    emit_l1f_block(c, x_in, x_out)
    p.wait("y", c.s_st[0])
    p.wait("y", c.s_st[1])
    return nc, c


def emit_l1f_block(c, x_in, x_out):
    xslot_free = [None, None]
    ps_ta_free = [None, None]
    emit_pre_ln_pass(c, 0, x_in, c.xnT, xslot_free, ps_ta_free)
    emit_l1f_core(c)
    emit_final_pre_ln_pass(c, 0, 1, x_in, x_out, c.oztiles, xslot_free, ps_ta_free)
    emit_l1f_core(c)
    emit_final_ln_pass(c, 1, x_in, x_out, c.oztiles, xslot_free)


def emit_l1f_core(c):
    p = c.p
    sT, sV, sS = c.sT, c.sV, c.sS

    q_free = [None, None, None, None]
    qi_rr = [0]

    def quarter():
        i = qi_rr[0] % 4
        qi_rr[0] += 1
        return i, c.PS_S[:, i * 512:(i + 1) * 512]

    p.wait("t", c.s_w)
    p.wait("t", sV)
    xr_free = [None, None]
    xk_free = [None, None]
    ka_free = [None, None]
    shift_done = [None, None]

    def emit_shift(g):
        par = g % 2
        xr, xk = c.xr[par], c.xk[par]
        p.wait_guard("v", xr_free[par])
        p.wait_guard("v", xk_free[par])
        last = [None]
        for cc in range(8):
            a = c.xnT[:, cc, g * 512:(g + 1) * 512]
            if g == 0:
                ap_ = c.xnT[:, cc, 0:511]
                for dst, mcol in ((xr, 0), (xk, 8)):
                    p.do("v", lambda e, a=a, ap_=ap_, dst=dst, cc=cc, mcol=mcol: e.tensor_sub(
                        c.scr[:, 0:511], a[:, 1:512], ap_))
                    p.do("v", lambda e, dst=dst, cc=cc, mcol=mcol, ap_=ap_: e.scalar_tensor_tensor(
                        dst[:, cc, 1:512], c.scr[:, 0:511], c.mu[:, mcol + cc:mcol + cc + 1],
                        ap_, ALU.mult, ALU.add))
                    p.do("v", lambda e, a=a, dst=dst, cc=cc, mcol=mcol: e.tensor_scalar(
                        dst[:, cc, 0:1], a[:, 0:1], c.mu[:, mcol + cc:mcol + cc + 1],
                        None, ALU.mult),
                        inc=sV if (cc == 7 and mcol == 8) else None)
            else:
                ap_ = c.xnT[:, cc, g * 512 - 1:(g + 1) * 512 - 1]
                for dst, mcol in ((xr, 0), (xk, 8)):
                    p.do("v", lambda e, a=a, ap_=ap_, dst=dst, cc=cc, mcol=mcol: e.tensor_sub(
                        c.scr[:, 0:512], a, ap_))
                    p.do("v", lambda e, dst=dst, cc=cc, mcol=mcol, ap_=ap_: e.scalar_tensor_tensor(
                        dst[:, cc, :], c.scr[:, 0:512], c.mu[:, mcol + cc:mcol + cc + 1],
                        ap_, ALU.mult, ALU.add),
                        inc=sV if (cc == 7 and mcol == 8) else None)
        shift_done[par] = sV.mark()

    emit_shift(0)
    for g in range(4):
        par = g % 2
        xr, xk, ka = c.xr[par], c.xk[par], c.ka[par]
        p.wait_guard("t", shift_done[par])
        p.wait_guard("s", ka_free[par])
        for it in range(8):
            qi, ps = quarter()
            p.wait_guard("t", q_free[qi])
            for kc in range(8):
                p.do("t", lambda e, it=it, kc=kc, ps=ps, xk=xk: e.matmul(
                    ps, c.wk[:, kc, it * 128:(it + 1) * 128],
                    xk[:, kc, :],
                    start=(kc == 0), stop=(kc == 7)),
                    inc=sT if kc == 7 else None)
            p.wait("s", sT)
            p.do("s", lambda e, it=it, ps=ps: e.activation(c.kre[:], ps, AF.Relu))
            p.do("s", lambda e, it=it, ka=ka: e.activation(
                ka[:, it, :], c.kre[:], AF.Square), inc=sS)
            q_free[qi] = sS.mark()
        xk_free[par] = sT.mark()
        if g + 1 < 4:
            emit_shift(g + 1)
        for tt in range(4):
            i = g * 4 + tt
            for jh in range(2):
                qi, ps = quarter()
                p.wait_guard("t", q_free[qi])
                p.wait("t", sS)
                for kc in range(8):
                    p.do("t", lambda e, tt=tt, kc=kc, jh=jh, ps=ps, xr=xr: e.matmul(
                        ps, xr[:, kc, tt * 128:(tt + 1) * 128],
                        c.wr[:, kc, jh * 512:(jh + 1) * 512],
                        start=(kc == 0), stop=(kc == 7)),
                        inc=sT if kc == 7 else None)
                p.wait("s", sT)
                p.wait("s", sV)
                p.do("s", lambda e, tt=tt, jh=jh, ps=ps: e.activation(
                    c.rg[:, tt, jh * 512:(jh + 1) * 512], ps, AF.Sigmoid), inc=sS)
                q_free[qi] = sS.mark()
            for jh in range(2):
                qi, ps = quarter()
                p.wait_guard("t", q_free[qi])
                for kc in range(8):
                    p.do("t", lambda e, tt=tt, kc=kc, jh=jh, ps=ps, ka=ka: e.matmul(
                        ps, ka[:, kc, tt * 128:(tt + 1) * 128],
                        c.wv[:, kc, jh * 512:(jh + 1) * 512],
                        start=(kc == 0), stop=(kc == 7)),
                        inc=sT if kc == 7 else None)
                p.wait("v", sT)
                p.wait("v", sS)
                p.do("v", lambda e, i=i, tt=tt, jh=jh, ps=ps: e.tensor_mul(
                    c.oztiles[i][:, jh * 512:(jh + 1) * 512], ps,
                    c.rg[:, tt, jh * 512:(jh + 1) * 512]), inc=sV)
                q_free[qi] = sV.mark()
        xr_free[par] = sT.mark()
        ka_free[par] = sT.mark()



def _barrier(p, engs=ENGINES):
    for e in engs:
        for s in p.sems:
            p.wait(e, s)


def _sub_ctx(c):
    n = type("C", (), {})()
    n.__dict__.update(c.__dict__)
    return n


def build_fused(debug=False, nblocks=4, start_block=0):
    nc = bass.Bass("TRN2", target_bir_lowering=False)
    x_in = nc.dram_tensor("x_in", [NSEQ, T, D], F32, kind="ExternalInput")
    y_out = nc.dram_tensor("y_out", [NSEQ, T, D], F32, kind="ExternalOutput")
    xA = nc.dram_tensor("xA", [NSEQ, T, D], F32)
    xB = nc.dram_tensor("xB", [NSEQ, T, D], F32)
    dstage = {}
    if debug:
        for i in range(3):
            dstage[i] = nc.dram_tensor(f"dstage{i}", [NSEQ, T, D], F32, kind="ExternalOutput")
    g_d = nc.dram_tensor("g_d", [NT, 128, VD], BF16)
    ident = nc.dram_tensor("ident", [128, 128], F32, kind="ExternalInput")
    ones1 = nc.dram_tensor("ones1", [1, 128], F32, kind="ExternalInput")
    causal01 = nc.dram_tensor("causal01", [128, 128], F32, kind="ExternalInput")
    cosf = nc.dram_tensor("cosf", [128, T], F32, kind="ExternalInput")
    sinf = nc.dram_tensor("sinf", [128, T], F32, kind="ExternalInput")
    Btab_d = nc.dram_tensor("Btab", [128, NT, H1], F32, kind="ExternalInput")
    Atab_d = nc.dram_tensor("Atab", [128, NT, H1], F32, kind="ExternalInput")
    wqkvT = nc.dram_tensor("wqkvT", [D, 3 * D], F32, kind="ExternalInput")
    woutT = nc.dram_tensor("woutT", [D, D], F32, kind="ExternalInput")
    rows3 = nc.dram_tensor("rows3", [3, D], F32, kind="ExternalInput")
    U1 = nc.dram_tensor("U1", [D, 256], F32, kind="ExternalInput")
    V1 = nc.dram_tensor("V1", [256, 4096], F32, kind="ExternalInput")
    U2 = nc.dram_tensor("U2", [4096, 256], F32, kind="ExternalInput")
    V2 = nc.dram_tensor("V2", [256, D], F32, kind="ExternalInput")
    b1T = nc.dram_tensor("b1T", [128, 32], F32, kind="ExternalInput")
    b2row = nc.dram_tensor("b2row", [1, D], F32, kind="ExternalInput")
    wq = nc.dram_tensor("wq", [D, D], F32, kind="ExternalInput")
    wk = nc.dram_tensor("wk", [D, D], F32, kind="ExternalInput")
    wv = nc.dram_tensor("wv", [D, VD], F32, kind="ExternalInput")
    wg = nc.dram_tensor("wg", [D, VD], F32, kind="ExternalInput")
    wo = nc.dram_tensor("wo", [VD, D], F32, kind="ExternalInput")
    WrT = nc.dram_tensor("WrT", [D, D], F32, kind="ExternalInput")
    WkT = nc.dram_tensor("WkT", [D, D], F32, kind="ExternalInput")
    WvT = nc.dram_tensor("WvT", [D, D], F32, kind="ExternalInput")
    muT = nc.dram_tensor("muT", [128, 16], F32, kind="ExternalInput")

    c = type("C", (), {})()
    c.nc = nc
    A = nc.alloc_sbuf_tensor
    c.ident_bf = A("ident_bf", [128, 128], BF16)
    c.ones1 = A("c_ones1", [1, 128], F32)
    c.onescol = A("onescol", [128, 1], BF16)
    c.epsc = A("epsc", [128, 1], F32)
    c.cm1 = A("cm1", [128, 1], F32)
    c.cp1 = A("cp1", [128, 1], F32)
    c.xt = [A("xt0", [128, D], F32), A("xt1", [128, D], F32)]
    c.scr = A("scr", [128, D], F32)
    c.xn_bf2 = c.scr.bitcast(BF16)[:, 0:D]
    c.oz = A("oz", [128, D], F32)
    c.xn_bf_t = A("xn_bf_t", [128, D], BF16)
    c.sm = A("sm", [128, 192], F32)
    AB = L1A_ARENA_BYTES
    c.ar = A("arena", [128, AB], mybir.dt.uint8)

    c.PS_S = nc.alloc_psum_tensor("PS_S", [128, 2048], F32)
    c.PS_O = nc.alloc_psum_tensor("PS_O", [128, 1024], F32)
    c.PS_TA = nc.alloc_psum_tensor("PS_TA", [128, 1024], BF16)
    c.PS_TB = nc.alloc_psum_tensor("PS_TB", [128, 1024], BF16)

    p = Prog()
    c.p = p
    c.s_w = p.sem("s_w", 16)
    c.s_x = [p.sem("s_x0", 16), p.sem("s_x1", 16)]
    c.s_x2 = [p.sem("s_x2a", 16), p.sem("s_x2b", 16)]
    c.s_st = [p.sem("s_st0", 16), p.sem("s_st1", 16)]
    c.s_g = [p.sem("s_g0", 16), p.sem("s_g1", 16)]
    sT = c.sT = p.sem("sT")
    sV = c.sV = p.sem("sV")
    sS = c.sS = p.sem("sS")
    c.sV2 = p.sem("sV2")

    p.do("v", lambda e: e.memset(c.epsc[:], EPS), inc=sV)
    p.do("v", lambda e: e.memset(c.cm1[:], -1.0), inc=sV)
    p.do("v", lambda e: e.memset(c.cp1[:], 1.0), inc=sV)
    p.do("v", lambda e: e.memset(c.onescol[:], 1.0), inc=sV)
    p.do("g", lambda e: e.dma_start(out=c.ident_bf[:], in_=ident[:]), inc=c.s_w)
    p.do("g", lambda e: e.dma_start(out=c.ones1[:], in_=ones1[:]), inc=c.s_w)

    def bcast_row(src_row_ap, dst):
        p.wait("g", sT)
        p.do("g", lambda e: e.dma_start(out=c.scr[0:1, :], in_=src_row_ap), inc=c.s_w)
        p.wait("t", c.s_w)
        p.do("t", lambda e: e.matmul(c.PS_O[:, 0:512], c.ones1[:], c.scr[0:1, 0:512], start=True, stop=True))
        p.do("t", lambda e: e.matmul(c.PS_O[:, 512:1024], c.ones1[:], c.scr[0:1, 512:1024], start=True, stop=True), inc=sT)
        p.wait("v", sT)
        p.do("v", lambda e, dst=dst: e.tensor_copy(dst, c.PS_O[:]), inc=sV)
        p.wait("t", sV)

    xio = [(x_in, xA), (xA, xB), (xB, xA), (xA, y_out)]
    bufs = [x_in, xA, xB, xA, y_out]
    xio = [(bufs[i], bufs[i + 1]) for i in range(4)]
    xio[nblocks - 1] = (xio[nblocks - 1][0], y_out)
    xio[start_block] = (x_in, xio[start_block][1])

    skip0 = start_block > 0
    c0 = _sub_ctx(c)
    o = 0
    c0.causal_bf = carve(c.ar, 0, BF16, [128, 128]); o = 256
    c0.gamma_b = carve(c.ar, o, F32, [128, D]); o += 4096
    c0.beta_b = carve(c.ar, o, F32, [128, D]); o += 4096
    c0.bout_b = carve(c.ar, o, F32, [128, D]); o += 4096
    c0.xnT = carve(c.ar, o, BF16, [128, 8, T]); o += 32768
    c0.w = carve(c.ar, o, BF16, [128, 8, D]); o += 16384
    c0.Q = carve(c.ar, o, BF16, [128, 8, T]); o += 32768
    c0.K = carve(c.ar, o, BF16, [128, 8, T]); o += 32768
    c0.V = carve(c.ar, o, BF16, [128, NT, 16 * 65]); o += 33280
    c0.woutT = carve(c.ar, o, BF16, [128, 8, D]); o += 16384
    c0.PT = carve(c.ar, o, BF16, [128, T]); o += 4096
    c0.PT2 = carve(c.ar, o, BF16, [128, T]); o += 4096
    c0.qq_sc = carve(c.ar, o, BF16, [128, T]); o += 4096
    c0.att = carve(c.ar, o, BF16, [128, D]); o += 2048
    c0.attT = c0.PT
    c0.xn_bf = c.xn_bf_t[:]
    c0.xn_f = c.scr[:]
    assert o <= AB
    if not skip0:
        p.do("g", lambda e: e.dma_start(out=c0.causal_bf, in_=causal01[:]), inc=c.s_w)
        p.do("g", lambda e: e.dma_start(
            out=c0.woutT, in_=woutT.rearrange("(c p) d -> p c d", p=128)), inc=c.s_w)
        for i, dst in enumerate((c0.gamma_b, c0.beta_b, c0.bout_b)):
            bcast_row(rows3[i:i + 1, :], dst)
        p.wait("s", sT)
        for s in range(NSEQ):
            emit_l0a_seq(c0, s, xio[0][0], wqkvT, xio[0][1])
    if dstage:
        p.wait("y", c.s_st[0]); p.wait("y", c.s_st[1])
        p.do("y", lambda e: e.dma_start(out=dstage[0][:], in_=xA[:]), inc=c.s_st[0])
    _barrier(p)

    if nblocks < 2:
        p.wait("y", c.s_st[0])
        p.wait("y", c.s_st[1])
        return nc, c
    c1 = _sub_ctx(c)
    o = 0
    c1.b1fm = carve(c.ar, o, F32, [128, 32]); o += 128
    c1.b2b = carve(c.ar, o, F32, [128, D]); o += 4096
    c1.xnT = carve(c.ar, o, BF16, [128, 8, T]); o += 32768
    c1.w1 = carve(c.ar, o, BF16, [128, 8, 256]); o += 4096
    c1.wv1 = carve(c.ar, o, BF16, [128, 2, 4096]); o += 16384
    c1.w2 = carve(c.ar, o, BF16, [128, 32, 256]); o += 16384
    c1.wv2 = carve(c.ar, o, BF16, [128, 2, D]); o += 4096
    c1.P2 = carve(c.ar, o, BF16, [128, 2, T]); o += 8192
    c1.Q2 = carve(c.ar, o, BF16, [128, 2, 512]); o += 2048
    hH = o
    c1.H = carve(c.ar, o, BF16, [128, 32, 512]); o += 32768
    c1.xt2 = [carve(c.ar, hH, F32, [128, D]), carve(c.ar, hH + 4096, F32, [128, D])]
    assert o == 120960
    c1.oztiles = (
        [carve(c.ar, 165376 + i * 2048, BF16, [128, D]) for i in range(11)]
        + [carve(c.ar, 120960 + (i - 11) * 2048, BF16, [128, D])
           for i in range(11, 16)])
    c1.xn_bf = c.xn_bf_t
    c1.scr = c.scr
    c1.oz = c.oz
    assert o <= AB
    c1.f1_ps_o = None
    p.do("g", lambda e: e.dma_start(out=c1.b1fm, in_=b1T[:]), inc=c.s_w)
    p.do("g", lambda e: e.dma_start(
        out=c1.w1, in_=U1.rearrange("(c p) r -> p c r", p=128)), inc=c.s_w)
    p.do("g", lambda e: e.dma_start(
        out=c1.wv1, in_=V1.rearrange("(c p) f -> p c f", p=128)), inc=c.s_w)
    p.do("g", lambda e: e.dma_start(
        out=c1.w2, in_=U2.rearrange("(c p) r -> p c r", p=128)), inc=c.s_w)
    p.do("g", lambda e: e.dma_start(
        out=c1.wv2, in_=V2.rearrange("(c p) d -> p c d", p=128)), inc=c.s_w)
    bcast_row(b2row[0:1, :], c1.b2b)
    p.wait("s", sT)
    if nblocks < 3:
        emit_l0f_block(c1, xio[1][0], xio[1][1])
        p.wait("y", c.s_st[0])
        p.wait("y", c.s_st[1])
        return nc, c
    c2 = _sub_ctx(c)
    alloc_l1a(c2, 0)
    assert L1A_ARENA_BYTES <= AB
    xsf1 = [None, None]
    ptf1 = [None, None]
    emit_pre_ln_pass(c1, 0, xio[1][0], c1.xnT, xsf1, ptf1)
    emit_l0f_core(c1)
    emit_final_pre_ln_pass(c1, 0, 1, xio[1][0], xio[1][1], c1.oztiles, xsf1, ptf1)
    emit_l0f_core(c1)
    emit_final_pre_ln_pass(c1, 1, 0, xio[1][0], xio[1][1], c1.oztiles, xsf1,
                           ptf1, pre_x=xio[2][0], xnT_dst=c2.xnT,
                           pre_wait_stores=True)
    if dstage:
        p.wait("y", c.s_st[0]); p.wait("y", c.s_st[1])
        p.do("y", lambda e: e.dma_start(out=dstage[1][:], in_=xB[:]), inc=c.s_st[0])
    _barrier(p)

    p.do("g", lambda e: e.dma_start(out=c2.causal_bf, in_=causal01[:]), inc=c.s_w)
    p.do("g", lambda e: e.dma_start(out=c2.Btab, in_=Btab_d[:]), inc=c.s_w)
    p.do("g", lambda e: e.dma_start(out=c2.Atab, in_=Atab_d[:]), inc=c.s_w)
    for s in range(NSEQ):
        emit_l1a_seq(c2, s, xio[2][0], xio[2][1], (wq, wk, wv, wg, wo),
                     (cosf, sinf), g_d, pre_done=(s == 0))
    if dstage:
        p.wait("y", c.s_st[0]); p.wait("y", c.s_st[1])
        p.do("y", lambda e: e.dma_start(out=dstage[2][:], in_=xA[:]), inc=c.s_st[0])
    _barrier(p)

    if nblocks < 4:
        p.wait("y", c.s_st[0])
        p.wait("y", c.s_st[1])
        return nc, c
    c3 = _sub_ctx(c)
    o = 0
    c3.mu = carve(c.ar, o, F32, [128, 16]); o += 64
    c3.xnT = carve(c.ar, o, BF16, [128, 8, T]); o += 32768
    c3.wr = carve(c.ar, o, BF16, [128, 8, D]); o += 16384
    c3.wk = carve(c.ar, o, BF16, [128, 8, D]); o += 16384
    c3.wv = carve(c.ar, o, BF16, [128, 8, D]); o += 16384
    c3.xr = [carve(c.ar, o, BF16, [128, 8, 512]),
             carve(c.ar, o + 8192, BF16, [128, 8, 512])]; o += 16384
    c3.xk = [carve(c.ar, o, BF16, [128, 8, 512]),
             carve(c.ar, o + 8192, BF16, [128, 8, 512])]; o += 16384
    c3.ka = [carve(c.ar, o, BF16, [128, 8, 512]),
             carve(c.ar, o + 8192, BF16, [128, 8, 512])]; o += 16384
    c3.kre = carve(c.ar, o, F32, [128, 512]); o += 2048
    c3.rg = carve(c.ar, o, BF16, [128, 4, D]); o += 8192
    c3.ozall = carve(c.ar, o, BF16, [128, NT, D]); o += 32768
    c3.oztiles = [c3.ozall[:, i, :] for i in range(NT)]
    c3.xt2 = [carve(c.ar, o, F32, [128, D]), carve(c.ar, o + 4096, F32, [128, D])]
    o += 8192
    c3.xn_bf = c.xn_bf_t
    assert o <= AB
    p.do("g", lambda e: e.dma_start(out=c3.mu, in_=muT[:]), inc=c.s_w)
    p.do("g", lambda e: e.dma_start(
        out=c3.wr, in_=WrT.rearrange("(c p) n -> p c n", p=128)), inc=c.s_w)
    p.do("g", lambda e: e.dma_start(
        out=c3.wk, in_=WkT.rearrange("(c p) n -> p c n", p=128)), inc=c.s_w)
    p.do("g", lambda e: e.dma_start(
        out=c3.wv, in_=WvT.rearrange("(c p) n -> p c n", p=128)), inc=c.s_w)
    emit_l1f_block(c3, xio[3][0], xio[3][1])
    p.wait("y", c.s_st[0])
    p.wait("y", c.s_st[1])
    return nc, c


def _fused_inputs(inputs, core):
    cst = _CONSTS
    mu = np.stack([inputs["L1_ffn_mu_r"], inputs["L1_ffn_mu_k"]])
    muT = np.ascontiguousarray(
        mu.reshape(2, 8, 128).transpose(2, 0, 1).reshape(128, 16))
    return {
        "x_in": np.ascontiguousarray(inputs["L0_attn_x"][2 * core:2 * core + 2]),
        "ident": cst["ident"],
        "ones1": cst["ones1"],
        "causal01": cst["causal01"],
        "cosf": cst["cos_fm"],
        "sinf": cst["sin_fm"],
        "Btab": cst["Btab"],
        "Atab": cst["Atab"],
        "wqkvT": np.ascontiguousarray(inputs["L0_attn_w_qkv"].T),
        "woutT": np.ascontiguousarray(inputs["L0_attn_w_out"].T),
        "rows3": np.ascontiguousarray(np.stack([
            inputs["L0_attn_ln_gamma"], inputs["L0_attn_ln_beta"],
            inputs["L0_attn_b_out"]])),
        "U1": inputs["L0_ffn_U1"],
        "V1": inputs["L0_ffn_V1"],
        "U2": inputs["L0_ffn_U2"],
        "V2": inputs["L0_ffn_V2"],
        "b1T": np.ascontiguousarray(inputs["L0_ffn_b1"].reshape(32, 128).T),
        "b2row": np.ascontiguousarray(inputs["L0_ffn_b2"][None, :]),
        "wq": inputs["L1_attn_wq"],
        "wk": inputs["L1_attn_wk"],
        "wv": inputs["L1_attn_wv"],
        "wg": inputs["L1_attn_wg"],
        "wo": inputs["L1_attn_wo"],
        "WrT": np.ascontiguousarray(inputs["L1_ffn_W_r"].T),
        "WkT": np.ascontiguousarray(inputs["L1_ffn_W_k"].T),
        "WvT": np.ascontiguousarray(inputs["L1_ffn_W_v"].T),
        "muT": muT,
    }


def kernel_fused(debug=False, nblocks=4, start_block=0, **inputs):
    nc, c = build_fused(debug, nblocks, start_block)
    finish_build(nc, c)
    in_maps = [_fused_inputs(inputs, core) for core in range(NCORES)]
    res = run_bass_kernel_spmd(nc, in_maps, core_ids=list(range(NCORES)))
    y = np.concatenate([r["y_out"] for r in res.results], axis=0)
    if debug:
        return y, res.results
    return y



def _l0f_inputs(inputs, xcur, core):
    cst = _CONSTS
    return {
        "x_in": np.ascontiguousarray(xcur[2 * core:2 * core + 2]),
        "U1": inputs["L0_ffn_U1"],
        "V1": inputs["L0_ffn_V1"],
        "U2": inputs["L0_ffn_U2"],
        "V2": inputs["L0_ffn_V2"],
        "b1T": np.ascontiguousarray(inputs["L0_ffn_b1"].reshape(32, 128).T),
        "b2row": np.ascontiguousarray(inputs["L0_ffn_b2"][None, :]),
        "ident": cst["ident"],
        "ones1": cst["ones1"],
    }


def _l1a_inputs(inputs, xcur, core):
    cst = _CONSTS
    return {
        "x_in": np.ascontiguousarray(xcur[2 * core:2 * core + 2]),
        "wq": inputs["L1_attn_wq"],
        "wk": inputs["L1_attn_wk"],
        "wv": inputs["L1_attn_wv"],
        "wg": inputs["L1_attn_wg"],
        "wo": inputs["L1_attn_wo"],
        "cosf": cst["cos_fm"],
        "sinf": cst["sin_fm"],
        "Btab": cst["Btab"],
        "Atab": cst["Atab"],
        "causal01": cst["causal01"],
        "ident": cst["ident"],
    }


def _l1f_inputs(inputs, xcur, core):
    cst = _CONSTS
    mu = np.stack([inputs["L1_ffn_mu_r"], inputs["L1_ffn_mu_k"]])
    muT = np.ascontiguousarray(
        mu.reshape(2, 8, 128).transpose(2, 0, 1).reshape(128, 16))
    return {
        "x_in": np.ascontiguousarray(xcur[2 * core:2 * core + 2]),
        "WrT": np.ascontiguousarray(inputs["L1_ffn_W_r"].T),
        "WkT": np.ascontiguousarray(inputs["L1_ffn_W_k"].T),
        "WvT": np.ascontiguousarray(inputs["L1_ffn_W_v"].T),
        "muT": muT,
        "ident": cst["ident"],
    }


def _l0a_inputs(inputs, xcur, core):
    cst = _CONSTS
    return {
        "x_in": np.ascontiguousarray(xcur[2 * core:2 * core + 2]),
        "wqkvT": np.ascontiguousarray(inputs["L0_attn_w_qkv"].T),
        "woutT": np.ascontiguousarray(inputs["L0_attn_w_out"].T),
        "rows3": np.ascontiguousarray(np.stack([
            inputs["L0_attn_ln_gamma"], inputs["L0_attn_ln_beta"],
            inputs["L0_attn_b_out"]])),
        "ident": cst["ident"],
        "causal": cst["causal01"],
        "ones1": cst["ones1"],
    }


def kernel(**inputs):
    return kernel_fused(**inputs)


def kernel_multilaunch(debug=False, upto=4, **inputs):
    xcur = np.ascontiguousarray(inputs["L0_attn_x"], dtype=np.float32)
    dbg0 = None
    stages = [(build_l0_attn, _l0a_inputs), (build_l0_ffn, _l0f_inputs),
              (build_l1_attn, _l1a_inputs), (build_l1_ffn, _l1f_inputs)][:upto]
    for bi, (build, mk_in) in enumerate(stages):
        if bi == 0 and debug:
            nc, c = build(True)
        else:
            nc, c = build()
        finish_build(nc, c)
        in_maps = [mk_in(inputs, xcur, core) for core in range(NCORES)]
        res = run_bass_kernel_spmd(nc, in_maps, core_ids=list(range(NCORES)))
        xcur = np.concatenate([r["x_out"] for r in res.results], axis=0)
        if bi == 0 and debug:
            dbg0 = res.results[0]
    if debug:
        return xcur, dbg0
    return xcur
